# version 11
# baseline (speedup 1.0000x reference)
import numpy as np
import ml_dtypes
from contextlib import ExitStack

import concourse.bass as bass
import concourse.mybir as mybir
from concourse.bass_utils import run_bass_kernel_spmd

F32 = mybir.dt.float32
BF16 = mybir.dt.bfloat16
AF = mybir.ActivationFunctionType
ALU = mybir.AluOpType
AX = mybir.AxisListType

D = 4096
SEQ = 4096
OWN = 2048
NH, KVL, QL, HD = 32, 4, 12, 128
NEXP, RANK = 64, 16
SCALING = 2.0
EPS = 1e-5
NEG = -1.0e30

QCG = 1024
QS = 512
MCH = 512
FCH = 128


def _consts():
    jj = np.arange(64)
    c64 = np.cos(2 * np.pi * np.outer(jj, jj) / 64).astype(np.float32)
    s64 = np.sin(2 * np.pi * np.outer(jj, jj) / 64).astype(np.float32)
    C = {}
    C["FA"] = np.concatenate([c64, -s64], axis=1)
    C["FHA"] = np.block([[c64, -s64], [s64, c64]])
    f = np.arange(D)
    ang = 2 * np.pi * np.outer(jj, f) / D
    Cb = np.cos(ang).astype(np.float32)
    Sb = np.sin(ang).astype(np.float32)
    slab = np.zeros((64, 16384), np.float32)
    for j1 in range(64):
        slab[:, j1 * 128:j1 * 128 + 64] = Cb[:, j1::64]
        slab[:, j1 * 128 + 64:(j1 + 1) * 128] = Sb[:, j1::64]
        slab[:, 8192 + j1 * 128:8192 + j1 * 128 + 64] = Sb[:, j1::64]
        slab[:, 8192 + j1 * 128 + 64:8192 + (j1 + 1) * 128] = -Cb[:, j1::64]
    C["SLAB"] = slab
    C["ONES128"] = np.ones((128, 1), np.float32)
    C["ONES1"] = np.ones((1, 128), np.float32)
    hm = np.zeros((128, 32), np.float32)
    for h in range(NH):
        hm[4 * h:4 * h + 4, h] = 1.0
    C["HEADMASK"] = hm
    ef = np.zeros((64, NEXP * RANK), np.float32)
    for e in range(NEXP):
        ef[e, RANK * e:RANK * (e + 1)] = 1.0
    C["EFULL"] = ef
    I64 = np.eye(64, dtype=np.float32)
    L1 = np.zeros((128, 128), np.float32)
    L1[0:64, 0:64] = I64
    L1[64:128, 0:64] = I64
    L2 = np.zeros((128, 128), np.float32)
    L2[0:64, 64:128] = -I64
    L2[64:128, 64:128] = I64
    C["LH1"] = L1 / D
    C["LH2"] = L2 / D
    SW = np.zeros((128, 128), np.float32)
    SW[0:64, 64:128] = I64
    SW[64:128, 0:64] = I64
    C["SW"] = SW
    return C


def _shard_inputs(inp):
    x = np.ascontiguousarray(np.asarray(inp["L0_attn_a_x"], np.float32))
    Wq = np.asarray(inp["L0_attn_a_Wq_d"], np.float32)
    bq = np.asarray(inp["L0_attn_a_bq_d"], np.float32)
    Wqk = np.asarray(inp["L0_attn_a_W_qk"], np.float32)
    bqk = np.asarray(inp["L0_attn_a_b_qk"], np.float32)
    Wkv = np.asarray(inp["L0_attn_a_Wkv_d"], np.float32)
    bkv = np.asarray(inp["L0_attn_a_bkv_d"], np.float32)
    Wvu = np.asarray(inp["L0_attn_a_Wv_u"], np.float32)
    bvu = np.asarray(inp["L0_attn_a_bv_u"], np.float32)
    Wo = np.ascontiguousarray(np.asarray(inp["L0_attn_a_Wo"], np.float32))
    bo = np.asarray(inp["L0_attn_a_bo"], np.float32)
    A = np.asarray(inp["L0_ffn_A"], np.float32)
    Bm = np.asarray(inp["L0_ffn_B"], np.float32)
    Wr = np.asarray(inp["L0_ffn_Wr"], np.float32)

    wpack = np.zeros((D, 36), np.float32)
    wpack[:, 0:12] = Wq
    wpack[:, 32:36] = Wkv
    b36 = np.zeros((36, 1), np.float32)
    b36[0:12, 0] = bq
    b36[32:36, 0] = bkv
    wv5 = np.zeros((5, D), np.float32)
    wv5[0:4] = Wvu
    wv5[4] = bvu
    AT = np.ascontiguousarray(A.reshape(NEXP * RANK, D).T)
    Barr = np.ascontiguousarray(Bm.transpose(0, 2, 1).reshape(NEXP * RANK, D))
    WrT = np.ascontiguousarray(Wr.T)

    C = _consts()
    shared = dict(wpack=wpack, b36=b36, wqk=Wqk, bqk=bqk.reshape(128, 1),
                  wv5=wv5, wo=Wo,
                  bo=np.ascontiguousarray(bo.reshape(32, 128).T),
                  at=AT, barr=Barr, wrt=WrT, **{k.lower(): v for k, v in C.items()})
    maps = []
    for c in range(8):
        b, half = c // 2, c % 2
        m = dict(shared)
        m["xq"] = np.ascontiguousarray(x[b, half * OWN:(half + 1) * OWN])
        m["xf"] = np.ascontiguousarray(x[b, (1 - half) * OWN:(2 - half) * OWN])
        maps.append(m)
    return maps


class Sem:
    def __init__(self, ctx, nc, name):
        self.h = ctx.enter_context(nc.semaphore(name))
        self.n = 0

    def inc(self, instr, k=1):
        instr.then_inc(self.h, k)
        self.n += k
        return self.n


class SemPair:
    def __init__(self, ctx, nc, name):
        self.s = [Sem(ctx, nc, name + "a"), Sem(ctx, nc, name + "b")]

    def inc(self, instr, i, k=16):
        return self.s[i % 2].inc(instr, k)

    def wait(self, eng, i, cnt_of_parity):
        eng.wait_ge(self.s[i % 2].h, 16 * cnt_of_parity)

    def wait_done(self, eng, i):
        eng.wait_ge(self.s[i % 2].h, 16 * (i // 2 + 1))

    def wait_total(self, eng, total):
        eng.wait_ge(self.s[0].h, 16 * ((total + 1) // 2))
        eng.wait_ge(self.s[1].h, 16 * (total // 2))


class SemQuad:
    def __init__(self, ctx, nc, name):
        self.s = [Sem(ctx, nc, name + chr(97 + i)) for i in range(4)]

    def inc(self, instr, i, k=16):
        return self.s[i % 4].inc(instr, k)

    def wait_done(self, eng, i):
        eng.wait_ge(self.s[i % 4].h, 16 * (i // 4 + 1))


def kernel(**inputs):
    maps = _shard_inputs(inputs)
    nc = build_program()
    res = run_bass_kernel_spmd(nc, maps, core_ids=list(range(8)))
    out = np.empty((4, SEQ, D), np.float32)
    for c in range(8):
        b, half = c // 2, c % 2
        out[b, half * OWN:(half + 1) * OWN] = res.results[c]["out"]
    return out


def build_program(phases=6, dbg=False, ph3sub=4):
    nc = bass.Bass("TRN2", target_bir_lowering=False, debug=False)

    d_xq = nc.dram_tensor("xq", [OWN, D], F32, kind="ExternalInput").ap()
    d_xf = nc.dram_tensor("xf", [OWN, D], F32, kind="ExternalInput").ap()
    d_wpack = nc.dram_tensor("wpack", [D, 36], F32, kind="ExternalInput").ap()
    d_b36 = nc.dram_tensor("b36", [36, 1], F32, kind="ExternalInput").ap()
    d_wqk = nc.dram_tensor("wqk", [QL, 128], F32, kind="ExternalInput").ap()
    d_bqk = nc.dram_tensor("bqk", [128, 1], F32, kind="ExternalInput").ap()
    d_wv5 = nc.dram_tensor("wv5", [5, D], F32, kind="ExternalInput").ap()
    d_wo = nc.dram_tensor("wo", [D, D], F32, kind="ExternalInput").ap()
    d_bo = nc.dram_tensor("bo", [128, 32], F32, kind="ExternalInput").ap()
    d_at = nc.dram_tensor("at", [D, NEXP * RANK], F32, kind="ExternalInput").ap()
    d_barr = nc.dram_tensor("barr", [NEXP * RANK, D], F32, kind="ExternalInput").ap()
    d_wrt = nc.dram_tensor("wrt", [D, NEXP], F32, kind="ExternalInput").ap()
    d_fa = nc.dram_tensor("fa", [64, 128], F32, kind="ExternalInput").ap()
    d_fha = nc.dram_tensor("fha", [128, 128], F32, kind="ExternalInput").ap()
    d_slab = nc.dram_tensor("slab", [64, 16384], F32, kind="ExternalInput").ap()
    d_ones128 = nc.dram_tensor("ones128", [128, 1], F32, kind="ExternalInput").ap()
    d_ones1 = nc.dram_tensor("ones1", [1, 128], F32, kind="ExternalInput").ap()
    d_headmask = nc.dram_tensor("headmask", [128, 32], F32, kind="ExternalInput").ap()
    d_efull = nc.dram_tensor("efull", [64, NEXP * RANK], F32, kind="ExternalInput").ap()
    d_lh1 = nc.dram_tensor("lh1", [128, 128], F32, kind="ExternalInput").ap()
    d_lh2 = nc.dram_tensor("lh2", [128, 128], F32, kind="ExternalInput").ap()
    d_sw = nc.dram_tensor("sw", [128, 128], F32, kind="ExternalInput").ap()
    d_out = nc.dram_tensor("out", [OWN, D], F32, kind="ExternalOutput").ap()

    dk0 = dict(kind="ExternalOutput") if dbg else {}
    xn_dram = nc.dram_tensor("xn_s", [SEQ, D], BF16, **dk0).ap()
    qt_dram = nc.dram_tensor("qt_s", [128, OWN], BF16, **dk0).ap()
    nb_dram = nc.dram_tensor("nb_s", [32, OWN], BF16, **dk0).ap()
    ckv_dram = nc.dram_tensor("ckv_s", [5, SEQ], BF16, **dk0).ap()
    mx_dram = nc.dram_tensor("mx_s", [4, 1], F32).ap()
    mx2_dram = nc.dram_tensor("mx2_s", [1, 1], F32).ap()
    aot_dram = nc.dram_tensor("aot_s", [D, OWN], BF16, **({"kind": "ExternalOutput"} if dbg else {})).ap()
    dk = dict(kind="ExternalOutput") if dbg else {}
    x1_dram = nc.dram_tensor("x1_s", [OWN, D], F32, **dk).ap()
    x2_dram = nc.dram_tensor("x2_s", [OWN, D], F32, **dk).ap()
    x3_dram = nc.dram_tensor("x3_s", [OWN, D], F32, **dk).ap()
    xnm_dram = nc.dram_tensor("xnm_s", [OWN, D], BF16).ap()
    fb_dram = nc.dram_tensor("fb_s", [2 * FCH, D], BF16).ap()
    fc_dram = nc.dram_tensor("fc_s", [128, 8192], BF16).ap()
    fg_dram = nc.dram_tensor("fg_s", [2 * FCH, D], BF16).ap()
    barr_bf = nc.dram_tensor("barr_bf", [NEXP * RANK, D], BF16).ap()
    fo_dram = nc.dram_tensor("fo_s", [128, 8192], BF16).ap()

    with ExitStack() as ctx:
        sb = lambda n, s, d: ctx.enter_context(nc.sbuf_tensor(n, s, d))
        mksem = lambda n: Sem(ctx, nc, n)

        s_wpack = sb("c_wpack", [128, 32 * 36], BF16)
        s_b36 = sb("c_b36", [36, 1], F32)
        s_wqk_raw = sb("c_wqkr", [QL, 128], F32)
        s_wqk05 = sb("c_wqk05", [QL, 128], BF16)
        s_bqk = sb("c_bqk", [128, 1], F32)
        s_bqk05 = sb("c_bqk05", [128, 1], F32)
        s_wv5 = sb("c_wv5", [5, D], BF16)
        s_wrt = sb("c_wrt", [128, 32 * NEXP], BF16)
        s_bo = sb("c_bo", [128, 32], F32)
        s_fa = sb("c_fa", [64, 128], BF16)
        s_fha = sb("c_fha", [128, 128], BF16)
        s_ones128 = sb("c_ones128", [128, 1], BF16)
        s_ones1 = sb("c_ones1", [1, 128], BF16)
        s_headmask = sb("c_hm", [128, 32], BF16)
        s_efull = sb("c_efull", [64, NEXP * RANK], BF16)
        s_lh1 = sb("c_lh1", [128, 128], BF16)
        s_lh2 = sb("c_lh2", [128, 128], BF16)
        s_sw = sb("c_sw", [128, 128], BF16)
        s_eps = sb("c_eps", [128, 1], F32)

        AW = 47872
        arena = sb("arena", [128, AW], F32)
        PS = ctx.enter_context(nc.psum_tensor("PS", [128, 4096], F32))

        def vf(off, n, p0=0, p1=128):
            return arena.ap()[p0:p1, off:off + n]

        def vb(off, n, p0=0, p1=128):
            return arena.ap()[p0:p1, off:off + n].bitcast(BF16)

        def bank(i, w=512):
            return PS[:, i * 512:i * 512 + w]

        def vsync(v, instr):
            vch.inc(instr)
            v.wait_ge(vch.h, vch.n)

        def ssync(s, instr):
            sch.inc(instr)
            s.wait_ge(sch.h, sch.n)

        def gsync(g, instr):
            gch.inc(instr)
            g.wait_ge(gch.h, gch.n)

        c_ld = mksem("c_ld")
        q_ld = SemPair(ctx, nc, "q_ld")
        vst = mksem("vst"); sst = mksem("sst"); vst2 = mksem("vst2")
        vch = mksem("vch"); sch = mksem("sch"); gch = mksem("gch")
        sst2 = mksem("sst2"); nrm = mksem("nrm"); stq = SemPair(ctx, nc, "stq")
        txn = SemPair(ctx, nc, "txn"); tpj = mksem("tpj"); spj = mksem("spj")
        tqt = mksem("tqt"); sqt = mksem("sqt"); sab = mksem("sab")
        tbs = mksem("tbs"); vmc = mksem("vmc"); dmc = mksem("dmc")
        vnb = mksem("vnb"); dq5 = mksem("dq5"); dck = mksem("dck")
        t_v = mksem("t_v"); v_ve = mksem("v_ve")
        t_sc = mksem("t_sc"); s_ex = mksem("s_ex"); t_av = mksem("t_av")
        v_zt = mksem("v_zt"); t_zm = mksem("t_zm"); v_rz = mksem("v_rz")
        t_zb = mksem("t_zb"); v_at = mksem("v_at")
        g_wo = SemPair(ctx, nc, "g_wo"); t_wo = mksem("t_wo"); s_wo = mksem("s_wo")
        st_wo = SemPair(ctx, nc, "st_wo")
        mg_x = SemPair(ctx, nc, "mg_x"); mg_aA = mksem("mg_aA"); mg_aB = mksem("mg_aB"); mg_r = SemPair(ctx, nc, "mg_r")
        ms_t = SemPair(ctx, nc, "ms_t"); ms_xn = SemPair(ctx, nc, "ms_xn"); ms_T1 = SemQuad(ctx, nc, "msT1")
        ms_T2 = SemQuad(ctx, nc, "msT2"); ms_o = SemPair(ctx, nc, "ms_o"); ms_x1 = SemPair(ctx, nc, "ms_x1")
        mv_1 = mksem("mv_1"); mv_g = mksem("mv_g"); mv_gx = mksem("mv_gx")
        mv_hg = mksem("mv_hg"); mv_oe = mksem("mv_oe")
        mc_s = mksem("mc_s"); mc_e = mksem("mc_e")
        tm_r = mksem("tm_r"); tm_g = mksem("tm_g"); tm_h = mksem("tm_h")
        tm_b = mksem("tm_b")
        fg_x = SemPair(ctx, nc, "fg_x"); fs_xn = mksem("fs_xn"); fs_D = mksem("fs_D")
        ft_A = mksem("ft_A"); ft_B = mksem("ft_B"); fs_cs = mksem("fs_cs")
        fv_ev = mksem("fv_ev"); fs_c = mksem("fs_c"); fs_cr = mksem("fs_cr")
        fv_g = mksem("fv_g"); fc_s = mksem("fc_s"); fs_g = mksem("fs_g")
        fs_gD = mksem("fs_gD"); fs_o = mksem("fs_o")
        fs_or = mksem("fs_or"); fv_x3 = mksem("fv_x3"); fs_x3 = mksem("fs_x3")
        fv_ln = mksem("fv_ln"); fc_ln = mksem("fc_ln")
        ft_P = mksem("ft_P"); fv_pe = mksem("fv_pe")
        ft_G = mksem("ft_G"); fv_gw = mksem("fv_gw")
        ft_H = mksem("ft_H"); fv_he = mksem("fv_he")

        block = ctx.enter_context(nc.Block())

        sc0, _ = nc.enter_named_scope("ph0", False)

        @block.gpsimd
        def _(g):
            g.dma_start(out=s_wpack.ap().rearrange("p (c w) -> p c w", w=36),
                        in_=d_wpack.rearrange("(c p) w -> p c w", p=128)).then_inc(c_ld.h, 16)
            g.dma_start(out=s_b36[:], in_=d_b36).then_inc(c_ld.h, 16)
            g.dma_start(out=s_wqk_raw[:], in_=d_wqk).then_inc(c_ld.h, 16)
            g.dma_start(out=s_bqk[:], in_=d_bqk).then_inc(c_ld.h, 16)
            g.dma_start(out=s_wv5[:], in_=d_wv5).then_inc(c_ld.h, 16)
            g.dma_start(out=s_wrt.ap().rearrange("p (c e) -> p c e", e=NEXP),
                        in_=d_wrt.rearrange("(c p) e -> p c e", p=128)).then_inc(c_ld.h, 16)
            g.dma_start(out=s_bo.ap().unsqueeze(2),
                        in_=d_bo.rearrange("(c p) o -> p c o", p=128)).then_inc(c_ld.h, 16)
            for s_t, d_t in [(s_fa, d_fa), (s_fha, d_fha),
                             (s_ones128, d_ones128), (s_ones1, d_ones1),
                             (s_headmask, d_headmask), (s_efull, d_efull),
                             (s_lh1, d_lh1), (s_lh2, d_lh2), (s_sw, d_sw)]:
                g.dma_start(out=s_t[:], in_=d_t).then_inc(c_ld.h, 16)
        c_ld.n = 16 * 16

        @block.vector
        def _(v):
            v.wait_ge(c_ld.h, c_ld.n)
            v.memset(s_eps[:], EPS)
            v.tensor_scalar(s_wqk05[:], s_wqk_raw[:], 0.5, None, op0=ALU.mult)
            vst.inc(v.tensor_scalar(s_bqk05[:], s_bqk[:], 0.5, None, op0=ALU.mult))

        nc.leave_named_scope("ph0", sc0, False)
        sc1, _ = nc.enter_named_scope("ph1", False)
        a_x = [0, 4096]
        a_sqs = 8192
        a_xn = [12288, 14336]
        a_sm = 16384

        def ph1_smalls(bi, k):
            return vf(a_sm + bi * 16 + k, 1)

        @block.gpsimd
        def _(g):
            for i in range(32):
                if i >= 2:
                    g.wait_ge(nrm.h, i - 1)
                    g.wait_ge(sst.h, i - 1)
                src = d_xq if i < 16 else d_xf
                row = (i % 16) * 128
                q_ld.inc(g.dma_start(out=vf(a_x[i % 2], 4096),
                                     in_=src[row:row + 128, :]), i)

        @block.vector
        def _(v):
            for i in range(32):
                bi = i % 2
                q_ld.wait_done(v, i)
                vsync(v, v.reduce_sum(ph1_smalls(bi, 0), vf(a_x[bi], 4096), axis=AX.X))
                vst.inc(v.tensor_scalar(ph1_smalls(bi, 1), ph1_smalls(bi, 0),
                                        -1.0 / D, None, op0=ALU.mult))
                v.wait_ge(vch.h, vch.n)
                vsync(v, v.engine_nop())
                v.wait_ge(sst.h, i + 1)
                vsync(v, v.tensor_tensor(ph1_smalls(bi, 2), ph1_smalls(bi, 1),
                                         ph1_smalls(bi, 1), op=ALU.mult))
                vsync(v, v.tensor_scalar(ph1_smalls(bi, 3), ph1_smalls(bi, 4), 1.0 / D,
                                         None, op0=ALU.mult))
                vst2.inc(v.tensor_tensor(ph1_smalls(bi, 5), ph1_smalls(bi, 3),
                                         ph1_smalls(bi, 2), op=ALU.subtract))
                v.wait_ge(sst2.h, i + 1)
                v.tensor_tensor(ph1_smalls(bi, 6), ph1_smalls(bi, 1),
                                ph1_smalls(bi, 7), op=ALU.mult)
                vsync(v, v.engine_nop())
                if i >= 2:
                    stq.wait(v, i, i // 2)
                nrm.inc(v.tensor_scalar(vb(a_xn[bi], 2048), vf(a_x[bi], 4096),
                                        ph1_smalls(bi, 7), ph1_smalls(bi, 6),
                                        op0=ALU.mult, op1=ALU.add))

        @block.scalar
        def _(s):
            for i in range(32):
                bi = i % 2
                q_ld.wait_done(s, i)
                ssync(s, s.memzero(ph1_smalls(bi, 4)))
                sst.inc(s.activation(vf(a_sqs, 4096), vf(a_x[bi], 4096), AF.Square,
                                     accum_out=ph1_smalls(bi, 4)))
                s.wait_ge(vst2.h, i + 1)
                ssync(s, s.activation(ph1_smalls(bi, 8), ph1_smalls(bi, 5), AF.Ln,
                                      bias=s_eps[:]))
                sst2.inc(s.activation(ph1_smalls(bi, 7), ph1_smalls(bi, 8), AF.Exp,
                                      scale=-0.5))

        @block.sync
        def _(sy):
            for i in range(32):
                bi = i % 2
                sy.wait_ge(nrm.h, i + 1)
                row = (i % 16) * 128 + (0 if i < 16 else OWN)
                stq.inc(sy.dma_start(out=xn_dram[row:row + 128, :],
                                     in_=vb(a_xn[bi], 2048)), i)

        nc.leave_named_scope("ph1", sc1, False)
        sc2, _ = nc.enter_named_scope("ph2", False)
        A_Q5, A_CKV4 = 32768, 44032
        HG = [(0, 0, 11), (1, 11, 11), (2, 22, 10)]

        @block.sync
        def _(sy):
            stq.wait_total(sy, 32)
            for ci in range(32):
                if ci >= 2:
                    sy.wait_ge(tpj.h, ci - 1)
                txn.inc(sy.dma_start(out=vb(2048 * (ci % 2), 2048),
                                     in_=xn_dram[:, ci * 128:(ci + 1) * 128],
                                     transpose=True), ci)

        @block.tensor
        def _(t):
            for ci in range(32):
                txn.wait_done(t, ci)
                if ci == 0:
                    t.wait_ge(c_ld.h, c_ld.n)
                for nb in range(8):
                    ins = t.matmul(PS[0:36, nb * 512:(nb + 1) * 512],
                                   s_wpack[:, ci * 36:(ci + 1) * 36],
                                   vb(2048 * (ci % 2), 2048)[:, nb * 512:(nb + 1) * 512],
                                   start=(ci == 0), stop=(ci == 31))
                tpj.inc(ins)

        @block.scalar
        def _(s):
            s.wait_ge(tpj.h, 32)
            s.wait_ge(vst.h, 34)
            for nb in range(8):
                spj.inc(s.activation(vb(4096, 2048, 0, 12)[:, nb * 512:(nb + 1) * 512],
                                     PS[0:12, nb * 512:(nb + 1) * 512], AF.Identity,
                                     bias=s_b36[0:12, :]))
                spj.inc(s.activation(vb(44032, 2048, 0, 4)[:, nb * 512:(nb + 1) * 512],
                                     PS[32:36, nb * 512:(nb + 1) * 512], AF.Identity,
                                     bias=s_b36[32:36, :]))

        @block.vector
        def _(v):
            v.wait_ge(c_ld.h, c_ld.n)
            vst.inc(v.memset(vb(44032, 2048, 0, 8), 1.0))

        @block.tensor
        def _(t):
            t.wait_ge(spj.h, 16)
            t.wait_ge(vst.h, 1)
            for nb in range(4):
                tqt.inc(t.matmul(bank(nb), s_wqk05[:], 
                                 vb(4096, 2048, 0, 12)[:, nb * 512:(nb + 1) * 512],
                                 start=True, stop=True))

        @block.scalar
        def _(s):
            for nb in range(4):
                s.wait_ge(tqt.h, nb + 1)
                sqt.inc(s.activation(vb(6144, 1024)[:, nb * 512:(nb + 1) * 512],
                                     bank(nb), AF.Identity, bias=s_bqk05[:]))
            s.wait_ge(sqt.h, 4)
            sab.inc(s.activation(vb(7168, 1024), vb(6144, 1024), AF.Abs))

        @block.tensor
        def _(t):
            t.wait_ge(sab.h, 1)
            for nb in range(4):
                tbs.inc(t.matmul(PS[0:32, 2048 + nb * 512: 2048 + (nb + 1) * 512],
                                 s_headmask[:], vb(7168, 1024)[:, nb * 512:(nb + 1) * 512],
                                 start=True, stop=True))

        @block.vector
        def _(v):
            v.wait_ge(spj.h, 16)
            vmc.inc(v.reduce_max(vf(9216, 1, 0, 4), vb(44032, 2048, 0, 4), axis=AX.X,
                                 apply_absolute_value=True))

        @block.sync
        def _(sy):
            sy.wait_ge(vmc.h, 1)
            dmc.inc(sy.dma_start(out=mx_dram, in_=vf(9216, 1, 0, 4)), 16)
            sy.wait_ge(dmc.h, 16)
            dmc.inc(sy.dma_start(out=vf(9220, 4, 0, 1), in_=mx_dram.rearrange("a b -> b a")), 16)

        @block.vector
        def _(v):
            v.wait_ge(dmc.h, 32)
            vsync(v, v.reduce_max(vf(9232, 1, 0, 1), vf(9220, 4, 0, 1), axis=AX.X))
            vmc.inc(v.tensor_scalar(vf(9236, 1, 0, 1), vf(9232, 1, 0, 1), -1.0, None,
                                    op0=ALU.mult))

        @block.sync
        def _(sy):
            sy.wait_ge(vmc.h, 2)
            dmc.inc(sy.dma_start(out=mx2_dram, in_=vf(9236, 1, 0, 1)), 16)
            sy.wait_ge(dmc.h, 48)
            dmc.inc(sy.dma_start(out=vf(9240, 1, 0, 32),
                                 in_=mx2_dram.to_broadcast([32, 1])), 16)

        @block.vector
        def _(v):
            v.wait_ge(dmc.h, 64)
            v.wait_ge(tbs.h, 4)
            for nb in range(4):
                vnb.inc(v.tensor_scalar(vb(8192, 1024, 0, 32)[:, nb * 512:(nb + 1) * 512],
                                        PS[0:32, 2048 + nb * 512:2048 + (nb + 1) * 512],
                                        vf(9240, 1, 0, 32), None, op0=ALU.mult))

        @block.sync
        def _(sy):
            sy.wait_ge(sqt.h, 4)
            dq5.inc(sy.dma_start(out=qt_dram, in_=vb(6144, 1024)), 16)
            sy.wait_ge(vnb.h, 4)
            dq5.inc(sy.dma_start(out=nb_dram, in_=vb(8192, 1024, 0, 32)), 16)
            sy.wait_ge(spj.h, 16)
            sy.wait_ge(vst.h, 34)
            dq5.inc(sy.dma_start(out=ckv_dram, in_=vb(44032, 2048, 0, 5)), 16)
            sy.wait_ge(dq5.h, 48)
            for grp, h0, nh in HG:
                p0 = 32 * grp
                dq5.inc(sy.dma_start(
                    out=vb(A_Q5, nh * 1024, p0, p0 + 4).rearrange("d (h t) -> d h t", h=nh),
                    in_=qt_dram.rearrange("(h d) t -> d h t", d=4)[:, h0:h0 + nh, :]), 16)
                dq5.inc(sy.dma_start(
                    out=vb(A_Q5, nh * 1024, p0 + 4, p0 + 5).rearrange("a (h t) -> a h t", h=nh),
                    in_=nb_dram.unsqueeze(0)[:, h0:h0 + nh, :]), 16)
                dck.inc(sy.dma_start(out=vb(A_CKV4, 2048, p0, p0 + 5), in_=ckv_dram), 16)

        def _ph3():
            A_AT, A_PB, A_ZT, A_VH = 0, 16384, 24576, 28672
            A_Z128, A_RZH, A_RZ, A_RZBF, A_WOUT = 46080, 46336, 46592, 47104, 20480
            A_WOC = 16384

            def q5slice(h, qsi_g, w=512):
                g = min(h // 11, 2)
                p0 = 32 * g
                nh = 11 if g < 2 else 10
                base = (h - 11 * g) * 2048 + qsi_g * 512
                return vb(A_Q5, nh * 1024, p0, p0 + 5)[:, base:base + w]

            def ckv4s(kb, p0):
                return vb(A_CKV4, 2048, p0, p0 + 5)[:, kb * 128:(kb + 1) * 128]

            def z_mm(t, ITz):
                t.wait_ge(v_zt.h, 2 * (ITz + 1))
                if ITz >= 1:
                    t.wait_ge(v_at.h, 2 * ITz)
                t.matmul(PS[0:1, 3072:3584], s_ones128[:],
                         vb(A_ZT, 256), start=True, stop=False)
                t_zm.inc(t.matmul(PS[0:1, 3072:3584], s_ones128[:],
                                  vb(A_ZT + 2048, 256), start=False, stop=True))

            def z_bc(t, ITz):
                t.wait_ge(v_rz.h, ITz + 1)
                t_zb.inc(t.matmul(PS[:, 3072:3584], s_ones1[:],
                                  vb(A_RZH, 256, 0, 1), start=True, stop=True))

            def attnv_half(t, H):
                ITp, hb = H // 2, H % 2
                HIp = ITp // 2
                t.wait_ge(s_ex.h, 8 * H + 8)
                t.wait_ge(v_ve.h, 8 * HIp + 4 + 4 * hb)
                if hb == 0 and ITp >= 1:
                    t.wait_ge(v_at.h, 2 * ITp)
                bc = 2048 + hb * 512
                for i in range(16):
                    kb = 16 * hb + i
                    ins = t.matmul(
                        PS[:, bc:bc + 512],
                        vb(A_VH + (HIp % 2) * 2048, 2048)[:, kb * 128:(kb + 1) * 128],
                        vb(A_PB, 8192)[:, kb * 512:(kb + 1) * 512],
                        start=(i == 0), stop=(i == 15))
                t_av.inc(ins, 1)

            @block.tensor
            def _(t):
                for qcg in range(2):
                    for h in range(32):
                        HI = qcg * 32 + h
                        for g8 in range(8):
                            if HI == 0 and g8 == 0:
                                t.wait_ge(vnb.h, 4)
                                t.wait_ge(dck.h, dck.n)
                                t.wait_ge(dq5.h, dq5.n)
                            if 8 * HI + g8 >= 1:
                                t.wait_ge(v_ve.h, 8 * HI + g8)
                            for j in range(4):
                                kb = g8 * 4 + j
                                ins = t.matmul(PS[:, 3584 + j * 128:3584 + (j + 1) * 128],
                                               ckv4s(kb, 0), s_wv5[:, h * 128:(h + 1) * 128],
                                               start=True, stop=True)
                            t_v.inc(ins, 4)
                        for qsi in range(2):
                            if ph3sub < 2:
                                continue
                            IT = qcg * 64 + h * 2 + qsi
                            QSI = qcg * 2 + qsi
                            for p in range(16):
                                PI = 16 * IT + p
                                if p == 0 and IT == 64 and ph3sub >= 4:
                                    t.wait_ge(s_wo.h, 64)
                                if PI >= 2:
                                    t.wait_ge(s_ex.h, PI - 1)
                                cb = (PI % 2) * 1024
                                t.matmul(PS[:, cb:cb + 512], ckv4s(2 * p, 32 * min(h // 11, 2)),
                                         q5slice(h, QSI), start=True, stop=True)
                                ins = t.matmul(PS[:, cb + 512:cb + 1024],
                                               ckv4s(2 * p + 1, 32 * min(h // 11, 2)),
                                               q5slice(h, QSI), start=True, stop=True)
                                t_sc.inc(ins, 2)
                                if p == 7 and ph3sub >= 3 and 2 * IT - 1 >= 2 * qcg * 64:
                                    z_mm(t, IT - 1)
                                    attnv_half(t, 2 * IT - 1)
                                if p == 10 and ph3sub >= 3 and IT - 1 >= qcg * 64:
                                    z_bc(t, IT - 1)
                            if ph3sub < 3:
                                continue
                            attnv_half(t, 2 * IT)
                    if ph3sub >= 3:
                        z_mm(t, qcg * 64 + 63)
                        attnv_half(t, 2 * (qcg * 64 + 63) + 1)
                        z_bc(t, qcg * 64 + 63)
                    if ph3sub < 4:
                        continue
                    t.wait_ge(v_at.h, 2 * 64 * (qcg + 1))
                    for dmt in range(32):
                        g_wo.wait_done(t, qcg * 32 + dmt)
                        for qc4 in range(2):
                            k = qcg * 64 + dmt * 2 + qc4
                            if k >= 2:
                                t.wait_ge(s_wo.h, k - 1)
                            for hd in range(32):
                                ins = t.matmul(
                                    PS[:, (k % 2) * 512:(k % 2) * 512 + 512],
                                    vb(A_WOC + (dmt % 2) * 2048, 2048)[:, hd * 128:(hd + 1) * 128],
                                    vb(A_AT, 16384)[:, hd * 1024 + qc4 * 512: hd * 1024 + qc4 * 512 + 512],
                                    start=(hd == 0), stop=(hd == 31))
                            t_wo.inc(ins, 1)

            def z_inv(s, ITz):
                s.wait_ge(t_zm.h, ITz + 1)
                ssync(s, s.activation(vf(A_RZBF, 512, 0, 1), PS[0:1, 3072:3584], AF.Ln))
                v_rz.inc(s.activation(vb(A_RZH, 256, 0, 1), vf(A_RZBF, 512, 0, 1),
                                      AF.Exp, scale=-1.0))

            @block.scalar
            def _(s):
                for qcg in range(2):
                    for h in range(32):
                        for qsi in range(2):
                            IT = qcg * 64 + h * 2 + qsi
                            if ph3sub < 2:
                                continue
                            for p in range(16):
                                PI = 16 * IT + p
                                s.wait_ge(t_sc.h, 2 * (PI + 1))
                                if IT >= 1 and ph3sub >= 3:
                                    s.wait_ge(t_av.h, 2 * (IT - 1) + p // 8 + 1)
                                if p % 8 == 0 and IT >= 1 and ph3sub >= 3:
                                    s.wait_ge(v_zt.h, 2 * (IT - 1) + p // 8 + 1)
                                if p == 0 and IT == 64 and ph3sub >= 4:
                                    s.wait_ge(t_wo.h, 64)
                                    s.wait_ge(s_wo.h, 64)
                                    st_wo.wait_total(s, 64)
                                cb = (PI % 2) * 1024
                                s_ex.inc(s.activation(
                                    vb(A_PB, 8192)[:, p * 1024:(p + 1) * 1024],
                                    PS[:, cb:cb + 1024], AF.Exp))
                                if p == 7 and IT - 1 >= qcg * 64 and ph3sub >= 3:
                                    z_inv(s, IT - 1)
                    if ph3sub >= 3:
                        z_inv(s, qcg * 64 + 63)
                    for dmt in range(32):
                        if ph3sub < 4:
                            continue
                        for qc4 in range(2):
                            k = qcg * 64 + dmt * 2 + qc4
                            s.wait_ge(t_wo.h, k + 1)
                            if k >= 2:
                                st_wo.wait(s, k, k // 2)
                            s_wo.inc(s.activation(
                                vb(A_WOUT + (k % 2) * 256, 256), PS[:, (k % 2) * 512:(k % 2) * 512 + 512],
                                AF.Identity, bias=s_bo[:, dmt:dmt + 1]))

            def att_evac(v, ITE):
                hE, qsE = (ITE % 64) // 2, ITE % 2
                v.wait_ge(t_zb.h, ITE + 1)
                v_at.inc(v.engine_nop())
                v.wait_ge(t_av.h, 2 * (ITE + 1))
                if ITE == 64 and ph3sub >= 4:
                    v.wait_ge(t_wo.h, 64)
                vsync(v, v.tensor_copy(vb(A_RZ, 256), PS[:, 2048:2560]))
                vsync(v, v.tensor_tensor(vb(A_RZ + 256, 256), PS[:, 2560:3072],
                                         vb(A_RZ, 256), op=ALU.add))
                v_at.inc(v.tensor_tensor(
                    vb(A_AT, 16384)[:, hE * 1024 + qsE * 512: hE * 1024 + qsE * 512 + 512],
                    PS[:, 3072:3584], vb(A_RZ + 256, 256), op=ALU.mult))

            @block.vector
            def _(v):
                for qcg in range(2):
                    for h in range(32):
                        HI = qcg * 32 + h
                        for g8 in range(8):
                            v.wait_ge(t_v.h, 32 * HI + 4 * (g8 + 1))
                            if g8 == 0 and HI >= 2 and ph3sub >= 3:
                                v.wait_ge(t_av.h, 2 * (2 * HI - 2))
                            v_ve.inc(v.tensor_copy(
                                vb(A_VH + (HI % 2) * 2048, 2048)[:, g8 * 512:(g8 + 1) * 512],
                                PS[:, 3584:4096]))
                        for qsi in range(2):
                            if ph3sub < 3:
                                continue
                            IT = qcg * 64 + h * 2 + qsi
                            zt = vb(A_ZT, 4096)
                            pb = vb(A_PB, 8192)
                            v.wait_ge(s_ex.h, 16 * IT + 8)
                            v.wait_ge(t_zm.h, IT)
                            vsync(v, v.tensor_tensor(zt[:, 0:4096], pb[:, 0:4096],
                                                     pb[:, 4096:8192], op=ALU.add))
                            vsync(v, v.tensor_tensor(zt[:, 0:2048], zt[:, 0:2048],
                                                     zt[:, 2048:4096], op=ALU.add))
                            vsync(v, v.tensor_tensor(zt[:, 0:1024], zt[:, 0:1024],
                                                     zt[:, 1024:2048], op=ALU.add))
                            v_zt.inc(v.tensor_tensor(zt[:, 0:512], zt[:, 0:512],
                                                     zt[:, 512:1024], op=ALU.add))
                            if IT - 1 >= qcg * 64:
                                att_evac(v, IT - 1)
                                v.wait_ge(v_at.h, v_at.n)
                            v.wait_ge(s_ex.h, 16 * (IT + 1))
                            vsync(v, v.tensor_tensor(zt[:, 4096:8192], pb[:, 8192:12288],
                                                     pb[:, 12288:16384], op=ALU.add))
                            vsync(v, v.tensor_tensor(zt[:, 4096:6144], zt[:, 4096:6144],
                                                     zt[:, 6144:8192], op=ALU.add))
                            vsync(v, v.tensor_tensor(zt[:, 4096:5120], zt[:, 4096:5120],
                                                     zt[:, 5120:6144], op=ALU.add))
                            v_zt.inc(v.tensor_tensor(zt[:, 4096:4608], zt[:, 4096:4608],
                                                     zt[:, 4608:5120], op=ALU.add))
                    if ph3sub >= 3:
                        att_evac(v, qcg * 64 + 63)


            @block.sync
            def _(sy):
                for qcg in range(2):
                    for dmt in range(32):
                        if ph3sub < 4:
                            continue
                        for qc4 in range(2):
                            k = qcg * 64 + dmt * 2 + qc4
                            sy.wait_ge(s_wo.h, k + 1)
                            st_wo.inc(sy.dma_start(
                                out=aot_dram[dmt * 128:(dmt + 1) * 128,
                                             qcg * 1024 + qc4 * 512: qcg * 1024 + qc4 * 512 + 512],
                                in_=vb(A_WOUT + (k % 2) * 256, 256)), k)

            @block.gpsimd
            def _(g):
                for qcg in range(2):
                    for dmt in range(32):
                        if ph3sub < 4:
                            continue
                        if dmt < 2:
                            g.wait_ge(t_av.h, 2 * 64 * (qcg + 1))
                            g.wait_ge(v_zt.h, 128 * (qcg + 1))
                        else:
                            g.wait_ge(t_wo.h, qcg * 64 + 2 * dmt - 2)
                        g_wo.inc(g.dma_start(
                            out=vb(A_WOC + (dmt % 2) * 2048, 2048).rearrange("p (hd c) -> p hd c", c=128),
                            in_=d_wo.rearrange("(hd p) c -> p hd c", p=128)[:, :, dmt * 128:(dmt + 1) * 128]),
                            qcg * 32 + dmt)

        nc.leave_named_scope("ph2", sc2, False)

        @block.gpsimd
        def _(g):
            g.dma_start(out=barr_bf, in_=d_barr).then_inc(c_ld.h, 16)
        c_ld.n += 16

        sc3, _ = nc.enter_named_scope("ph3", False)
        if phases >= 3:
            _ph3()
        nc.leave_named_scope("ph3", sc3, False)

        M_AB, M_HGT, M_GX = 0, 16384, 24576
        M_XT1, M_XT2 = 26624, 27648
        M_MX = [28672, 32768]
        M_SQS = 36864
        M_MAT = [40960, 43008]
        M_SM = 45056
        G_SCR, G_SCR2, G_M8 = M_SM, M_SM + 64, M_SM + 128
        G_NEGM, G_ZD, G_Z16, G_RZ, G_RZ2 = M_SM + 144, M_SM + 145, M_SM + 146, M_SM + 147, M_SM + 148
        G_MASK = M_SM + 152
        G_ME, G_E16O = M_SM + 184, M_SM + 248
        G_G, G_GT32, G_GTB = M_SM + 264, M_SM + 328, M_SM + 456
        GT4 = 46336
        M_MO = [45824, 46336]

        def msm(bi, j):
            return vf(46848 + bi * 16 + j, 1)

        def moe_call(call, d_src, d_dst, d_res):
            KB = call * 16
            RB = call * 128
            HB = call * 128
            GB = call * 16
            XB = call * 32

            @block.gpsimd
            def _(g):
                if call == 0:
                    g.wait_ge(t_wo.h, 128)
                else:
                    g.wait_ge(fs_x3.h, 256)
                    g.wait_ge(fv_x3.h, 16)
                    g.wait_ge(ft_B.h, 1024)
                    g.wait_ge(fv_ev.h, 1024)
                    g.wait_ge(fv_g.h, 48)
                    g.wait_ge(fv_pe.h, 256)
                    g.wait_ge(fv_gw.h, 256)
                    g.wait_ge(fv_he.h, 256)
                mg_aA.inc(g.dma_start(
                    out=vb(M_AB, 16384).rearrange("p (c e) -> p c e", e=1024),
                    in_=d_at.rearrange("(c p) e -> p c e", p=128)), 16)
                for k in range(16):
                    kk = KB + k
                    if k == 0 and call == 0:
                        g.wait_ge(t_av.h, 256)
                        g.wait_ge(t_sc.h, 4096)
                    if k >= 2:
                        g.wait_ge(mv_1.h, 3 * (kk - 2) + 3)
                        g.wait_ge(mc_s.h, 2 * (kk - 2) + 1)
                        if call == 0:
                            ms_x1.wait(g, k, k // 2)
                    if call == 1:
                        g.wait_ge(fs_x3.h, 16 * (k + 1))
                        if k == 0:
                            g.wait_ge(fs_xn.h, 256)
                            g.wait_ge(fs_g.h, 256)
                    row = k * 128
                    mg_x.inc(g.dma_start(out=vf(M_MX[k % 2], 4096),
                                         in_=d_src[row:row + 128, :]), kk)
                g.wait_ge(tm_h.h, HB + 128)
                g.wait_ge(c_ld.h, 272)
                mg_aB.inc(g.dma_start(
                    out=vb(M_AB, 16384).rearrange("p (e d) -> p e d", d=4096),
                    in_=barr_bf.rearrange("(e p) d -> p e d", p=128)), 16)
                for k in range(16):
                    kk = KB + k
                    if k >= 2:
                        g.wait_ge(mv_oe.h, call * 128 + (k - 2) * 8 + 8)
                        g.wait_ge(mc_s.h, 2 * KB + 32)
                    else:
                        if call == 0:
                            ms_x1.wait_total(g, 16)
                    if call == 1:
                        g.wait_ge(fs_x3.h, 16 * (k + 1))
                    row = k * 128
                    mg_r.inc(g.dma_start(out=vf(M_MX[k % 2], 4096),
                                         in_=d_res[row:row + 128, :]), kk)

            @block.sync
            def _(sy):
                if call == 0:
                    st_wo.wait_total(sy, 128)
                    sy.wait_ge(t_sc.h, 4096)
                    sy.wait_ge(t_v.h, 2048)
                for k in range(16):
                    kk = KB + k
                    if call == 0:
                        if k >= 2:
                            sy.wait_ge(mv_1.h, 3 * (k - 2) + 3)
                            ms_xn.wait(sy, k, k // 2)
                        ms_t.inc(sy.dma_start(out=vb(M_MAT[k % 2], 2048),
                                              in_=aot_dram[:, k * 128:(k + 1) * 128],
                                              transpose=True), k)
                        sy.wait_ge(mv_1.h, 3 * k + 1)
                        ms_x1.inc(sy.dma_start(out=x1_dram[k * 128:(k + 1) * 128, :],
                                               in_=vf(M_MX[k % 2], 4096)), k)
                    sy.wait_ge(mv_1.h, 3 * kk + 3)
                    ms_xn.inc(sy.dma_start(out=xnm_dram[k * 128:(k + 1) * 128, :],
                                           in_=vb(M_MAT[k % 2], 2048)), kk)
                def t2_loads(hp):
                    for ci in range(32):
                        H = HB + hp * 32 + ci
                        if H >= 4:
                            sy.wait_ge(tm_h.h, H - 3)
                        ms_T2.inc(sy.dma_start(out=vb(M_XT2 + (ci % 4) * 256, 256),
                                               in_=xnm_dram[hp * 512:(hp + 1) * 512,
                                                            ci * 128:(ci + 1) * 128],
                                               transpose=True), H)

                for mc in range(4):
                    ms_xn.wait_total(sy, KB + mc * 4 + 4)
                    for ci in range(32):
                        R = RB + mc * 32 + ci
                        if call == 0 and mc == 0 and ci == 0:
                            sy.wait_ge(v_zt.h, 256)
                        if R >= 4:
                            sy.wait_ge(tm_r.h, R - 3)
                        ms_T1.inc(sy.dma_start(out=vb(M_XT1 + (ci % 4) * 256, 256),
                                               in_=xnm_dram[mc * 512:(mc + 1) * 512,
                                                            ci * 128:(ci + 1) * 128],
                                               transpose=True), R)
                    if mc >= 1:
                        t2_loads(mc - 1)
                t2_loads(3)
                for k in range(16):
                    for dc in range(8):
                        kk = (KB + k) * 8 + dc
                        sy.wait_ge(mv_oe.h, kk + 1)
                        row = k * 128
                        ms_o.inc(sy.dma_start(out=d_dst[row:row + 128, dc * 512:(dc + 1) * 512],
                                              in_=vf(M_MO[kk % 2], 512)), kk)

            @block.vector
            def _(v):
                for k in range(16):
                    kk = KB + k
                    bi = k % 2
                    mg_x.wait_done(v, kk)
                    if k == 0 and call == 0:
                        v.wait_ge(t_sc.h, 4096)
                        v.wait_ge(t_v.h, 2048)
                    if call == 0:
                        ms_t.wait_done(v, k)
                        vsync(v, v.tensor_tensor(vf(M_MX[bi], 4096), vf(M_MX[bi], 4096),
                                                 vb(M_MAT[bi], 2048), op=ALU.add))
                    vsync(v, v.reduce_sum(msm(bi, 0), vf(M_MX[bi], 4096), axis=AX.X))
                    mv_1.inc(v.tensor_scalar(msm(bi, 1), msm(bi, 0), -1.0 / D, None,
                                             op0=ALU.mult))
                    vsync(v, v.engine_nop())
                    v.wait_ge(mc_s.h, 2 * kk + 1)
                    vsync(v, v.tensor_tensor(msm(bi, 2), msm(bi, 1), msm(bi, 1),
                                             op=ALU.mult))
                    vsync(v, v.tensor_scalar(msm(bi, 3), msm(bi, 4), 1.0 / D, None,
                                             op0=ALU.mult))
                    mv_1.inc(v.tensor_tensor(msm(bi, 5), msm(bi, 3), msm(bi, 2),
                                             op=ALU.subtract))
                    v.wait_ge(mc_s.h, 2 * kk + 2)
                    vsync(v, v.tensor_tensor(msm(bi, 6), msm(bi, 1), msm(bi, 7),
                                             op=ALU.mult))
                    if kk >= 2:
                        ms_xn.wait(v, kk, kk // 2)
                    mv_1.inc(v.tensor_scalar(vb(M_MAT[bi], 2048), vf(M_MX[bi], 4096),
                                             msm(bi, 7), msm(bi, 6),
                                             op0=ALU.mult, op1=ALU.add))
                def hg_evac(v, hp):
                    v.wait_ge(tm_h.h, HB + (hp + 1) * 32)
                    for er in range(8):
                        mv_hg.inc(v.tensor_tensor(
                            vb(M_HGT, 8192)[:, hp * 4096 + er * 512:hp * 4096 + (er + 1) * 512],
                            PS[:, er * 512:(er + 1) * 512],
                            vb(M_GX, 2048)[:, er * 512:(er + 1) * 512], op=ALU.mult))

                for mc in range(4):
                    v.wait_ge(tm_r.h, RB + (mc + 1) * 32)
                    if call == 1 and mc == 0:
                        ms_o.wait_total(v, 128)
                    for tt in range(4):
                        ins = v.tensor_copy(vf(GT4 + tt * 64, 64),
                                            PS[:, tt * 512:tt * 512 + 64])
                        if tt < 3:
                            vsync(v, ins)
                    mv_g.inc(ins)
                    for tt in range(4):
                        Gi = GB + mc * 4 + tt
                        scr = vf(GT4 + tt * 64, 64)
                        vsync(v, v.max(out=vf(G_M8, 8), in_=scr))
                        vsync(v, v.match_replace(out=vf(G_SCR2, 64), in_to_replace=vf(G_M8, 8),
                                                 in_values=scr, imm_value=NEG))
                        vsync(v, v.max(out=vf(G_M8 + 8, 8), in_=vf(G_SCR2, 64)))
                        mv_g.inc(v.tensor_scalar(vf(G_NEGM, 1), vf(G_M8, 1), -1.0,
                                                 None, op0=ALU.mult))
                        v.wait_ge(mc_e.h, Gi + 1)
                        vsync(v, v.reciprocal(vf(G_RZ, 1), vf(G_Z16, 1)))
                        vsync(v, v.tensor_scalar(vf(G_RZ2, 1), vf(G_RZ, 1), SCALING, None,
                                                 op0=ALU.mult))
                        vsync(v, v.tensor_scalar(vb(G_MASK, 32), scr,
                                                 vf(G_M8 + 15, 1), None, op0=ALU.is_ge))
                        vsync(v, v.scalar_tensor_tensor(vf(G_G, 64), vf(G_ME, 64),
                                                        vf(G_RZ2, 1), vb(G_MASK, 32),
                                                        op0=ALU.mult, op1=ALU.mult))
                        for bi4 in range(4):
                            for bj in range(2):
                                ins = v.transpose(
                                    vf(G_GT32, 128, 32 * bj, 32 * bj + 32)[:, 32 * bi4:32 * bi4 + 32],
                                    vf(G_G, 64, 32 * bi4, 32 * bi4 + 32)[:, 32 * bj:32 * bj + 32])
                        vsync(v, ins)
                        mv_g.inc(v.tensor_copy(
                            vb(G_GTB, 256, 0, 64)[:, tt * 128:(tt + 1) * 128],
                            vf(G_GT32, 128, 0, 64)))
                    if mc >= 1:
                        hg_evac(v, mc - 1)
                    for er in range(8):
                        gx = XB + mc * 8 + er
                        v.wait_ge(tm_g.h, gx + 1)
                        mv_gx.inc(v.tensor_copy(vb(M_GX, 2048)[:, er * 512:(er + 1) * 512],
                                                PS[:, 4 * 512:4 * 512 + 512]))
                hg_evac(v, 3)
                for k in range(16):
                    for dc in range(8):
                        kk = (KB + k) * 8 + dc
                        v.wait_ge(tm_b.h, kk + 1)
                        mg_r.wait_done(v, KB + k)
                        if kk >= 2:
                            ms_o.wait(v, kk, kk // 2)
                        mv_oe.inc(v.tensor_tensor(
                            vf(M_MO[kk % 2], 512),
                            PS[:, (kk % 2) * 512:(kk % 2) * 512 + 512],
                            vf(M_MX[k % 2], 4096)[:, dc * 512:(dc + 1) * 512], op=ALU.add))

            @block.scalar
            def _(s):
                for k in range(16):
                    kk = KB + k
                    bi = k % 2
                    s.wait_ge(mv_1.h, 3 * kk + 1)
                    ssync(s, s.memzero(msm(bi, 4)))
                    mc_s.inc(s.activation(vf(M_SQS, 4096), vf(M_MX[bi], 4096),
                                          AF.Square, accum_out=msm(bi, 4)))
                    s.wait_ge(mv_1.h, 3 * kk + 2)
                    ssync(s, s.activation(msm(bi, 8), msm(bi, 5), AF.Ln, bias=s_eps[:]))
                    mc_s.inc(s.activation(msm(bi, 7), msm(bi, 8), AF.Exp, scale=-0.5))
                for mc in range(4):
                    for tt in range(4):
                        Gi = GB + mc * 4 + tt
                        s.wait_ge(mv_g.h, call * 36 + mc * 9 + 2 * tt + 2)
                        s.memzero(vf(G_ZD, 1))
                        ssync(s, s.memzero(vf(G_Z16, 1)))
                        s.activation(vf(G_ME, 64), vf(GT4 + tt * 64, 64), AF.Exp,
                                     bias=vf(G_NEGM, 1), accum_out=vf(G_ZD, 1))
                        mc_e.inc(s.activation(vf(G_E16O, 16), vf(G_M8, 16), AF.Exp,
                                              bias=vf(G_NEGM, 1), accum_out=vf(G_Z16, 1)))

            @block.tensor
            def _(t):
                def hpass(t, hp):
                    for ci in range(32):
                        H = HB + hp * 32 + ci
                        ms_T2.wait_done(t, H)
                        if ci == 0:
                            t.wait_ge(mg_aA.h, 16 * (call + 1))
                            t.wait_ge(mv_gx.h, XB + hp * 8 + 8)
                            if hp <= 2:
                                t.wait_ge(mv_g.h, call * 36 + (hp + 1) * 9 + 1)
                            if call == 0 and hp == 0:
                                t.wait_ge(v_rz.h, 128)
                                t.wait_ge(v_at.h, 256)
                                t.wait_ge(v_ve.h, 512)
                            if hp > 0 or call == 1:
                                t.wait_ge(mv_hg.h, call * 32 + hp * 8)
                        for er in range(8):
                            ins = t.matmul(PS[:, er * 512:(er + 1) * 512],
                                           vb(M_AB, 16384)
                                           [:, ci * 1024 + er * 128:ci * 1024 + (er + 1) * 128],
                                           vb(M_XT2 + (ci % 4) * 256, 256),
                                           start=(ci == 0), stop=(ci == 31))
                        tm_h.inc(ins)

                for mc in range(4):
                    for ci in range(32):
                        R = RB + mc * 32 + ci
                        ms_T1.wait_done(t, R)
                        if ci == 0 and mc >= 1:
                            t.wait_ge(mv_hg.h, call * 32 + (mc - 1) * 8)
                        if R == 0:
                            t.wait_ge(s_wo.h, 128)
                            t.wait_ge(s_ex.h, 2048)
                        if call == 1 and R == RB:
                            t.wait_ge(mv_oe.h, 128)
                            t.wait_ge(fv_ev.h, fv_ev.n)
                            t.wait_ge(fse_A.h, fse_A.n)
                            t.wait_ge(fve_A.h, fve_A.n)
                            t.wait_ge(fv_pe.h, fv_pe.n)
                            t.wait_ge(fv_gw.h, fv_gw.n)
                            t.wait_ge(fv_he.h, fv_he.n)
                        for tt in range(4):
                            ins = t.matmul(PS[:, tt * 512:tt * 512 + 64],
                                           vb(M_XT1 + (ci % 4) * 256, 256)
                                           [:, tt * 128:(tt + 1) * 128],
                                           s_wrt[:, ci * 64:(ci + 1) * 64],
                                           start=(ci == 0), stop=(ci == 31))
                        tm_r.inc(ins)
                    if mc >= 1:
                        hpass(t, mc - 1)
                    for er in range(8):
                        gx = XB + mc * 8 + er
                        t.wait_ge(mv_g.h, call * 36 + mc * 9 + 9)
                        if er == 0 and mc >= 1:
                            t.wait_ge(mv_hg.h, call * 32 + (mc - 1) * 8 + 5)
                        if call == 0 and mc == 0 and er == 0:
                            t.wait_ge(v_at.h, 256)
                        if gx >= 1:
                            t.wait_ge(mv_gx.h, gx)
                        tm_g.inc(t.matmul(PS[:, 4 * 512:4 * 512 + 512],
                                          s_efull[:, er * 128:(er + 1) * 128],
                                          vb(G_GTB, 256, 0, 64), start=True, stop=True))
                hpass(t, 3)
                for k in range(16):
                    mc, tt = k // 4, k % 4
                    for dc in range(8):
                        kk = (KB + k) * 8 + dc
                        t.wait_ge(mg_aB.h, 16 * (call + 1))
                        if k == 0 and dc == 0:
                            t.wait_ge(mv_hg.h, call * 32 + 32)
                        if kk >= 2:
                            t.wait_ge(mv_oe.h, kk - 1)
                        for er in range(8):
                            ins = t.matmul(
                                PS[:, (kk % 2) * 512:(kk % 2) * 512 + 512],
                                vb(M_HGT, 8192)[:, mc * 4096 + er * 512 + tt * 128:
                                                mc * 4096 + er * 512 + (tt + 1) * 128],
                                vb(M_AB, 16384)[:, er * 4096 + dc * 512:er * 4096 + (dc + 1) * 512],
                                start=(er == 0), stop=(er == 7))
                        tm_b.inc(ins)

        sc4, _ = nc.enter_named_scope("ph4", False)
        if phases >= 4:
            moe_call(0, d_xq, x2_dram, x1_dram)
        nc.leave_named_scope("ph4", sc4, False)

        if phases <= 4:
            @block.gpsimd
            def _(g):
                if phases == 4:
                    ms_o.wait_total(g, 128)
                elif phases == 3:
                    if ph3sub >= 4:
                        st_wo.wait_total(g, 128)
                    elif ph3sub == 3:
                        g.wait_ge(v_at.h, 256)
                    elif ph3sub == 2:
                        g.wait_ge(s_ex.h, 2048)
                    else:
                        g.wait_ge(v_ve.h, 512)
                elif phases == 2:
                    g.wait_ge(dq5.h, dq5.n)
                else:
                    stq.wait_total(g, 32)
            return nc
        F_X2 = [0, 4096]
        F_V, F_ST = 8192, 12288
        F_W1, F_W2, F_W3, F_W4 = 20480, 24576, 28672, 32768
        F_XN, F_SM, F_SLAB = 36864, 38912, 39424
        F_E, F_MASK, F_G = 30720, 24576, 36864

        def fsm5(j):
            return vf(F_SM + j, 1)

        NCH = OWN // FCH
        fse_A, fve_A = sqt, vmc
        FSE0, FVE0, FCS0 = fse_A.n, fve_A.n, 0
        s_slabF = vb(F_SLAB, 4096, 0, 64)
        s_slabF2 = vb(F_SLAB + 4096, 4096, 0, 64)
        st4 = vb(F_ST, 8192, 0, 64).rearrange("y (r c j) -> y r c j", r=128, c=2, j=64)
        P4 = vb(F_W1, 4096, 0, 64).rearrange("k (j r) -> k j r", r=128)
        H4 = vb(F_W4, 4096).rearrange("p (j r) -> p j r", r=128)

        sc5, _ = nc.enter_named_scope("ph5", False)
        @block.tensor
        def _(t):
            for fc in range(NCH):
                for tf in range(4):
                    T = 4 * fc + tf
                    bmap = [0, 1, 6, 7] if tf < 3 else [0, 1, 2, 3]
                    for g4 in range(32):
                        if T == 0 and g4 == 0:
                            t.wait_ge(mv_oe.h, 128)
                        if g4 == 0:
                            if tf == 0:
                                t.wait_ge(fs_D.h, 16 * (fc + 1))
                            elif tf == 1:
                                t.wait_ge(fv_pe.h, 16 * (fc + 1))
                            elif tf == 2:
                                t.wait_ge(fs_gD.h, 16 * (fc + 1))
                            else:
                                t.wait_ge(fv_he.h, 16 * (fc + 1))
                            if T >= 1:
                                t.wait_ge(fse_A.h, FSE0 + 8 * T)
                                t.wait_ge(fve_A.h, FVE0 + 8 * T)
                        if g4 == 2:
                            if tf == 0 and fc >= 1:
                                t.wait_ge(fv_he.h, 16 * fc)
                            if tf == 3:
                                t.wait_ge(fv_ev.h, 16 * (4 * fc + 3))
                        if g4 >= 4:
                            q2 = g4 // 2 - 2
                            if q2 % 2 == 0:
                                t.wait_ge(fse_A.h, FSE0 + 8 * T + q2 // 2 + 1)
                            else:
                                t.wait_ge(fve_A.h, FVE0 + 8 * T + q2 // 2 + 1)
                        bk = bmap[g4 % 4]
                        for j in range(4):
                            r = g4 * 4 + j
                            ob = PS[0:64, bk * 512 + j * 128:bk * 512 + (j + 1) * 128]
                            if tf == 0:
                                ins = t.matmul(ob, vb(F_W4, 4096, 0, 64)[:, r * 64:(r + 1) * 64],
                                               s_fa[:], start=True, stop=True)
                            elif tf == 2:
                                ins = t.matmul(ob, vb(F_W1, 4096, 0, 64)[:, r * 64:(r + 1) * 64],
                                               s_fa[:], start=True, stop=True)
                            elif tf == 1:
                                ins = t.matmul(ob, P4[:, :, r], s_fa[:], start=True, stop=True)
                            else:
                                ins = t.matmul(ob, H4[:, :, r], s_fha[:], start=True, stop=True)
                        ft_A.inc(ins)
                    for jg in range(16):
                        GP = 64 * fc + 16 * tf + jg
                        if jg == 0:
                            t.wait_ge(fse_A.h, FSE0 + 8 * (T + 1))
                            t.wait_ge(fve_A.h, FVE0 + 8 * (T + 1))
                            if T == 0:
                                t.wait_ge(fs_cs.h, FCS0 + 16)
                            if tf == 3:
                                t.wait_ge(fv_gw.h, 16 * (fc + 1))
                        if GP >= 2:
                            t.wait_ge(fv_ev.h, GP - 1)
                        bb = (2 + jg % 2) if tf < 3 else (4 + jg % 2)
                        for q in range(4):
                            j1 = 4 * jg + q
                            if tf in (0, 2):
                                ob = PS[0:128, bb * 512 + q * 128:bb * 512 + (q + 1) * 128]
                                t.matmul(ob, s_slabF[:, j1 * 128:(j1 + 1) * 128],
                                         st4[:, :, 0, j1], start=True, stop=False)
                                ins = t.matmul(ob, s_slabF2[:, j1 * 128:(j1 + 1) * 128],
                                               st4[:, :, 1, j1], start=False, stop=True)
                            else:
                                ob = PS[0:64, bb * 512 + q * 128:bb * 512 + (q + 1) * 128]
                                t.matmul(ob, s_slabF[:, j1 * 128:j1 * 128 + 64],
                                         st4[:, :, 0, j1], start=True, stop=False)
                                ins = t.matmul(ob, s_slabF[:, j1 * 128 + 64:(j1 + 1) * 128],
                                               st4[:, :, 1, j1], start=False, stop=True)
                        ft_B.inc(ins)
                    if tf == 0:
                        for i in range(16):
                            if i == 0:
                                t.wait_ge(fc_s.h, 3 * fc + 1)
                                t.wait_ge(fv_ev.h, 64 * fc)
                                if fc == 0:
                                    t.wait_ge(mv_gx.h, 32)
                                    t.wait_ge(mv_hg.h, 32)
                            if 16 * fc + i >= 2:
                                t.wait_ge(fv_pe.h, 16 * fc + i - 1)
                            ft_P.inc(t.matmul(
                                PS[0:64, 2048 + (i % 2) * 512:2048 + (i % 2) * 512 + 512],
                                s_lh1[:, 0:64], vb(F_W2, 4096)[:, i * 512:(i + 1) * 512],
                                start=True, stop=True))
                    if tf == 2:
                        for i in range(16):
                            t.wait_ge(fv_ev.h, 64 * fc + 32 + i + 1)
                            if 16 * fc + i >= 2:
                                t.wait_ge(fv_gw.h, 16 * fc + i - 1)
                            ft_G.inc(t.matmul(
                                PS[:, 2048 + (i % 2) * 512:2048 + (i % 2) * 512 + 512],
                                s_sw[:], vb(F_W2, 4096)[:, i * 512:(i + 1) * 512],
                                start=True, stop=True))
                        for i in range(16):
                            if i == 0:
                                t.wait_ge(fv_g.h, 3 * fc + 3)
                            if 16 * fc + i >= 2:
                                t.wait_ge(fv_he.h, 16 * fc + i - 1)
                            pb = PS[:, 3072 + (i % 2) * 512:3072 + (i % 2) * 512 + 512]
                            t.matmul(pb, s_lh1[:], vb(F_W1, 4096)[:, i * 512:(i + 1) * 512],
                                     start=True, stop=False)
                            ft_H.inc(t.matmul(pb, s_lh2[:], vb(F_W3, 4096)[:, i * 512:(i + 1) * 512],
                                              start=False, stop=True))

        def ph5_ln(v, c):
            fg_x.wait_done(v, c)
            fx2 = vf(F_X2[c % 2], 4096)
            vsync(v, v.reduce_sum(fsm5(0), fx2, axis=AX.X))
            fv_ln.inc(v.tensor_scalar(fsm5(1), fsm5(0), -1.0 / D, None, op0=ALU.mult))
            vsync(v, v.engine_nop())
            v.wait_ge(fc_ln.h, 2 * c + 1)
            vsync(v, v.tensor_tensor(fsm5(48), fsm5(40), fsm5(41), op=ALU.add))
            vsync(v, v.tensor_tensor(fsm5(49), fsm5(42), fsm5(43), op=ALU.add))
            vsync(v, v.tensor_tensor(fsm5(50), fsm5(44), fsm5(45), op=ALU.add))
            vsync(v, v.tensor_tensor(fsm5(51), fsm5(46), fsm5(47), op=ALU.add))
            vsync(v, v.tensor_tensor(fsm5(48), fsm5(48), fsm5(49), op=ALU.add))
            vsync(v, v.tensor_tensor(fsm5(50), fsm5(50), fsm5(51), op=ALU.add))
            vsync(v, v.tensor_tensor(fsm5(4), fsm5(48), fsm5(50), op=ALU.add))
            vsync(v, v.tensor_tensor(fsm5(2), fsm5(1), fsm5(1), op=ALU.mult))
            vsync(v, v.tensor_scalar(fsm5(3), fsm5(4), 1.0 / D, None, op0=ALU.mult))
            fv_ln.inc(v.tensor_tensor(fsm5(5), fsm5(3), fsm5(2), op=ALU.subtract))
            v.wait_ge(fc_ln.h, 2 * c + 2)
            vsync(v, v.tensor_tensor(fsm5(6), fsm5(1), fsm5(7), op=ALU.mult))
            if c >= 1:
                v.wait_ge(fs_xn.h, 16 * c)
                v.wait_ge(fs_g.h, 16 * c)
            fv_ln.inc(v.tensor_scalar(vb(F_XN, 2048), fx2, fsm5(7), fsm5(6),
                                      op0=ALU.mult, op1=ALU.add))

        @block.vector
        def _(v):
            for fc in range(NCH):
                if fc == 0:
                    v.wait_ge(mv_oe.h, 128)
                    ph5_ln(v, 0)
                for tf in range(4):
                    T = 4 * fc + tf
                    for q in range(1, 16, 2):
                        v.wait_ge(ft_A.h, 32 * T + 2 * q + 2)
                        if q == 1:
                            v.wait_ge(ft_B.h, 16 * T)
                            if tf == 3:
                                v.wait_ge(fv_g.h, 3 * fc + 3)
                        base = 3072 if tf < 3 else 1024
                        fve_A.inc(v.tensor_copy(
                            vb(F_ST, 8192, 0, 64)[:, q * 1024:(q + 1) * 1024],
                            PS[0:64, base:base + 1024]))
                    for jg in range(16):
                        GP = 64 * fc + 16 * tf + jg
                        v.wait_ge(ft_B.h, GP + 1)
                        bb = (2 + jg % 2) if tf < 3 else (4 + jg % 2)
                        if tf == 0:
                            fv_ev.inc(v.tensor_copy(
                                vb(F_V, 4096)[:, jg * 512:(jg + 1) * 512],
                                PS[:, bb * 512:bb * 512 + 512]))
                        elif tf == 2:
                            if jg == 0:
                                v.wait_ge(fv_g.h, 3 * fc + 2)
                                v.wait_ge(fs_c.h, 16 * (fc + 1))
                            fv_ev.inc(v.tensor_copy(
                                vb(F_W2, 4096)[:, jg * 512:(jg + 1) * 512],
                                PS[:, bb * 512:bb * 512 + 512]))
                        else:
                            wreg = F_W2 if tf == 1 else F_W1
                            dst = vb(wreg, 4096, 0, 64).rearrange(
                                "p (r j) -> p r j", j=64)[:, :, 4 * jg:4 * jg + 4]
                            srcp = PS[0:64, bb * 512:bb * 512 + 512].rearrange(
                                "p (j r) -> p r j", j=4)
                            fv_ev.inc(v.tensor_copy(dst, srcp))
                    if tf == 0:
                        if fc >= 1:
                            v.wait_ge(fs_or.h, 16 * fc)
                            fv_x3.inc(v.tensor_tensor(vf(F_W3, 4096), vf(F_X2[(fc - 1) % 2], 4096),
                                                      vb(F_W2, 2048), op=ALU.add))
                        for i in range(16):
                            v.wait_ge(ft_P.h, 16 * fc + i + 1)
                            if i == 0 and fc >= 1:
                                v.wait_ge(fs_o.h, 16 * fc)
                            fv_pe.inc(v.tensor_copy(
                                vb(F_W1, 4096, 0, 64)[:, i * 512:(i + 1) * 512],
                                PS[0:64, 2048 + (i % 2) * 512:2048 + (i % 2) * 512 + 512]))
                    elif tf == 1:
                        v.wait_ge(fs_cr.h, 16 * (fc + 1))
                        vsync(v, v.max(out=vb(F_SM + 16, 4), in_=vb(F_W3, 2048)))
                        fv_g.inc(v.tensor_scalar(fsm5(24), vb(F_SM + 16, 4)[:, 0:1], -1.0,
                                                 None, op0=ALU.mult))
                        v.wait_ge(fc_s.h, 3 * fc + 3)
                        vsync(v, v.reciprocal(fsm5(25), fsm5(26)))
                        vsync(v, v.tensor_copy(fsm5(27), vb(F_SM + 16, 4)[:, 7:8]))
                        vsync(v, v.tensor_scalar(vb(F_MASK, 2048), vb(F_W3, 2048),
                                                 fsm5(27), None, op0=ALU.is_ge))
                        v.wait_ge(fs_xn.h, 16 * (fc + 1))
                        fv_g.inc(v.scalar_tensor_tensor(vb(F_G, 2048), vb(F_E, 2048),
                                                        fsm5(25), vb(F_MASK, 2048),
                                                        op0=ALU.mult, op1=ALU.mult))
                    elif tf == 2:
                        for i in range(16):
                            v.wait_ge(ft_G.h, 16 * fc + i + 1)
                            fv_gw.inc(v.tensor_copy(
                                vb(F_ST, 4096)[:, i * 512:(i + 1) * 512],
                                PS[:, 2048 + (i % 2) * 512:2048 + (i % 2) * 512 + 512]))
                        vsync(v, v.engine_nop())
                        vsync(v, v.tensor_tensor(vb(F_W1, 4096), vb(F_V, 4096),
                                                 vb(F_W2, 4096), op=ALU.mult))
                        fv_g.inc(v.tensor_tensor(vb(F_W3, 4096), vb(F_V, 4096),
                                                 vb(F_ST, 4096), op=ALU.mult))
                        for i in range(16):
                            v.wait_ge(ft_H.h, 16 * fc + i + 1)
                            fv_he.inc(v.tensor_copy(
                                vb(F_W4, 4096)[:, i * 512:(i + 1) * 512],
                                PS[:, 3072 + (i % 2) * 512:3072 + (i % 2) * 512 + 512]))
                        if fc + 1 < NCH:
                            ph5_ln(v, fc + 1)

            v.wait_ge(fs_or.h, 16 * NCH)
            fv_x3.inc(v.tensor_tensor(vf(F_W3, 4096), vf(F_X2[(NCH - 1) % 2], 4096),
                                      vb(F_W2, 2048), op=ALU.add))

        def ph5_lns(s, c):
            s.wait_ge(fv_ln.h, 3 * c + 1)
            for i in range(8):
                ssync(s, s.memzero(fsm5(40 + i)))
                ins = s.activation(vb(47616, 256),
                                   vf(F_X2[c % 2], 4096)[:, i * 512:(i + 1) * 512],
                                   AF.Square, accum_out=fsm5(40 + i))
                if i < 7:
                    ssync(s, ins)
            fc_ln.inc(ins)
            s.wait_ge(fv_ln.h, 3 * c + 2)
            ssync(s, s.activation(fsm5(8), fsm5(5), AF.Ln, bias=s_eps[:]))
            fc_ln.inc(s.activation(fsm5(7), fsm5(8), AF.Exp, scale=-0.5))

        @block.scalar
        def _(s):
            for fc in range(NCH):
                if fc == 0:
                    ph5_lns(s, 0)
                for tf in range(4):
                    T = 4 * fc + tf
                    for q in range(0, 16, 2):
                        s.wait_ge(ft_A.h, 32 * T + 2 * q + 2)
                        if q == 0:
                            s.wait_ge(ft_B.h, 16 * T)
                            if tf == 3:
                                s.wait_ge(fv_g.h, 3 * fc + 3)
                        fse_A.inc(s.activation(
                            vb(F_ST, 8192, 0, 64)[:, q * 1024:(q + 1) * 1024],
                            PS[0:64, 0:1024], AF.Identity))
                    if tf == 0:
                        s.wait_ge(fv_ev.h, 64 * fc + 16)
                        s.wait_ge(fv_x3.h, fc)
                        fc_s.inc(s.activation(vb(F_W2, 4096), vb(F_V, 4096), AF.Square))
                    elif tf == 1:
                        s.wait_ge(fv_g.h, 3 * fc + 1)
                        s.wait_ge(fs_cr.h, 16 * (fc + 1))
                        fc_s.inc(s.activation(vb(F_E, 2048), vb(F_W3, 2048), AF.Exp,
                                              bias=fsm5(24)))
                        ssync(s, s.memzero(fsm5(26)))
                        fc_s.inc(s.activation(vf(F_SM + 32, 8), vb(F_SM + 16, 4), AF.Exp,
                                              bias=fsm5(24), accum_out=fsm5(26)))
                    elif tf == 2 and fc + 1 < NCH:
                        ph5_lns(s, fc + 1)

        @block.sync
        def _(sy):
            def do_xn_D(c):
                row2 = (c % 2) * FCH
                sy.wait_ge(fv_ln.h, 3 * c + 3)
                fs_xn.inc(sy.dma_start(out=fb_dram[row2:row2 + FCH, :], in_=vb(F_XN, 2048)), 16)
                sy.wait_ge(fs_xn.h, 16 * (c + 1))
                if c == 0:
                    sy.wait_ge(mv_oe.h, 128)
                else:
                    sy.wait_ge(ft_A.h, 128 * c)
                fs_D.inc(sy.dma_start(
                    out=vb(F_W4, 4096, 0, 64).rearrange("x (r y) -> x r y", r=FCH),
                    in_=fb_dram[row2:row2 + FCH, :].rearrange("r (x y) -> x r y", x=64, y=64)), 16)

            do_xn_D(0)
            for fc in range(NCH):
                row2 = (fc % 2) * FCH
                if fc >= 1:
                    sy.wait_ge(fv_x3.h, fc)
                    fs_x3.inc(sy.dma_start(out=x3_dram[(fc - 1) * FCH:fc * FCH, :],
                                           in_=vf(F_W3, 4096)), 16)
                sy.wait_ge(fv_ev.h, 64 * fc + 32)
                fs_c.inc(sy.dma_start(out=fc_dram[(fc % 2) * 64:(fc % 2) * 64 + 64, :],
                                      in_=vb(F_W2, 4096, 0, 64)), 16)
                sy.wait_ge(fs_c.h, 16 * (fc + 1))
                sy.wait_ge(fs_x3.h, 16 * fc)
                fs_cr.inc(sy.dma_start(
                    out=vb(F_W3, 2048).rearrange("r (l m) -> r l m", l=64),
                    in_=fc_dram[(fc % 2) * 64:(fc % 2) * 64 + 64, :]
                    .rearrange("l (r m) -> r l m", r=FCH)), 16)
                sy.wait_ge(fv_g.h, 3 * fc + 2)
                fs_g.inc(sy.dma_start(out=fg_dram[row2:row2 + FCH, :], in_=vb(F_G, 2048)), 16)
                sy.wait_ge(fs_g.h, 16 * (fc + 1))
                fs_gD.inc(sy.dma_start(
                    out=vb(F_W1, 4096, 0, 64).rearrange("x (r y) -> x r y", r=FCH),
                    in_=fg_dram[row2:row2 + FCH, :].rearrange("r (x y) -> x r y", x=64, y=64)), 16)
                if fc + 1 < NCH:
                    do_xn_D(fc + 1)
                sy.wait_ge(fv_ev.h, 64 * fc + 64)
                fs_o.inc(sy.dma_start(out=fo_dram[(fc % 2) * 64:(fc % 2) * 64 + 64, :],
                                      in_=vb(F_W1, 4096, 0, 64)), 16)
                sy.wait_ge(fs_o.h, 16 * (fc + 1))
                sy.wait_ge(fv_g.h, 3 * fc + 3)
                fs_or.inc(sy.dma_start(
                    out=vb(F_W2, 2048).rearrange("r (l m) -> r l m", l=64),
                    in_=fo_dram[(fc % 2) * 64:(fc % 2) * 64 + 64, :]
                    .rearrange("l (r m) -> r l m", r=FCH)), 16)
            sy.wait_ge(fv_x3.h, NCH)
            fs_x3.inc(sy.dma_start(out=x3_dram[(NCH - 1) * FCH:NCH * FCH, :],
                                   in_=vf(F_W3, 4096)), 16)

        @block.gpsimd
        def _(g):
            g.wait_ge(tm_b.h, 128)
            ms_xn.wait_total(g, 16)
            ms_o.wait_total(g, 128)
            fs_cs.inc(g.dma_start(out=vb(F_SLAB, 8192, 0, 64), in_=d_slab), 16)
            for fc in range(NCH):
                if fc == 0:
                    g.wait_ge(mv_oe.h, 128)
                ms_o.wait_total(g, 8 * (fc + 1))
                if fc >= 2:
                    g.wait_ge(fv_x3.h, fc - 1)
                fg_x.inc(g.dma_start(out=vf(F_X2[fc % 2], 4096),
                                     in_=x2_dram[fc * FCH:(fc + 1) * FCH, :]), fc)

        nc.leave_named_scope("ph5", sc5, False)
        if phases <= 5:
            @block.gpsimd
            def _(g):
                g.wait_ge(fs_x3.h, fs_x3.n)
            return nc
        sc6, _ = nc.enter_named_scope("ph6", False)
        moe_call(1, x3_dram, d_out, x3_dram)
        nc.leave_named_scope("ph6", sc6, False)

        @block.gpsimd
        def _(g):
            ms_o.wait_total(g, 256)
    return nc


# version 12
# speedup vs baseline: 1.0050x; 1.0050x over previous
import numpy as np
import ml_dtypes
from contextlib import ExitStack

import concourse.bass as bass
import concourse.mybir as mybir
from concourse.bass_utils import run_bass_kernel_spmd

F32 = mybir.dt.float32
BF16 = mybir.dt.bfloat16
AF = mybir.ActivationFunctionType
ALU = mybir.AluOpType
AX = mybir.AxisListType

D = 4096
SEQ = 4096
OWN = 2048
NH, KVL, QL, HD = 32, 4, 12, 128
NEXP, RANK = 64, 16
SCALING = 2.0
EPS = 1e-5
NEG = -1.0e30

QCG = 1024
QS = 512
MCH = 512
FCH = 128


def _consts():
    jj = np.arange(64)
    c64 = np.cos(2 * np.pi * np.outer(jj, jj) / 64).astype(np.float32)
    s64 = np.sin(2 * np.pi * np.outer(jj, jj) / 64).astype(np.float32)
    C = {}
    C["FA"] = np.concatenate([c64, -s64], axis=1)
    C["FHA"] = np.block([[c64, -s64], [s64, c64]])
    f = np.arange(D)
    ang = 2 * np.pi * np.outer(jj, f) / D
    Cb = np.cos(ang).astype(np.float32)
    Sb = np.sin(ang).astype(np.float32)
    slab = np.zeros((64, 16384), np.float32)
    for j1 in range(64):
        slab[:, j1 * 128:j1 * 128 + 64] = Cb[:, j1::64]
        slab[:, j1 * 128 + 64:(j1 + 1) * 128] = Sb[:, j1::64]
        slab[:, 8192 + j1 * 128:8192 + j1 * 128 + 64] = Sb[:, j1::64]
        slab[:, 8192 + j1 * 128 + 64:8192 + (j1 + 1) * 128] = -Cb[:, j1::64]
    C["SLAB"] = slab
    C["ONES128"] = np.ones((128, 1), np.float32)
    C["ONES1"] = np.ones((1, 128), np.float32)
    hm = np.zeros((128, 32), np.float32)
    for h in range(NH):
        hm[4 * h:4 * h + 4, h] = 1.0
    C["HEADMASK"] = hm
    ef = np.zeros((64, NEXP * RANK), np.float32)
    for e in range(NEXP):
        ef[e, RANK * e:RANK * (e + 1)] = 1.0
    C["EFULL"] = ef
    I64 = np.eye(64, dtype=np.float32)
    L1 = np.zeros((128, 128), np.float32)
    L1[0:64, 0:64] = I64
    L1[64:128, 0:64] = I64
    L2 = np.zeros((128, 128), np.float32)
    L2[0:64, 64:128] = -I64
    L2[64:128, 64:128] = I64
    C["LH1"] = L1 / D
    C["LH2"] = L2 / D
    SW = np.zeros((128, 128), np.float32)
    SW[0:64, 64:128] = I64
    SW[64:128, 0:64] = I64
    C["SW"] = SW
    return C


def _shard_inputs(inp):
    x = np.ascontiguousarray(np.asarray(inp["L0_attn_a_x"], np.float32))
    Wq = np.asarray(inp["L0_attn_a_Wq_d"], np.float32)
    bq = np.asarray(inp["L0_attn_a_bq_d"], np.float32)
    Wqk = np.asarray(inp["L0_attn_a_W_qk"], np.float32)
    bqk = np.asarray(inp["L0_attn_a_b_qk"], np.float32)
    Wkv = np.asarray(inp["L0_attn_a_Wkv_d"], np.float32)
    bkv = np.asarray(inp["L0_attn_a_bkv_d"], np.float32)
    Wvu = np.asarray(inp["L0_attn_a_Wv_u"], np.float32)
    bvu = np.asarray(inp["L0_attn_a_bv_u"], np.float32)
    Wo = np.ascontiguousarray(np.asarray(inp["L0_attn_a_Wo"], np.float32))
    bo = np.asarray(inp["L0_attn_a_bo"], np.float32)
    A = np.asarray(inp["L0_ffn_A"], np.float32)
    Bm = np.asarray(inp["L0_ffn_B"], np.float32)
    Wr = np.asarray(inp["L0_ffn_Wr"], np.float32)

    wpack = np.zeros((D, 36), np.float32)
    wpack[:, 0:12] = Wq
    wpack[:, 32:36] = Wkv
    b36 = np.zeros((36, 1), np.float32)
    b36[0:12, 0] = bq
    b36[32:36, 0] = bkv
    wv5 = np.zeros((5, D), np.float32)
    wv5[0:4] = Wvu
    wv5[4] = bvu
    AT = np.ascontiguousarray(A.reshape(NEXP * RANK, D).T)
    Barr = np.ascontiguousarray(Bm.transpose(0, 2, 1).reshape(NEXP * RANK, D))
    WrT = np.ascontiguousarray(Wr.T)

    C = _consts()
    shared = dict(wpack=wpack, b36=b36, wqk=Wqk, bqk=bqk.reshape(128, 1),
                  wv5=wv5, wo=Wo,
                  bo=np.ascontiguousarray(bo.reshape(32, 128).T),
                  at=AT, barr=Barr, wrt=WrT, **{k.lower(): v for k, v in C.items()})
    maps = []
    for c in range(8):
        b, half = c // 2, c % 2
        m = dict(shared)
        m["xq"] = np.ascontiguousarray(x[b, half * OWN:(half + 1) * OWN])
        m["xf"] = np.ascontiguousarray(x[b, (1 - half) * OWN:(2 - half) * OWN])
        maps.append(m)
    return maps


class Sem:
    def __init__(self, ctx, nc, name):
        self.h = ctx.enter_context(nc.semaphore(name))
        self.n = 0

    def inc(self, instr, k=1):
        instr.then_inc(self.h, k)
        self.n += k
        return self.n


class SemPair:
    def __init__(self, ctx, nc, name):
        self.s = [Sem(ctx, nc, name + "a"), Sem(ctx, nc, name + "b")]

    def inc(self, instr, i, k=16):
        return self.s[i % 2].inc(instr, k)

    def wait(self, eng, i, cnt_of_parity):
        eng.wait_ge(self.s[i % 2].h, 16 * cnt_of_parity)

    def wait_done(self, eng, i):
        eng.wait_ge(self.s[i % 2].h, 16 * (i // 2 + 1))

    def wait_total(self, eng, total):
        eng.wait_ge(self.s[0].h, 16 * ((total + 1) // 2))
        eng.wait_ge(self.s[1].h, 16 * (total // 2))


class SemQuad:
    def __init__(self, ctx, nc, name):
        self.s = [Sem(ctx, nc, name + chr(97 + i)) for i in range(4)]

    def inc(self, instr, i, k=16):
        return self.s[i % 4].inc(instr, k)

    def wait_done(self, eng, i):
        eng.wait_ge(self.s[i % 4].h, 16 * (i // 4 + 1))


def kernel(**inputs):
    maps = _shard_inputs(inputs)
    nc = build_program()
    res = run_bass_kernel_spmd(nc, maps, core_ids=list(range(8)))
    out = np.empty((4, SEQ, D), np.float32)
    for c in range(8):
        b, half = c // 2, c % 2
        out[b, half * OWN:(half + 1) * OWN] = res.results[c]["out"]
    return out


def build_program(phases=6, dbg=False, ph3sub=4):
    nc = bass.Bass("TRN2", target_bir_lowering=False, debug=False)

    d_xq = nc.dram_tensor("xq", [OWN, D], F32, kind="ExternalInput").ap()
    d_xf = nc.dram_tensor("xf", [OWN, D], F32, kind="ExternalInput").ap()
    d_wpack = nc.dram_tensor("wpack", [D, 36], F32, kind="ExternalInput").ap()
    d_b36 = nc.dram_tensor("b36", [36, 1], F32, kind="ExternalInput").ap()
    d_wqk = nc.dram_tensor("wqk", [QL, 128], F32, kind="ExternalInput").ap()
    d_bqk = nc.dram_tensor("bqk", [128, 1], F32, kind="ExternalInput").ap()
    d_wv5 = nc.dram_tensor("wv5", [5, D], F32, kind="ExternalInput").ap()
    d_wo = nc.dram_tensor("wo", [D, D], F32, kind="ExternalInput").ap()
    d_bo = nc.dram_tensor("bo", [128, 32], F32, kind="ExternalInput").ap()
    d_at = nc.dram_tensor("at", [D, NEXP * RANK], F32, kind="ExternalInput").ap()
    d_barr = nc.dram_tensor("barr", [NEXP * RANK, D], F32, kind="ExternalInput").ap()
    d_wrt = nc.dram_tensor("wrt", [D, NEXP], F32, kind="ExternalInput").ap()
    d_fa = nc.dram_tensor("fa", [64, 128], F32, kind="ExternalInput").ap()
    d_fha = nc.dram_tensor("fha", [128, 128], F32, kind="ExternalInput").ap()
    d_slab = nc.dram_tensor("slab", [64, 16384], F32, kind="ExternalInput").ap()
    d_ones128 = nc.dram_tensor("ones128", [128, 1], F32, kind="ExternalInput").ap()
    d_ones1 = nc.dram_tensor("ones1", [1, 128], F32, kind="ExternalInput").ap()
    d_headmask = nc.dram_tensor("headmask", [128, 32], F32, kind="ExternalInput").ap()
    d_efull = nc.dram_tensor("efull", [64, NEXP * RANK], F32, kind="ExternalInput").ap()
    d_lh1 = nc.dram_tensor("lh1", [128, 128], F32, kind="ExternalInput").ap()
    d_lh2 = nc.dram_tensor("lh2", [128, 128], F32, kind="ExternalInput").ap()
    d_sw = nc.dram_tensor("sw", [128, 128], F32, kind="ExternalInput").ap()
    d_out = nc.dram_tensor("out", [OWN, D], F32, kind="ExternalOutput").ap()

    dk0 = dict(kind="ExternalOutput") if dbg else {}
    xn_dram = nc.dram_tensor("xn_s", [SEQ, D], BF16, **dk0).ap()
    qt_dram = nc.dram_tensor("qt_s", [128, OWN], BF16, **dk0).ap()
    nb_dram = nc.dram_tensor("nb_s", [32, OWN], BF16, **dk0).ap()
    ckv_dram = nc.dram_tensor("ckv_s", [5, SEQ], BF16, **dk0).ap()
    mx_dram = nc.dram_tensor("mx_s", [4, 1], F32).ap()
    mx2_dram = nc.dram_tensor("mx2_s", [1, 1], F32).ap()
    aot_dram = nc.dram_tensor("aot_s", [D, OWN], BF16, **({"kind": "ExternalOutput"} if dbg else {})).ap()
    dk = dict(kind="ExternalOutput") if dbg else {}
    x1_dram = nc.dram_tensor("x1_s", [OWN, D], F32, **dk).ap()
    x2_dram = nc.dram_tensor("x2_s", [OWN, D], F32, **dk).ap()
    x3_dram = nc.dram_tensor("x3_s", [OWN, D], F32, **dk).ap()
    xnm_dram = nc.dram_tensor("xnm_s", [OWN, D], BF16).ap()
    fb_dram = nc.dram_tensor("fb_s", [2 * FCH, D], BF16).ap()
    fc_dram = nc.dram_tensor("fc_s", [128, 8192], BF16).ap()
    fg_dram = nc.dram_tensor("fg_s", [2 * FCH, D], BF16).ap()
    barr_bf = nc.dram_tensor("barr_bf", [NEXP * RANK, D], BF16).ap()
    fo_dram = nc.dram_tensor("fo_s", [128, 8192], BF16).ap()

    with ExitStack() as ctx:
        sb = lambda n, s, d: ctx.enter_context(nc.sbuf_tensor(n, s, d))
        mksem = lambda n: Sem(ctx, nc, n)

        s_wpack = sb("c_wpack", [128, 32 * 36], BF16)
        s_b36 = sb("c_b36", [36, 1], F32)
        s_wqk_raw = sb("c_wqkr", [QL, 128], F32)
        s_wqk05 = sb("c_wqk05", [QL, 128], BF16)
        s_bqk = sb("c_bqk", [128, 1], F32)
        s_bqk05 = sb("c_bqk05", [128, 1], F32)
        s_wv5 = sb("c_wv5", [5, D], BF16)
        s_wrt = sb("c_wrt", [128, 32 * NEXP], BF16)
        s_bo = sb("c_bo", [128, 32], F32)
        s_fa = sb("c_fa", [64, 128], BF16)
        s_fha = sb("c_fha", [128, 128], BF16)
        s_ones128 = sb("c_ones128", [128, 1], BF16)
        s_ones1 = sb("c_ones1", [1, 128], BF16)
        s_headmask = sb("c_hm", [128, 32], BF16)
        s_efull = sb("c_efull", [64, NEXP * RANK], BF16)
        s_lh1 = sb("c_lh1", [128, 128], BF16)
        s_lh2 = sb("c_lh2", [128, 128], BF16)
        s_sw = sb("c_sw", [128, 128], BF16)
        s_eps = sb("c_eps", [128, 1], F32)

        AW = 47872
        arena = sb("arena", [128, AW], F32)
        PS = ctx.enter_context(nc.psum_tensor("PS", [128, 4096], F32))

        def vf(off, n, p0=0, p1=128):
            return arena.ap()[p0:p1, off:off + n]

        def vb(off, n, p0=0, p1=128):
            return arena.ap()[p0:p1, off:off + n].bitcast(BF16)

        def bank(i, w=512):
            return PS[:, i * 512:i * 512 + w]

        def vsync(v, instr):
            vch.inc(instr)
            v.wait_ge(vch.h, vch.n)

        def ssync(s, instr):
            sch.inc(instr)
            s.wait_ge(sch.h, sch.n)

        def gsync(g, instr):
            gch.inc(instr)
            g.wait_ge(gch.h, gch.n)

        c_ld = mksem("c_ld")
        q_ld = SemPair(ctx, nc, "q_ld")
        vst = mksem("vst"); sst = mksem("sst"); vst2 = mksem("vst2")
        vch = mksem("vch"); sch = mksem("sch"); gch = mksem("gch")
        sst2 = mksem("sst2"); nrm = mksem("nrm"); stq = SemPair(ctx, nc, "stq")
        txn = SemPair(ctx, nc, "txn"); tpj = mksem("tpj"); spj = mksem("spj")
        tqt = mksem("tqt"); sqt = mksem("sqt"); sab = mksem("sab")
        tbs = mksem("tbs"); vmc = mksem("vmc"); dmc = mksem("dmc")
        vnb = mksem("vnb"); dq5 = mksem("dq5"); dck = mksem("dck")
        t_v = mksem("t_v"); v_ve = mksem("v_ve")
        t_sc = mksem("t_sc"); s_ex = mksem("s_ex"); t_av = mksem("t_av")
        v_zt = mksem("v_zt"); t_zm = mksem("t_zm"); v_rz = mksem("v_rz")
        t_zb = mksem("t_zb"); v_at = mksem("v_at")
        g_wo = SemPair(ctx, nc, "g_wo"); t_wo = mksem("t_wo"); s_wo = mksem("s_wo")
        st_wo = SemPair(ctx, nc, "st_wo")
        mg_x = SemPair(ctx, nc, "mg_x"); mg_aA = mksem("mg_aA"); mg_aB = mksem("mg_aB"); mg_r = SemPair(ctx, nc, "mg_r")
        ms_t = SemPair(ctx, nc, "ms_t"); ms_xn = SemPair(ctx, nc, "ms_xn"); ms_T1 = SemQuad(ctx, nc, "msT1")
        ms_T2 = SemQuad(ctx, nc, "msT2"); ms_o = SemPair(ctx, nc, "ms_o"); ms_x1 = SemPair(ctx, nc, "ms_x1")
        mv_1 = mksem("mv_1"); mv_g = mksem("mv_g"); mv_gx = mksem("mv_gx")
        mv_hg = mksem("mv_hg"); mv_oe = mksem("mv_oe")
        mc_s = mksem("mc_s"); mc_e = mksem("mc_e")
        tm_r = mksem("tm_r"); tm_g = mksem("tm_g"); tm_h = mksem("tm_h")
        tm_b = mksem("tm_b")
        fg_x = SemPair(ctx, nc, "fg_x"); fs_xn = mksem("fs_xn"); fs_D = mksem("fs_D")
        ft_A = mksem("ft_A"); ft_B = mksem("ft_B"); fs_cs = mksem("fs_cs")
        fv_ev = mksem("fv_ev"); fs_c = mksem("fs_c"); fs_cr = mksem("fs_cr")
        fv_g = mksem("fv_g"); fc_s = mksem("fc_s"); fs_g = mksem("fs_g")
        fs_gD = mksem("fs_gD"); fs_o = mksem("fs_o")
        fs_or = mksem("fs_or"); fv_x3 = mksem("fv_x3"); fs_x3 = mksem("fs_x3")
        fv_ln = mksem("fv_ln"); fc_ln = mksem("fc_ln")
        ft_P = mksem("ft_P"); fv_pe = mksem("fv_pe")
        ft_G = mksem("ft_G"); fv_gw = mksem("fv_gw")
        ft_H = mksem("ft_H"); fv_he = mksem("fv_he")

        block = ctx.enter_context(nc.Block())

        sc0, _ = nc.enter_named_scope("ph0", False)

        @block.gpsimd
        def _(g):
            g.dma_start(out=s_wpack.ap().rearrange("p (c w) -> p c w", w=36),
                        in_=d_wpack.rearrange("(c p) w -> p c w", p=128)).then_inc(c_ld.h, 16)
            g.dma_start(out=s_b36[:], in_=d_b36).then_inc(c_ld.h, 16)
            g.dma_start(out=s_wqk_raw[:], in_=d_wqk).then_inc(c_ld.h, 16)
            g.dma_start(out=s_bqk[:], in_=d_bqk).then_inc(c_ld.h, 16)
            g.dma_start(out=s_wv5[:], in_=d_wv5).then_inc(c_ld.h, 16)
            g.dma_start(out=s_wrt.ap().rearrange("p (c e) -> p c e", e=NEXP),
                        in_=d_wrt.rearrange("(c p) e -> p c e", p=128)).then_inc(c_ld.h, 16)
            g.dma_start(out=s_bo.ap().unsqueeze(2),
                        in_=d_bo.rearrange("(c p) o -> p c o", p=128)).then_inc(c_ld.h, 16)
            for s_t, d_t in [(s_fa, d_fa), (s_fha, d_fha),
                             (s_ones128, d_ones128), (s_ones1, d_ones1),
                             (s_headmask, d_headmask), (s_efull, d_efull),
                             (s_lh1, d_lh1), (s_lh2, d_lh2), (s_sw, d_sw)]:
                g.dma_start(out=s_t[:], in_=d_t).then_inc(c_ld.h, 16)
        c_ld.n = 16 * 16

        @block.vector
        def _(v):
            v.wait_ge(c_ld.h, c_ld.n)
            v.memset(s_eps[:], EPS)
            v.tensor_scalar(s_wqk05[:], s_wqk_raw[:], 0.5, None, op0=ALU.mult)
            vst.inc(v.tensor_scalar(s_bqk05[:], s_bqk[:], 0.5, None, op0=ALU.mult))

        nc.leave_named_scope("ph0", sc0, False)
        sc1, _ = nc.enter_named_scope("ph1", False)
        a_x = [0, 4096]
        a_sqs = 8192
        a_xn = [12288, 14336]
        a_sm = 16384

        def ph1_smalls(bi, k):
            return vf(a_sm + bi * 16 + k, 1)

        @block.gpsimd
        def _(g):
            for i in range(32):
                if i >= 2:
                    g.wait_ge(nrm.h, i - 1)
                    g.wait_ge(sst.h, i - 1)
                src = d_xq if i < 16 else d_xf
                row = (i % 16) * 128
                q_ld.inc(g.dma_start(out=vf(a_x[i % 2], 4096),
                                     in_=src[row:row + 128, :]), i)

        @block.vector
        def _(v):
            for i in range(32):
                bi = i % 2
                q_ld.wait_done(v, i)
                vsync(v, v.reduce_sum(ph1_smalls(bi, 0), vf(a_x[bi], 4096), axis=AX.X))
                vst.inc(v.tensor_scalar(ph1_smalls(bi, 1), ph1_smalls(bi, 0),
                                        -1.0 / D, None, op0=ALU.mult))
                v.wait_ge(vch.h, vch.n)
                vsync(v, v.engine_nop())
                v.wait_ge(sst.h, i + 1)
                vsync(v, v.tensor_tensor(ph1_smalls(bi, 2), ph1_smalls(bi, 1),
                                         ph1_smalls(bi, 1), op=ALU.mult))
                vsync(v, v.tensor_scalar(ph1_smalls(bi, 3), ph1_smalls(bi, 4), 1.0 / D,
                                         None, op0=ALU.mult))
                vst2.inc(v.tensor_tensor(ph1_smalls(bi, 5), ph1_smalls(bi, 3),
                                         ph1_smalls(bi, 2), op=ALU.subtract))
                v.wait_ge(sst2.h, i + 1)
                v.tensor_tensor(ph1_smalls(bi, 6), ph1_smalls(bi, 1),
                                ph1_smalls(bi, 7), op=ALU.mult)
                vsync(v, v.engine_nop())
                if i >= 2:
                    stq.wait(v, i, i // 2)
                nrm.inc(v.tensor_scalar(vb(a_xn[bi], 2048), vf(a_x[bi], 4096),
                                        ph1_smalls(bi, 7), ph1_smalls(bi, 6),
                                        op0=ALU.mult, op1=ALU.add))

        @block.scalar
        def _(s):
            for i in range(32):
                bi = i % 2
                q_ld.wait_done(s, i)
                ssync(s, s.memzero(ph1_smalls(bi, 4)))
                sst.inc(s.activation(vf(a_sqs, 4096), vf(a_x[bi], 4096), AF.Square,
                                     accum_out=ph1_smalls(bi, 4)))
                s.wait_ge(vst2.h, i + 1)
                ssync(s, s.activation(ph1_smalls(bi, 8), ph1_smalls(bi, 5), AF.Ln,
                                      bias=s_eps[:]))
                sst2.inc(s.activation(ph1_smalls(bi, 7), ph1_smalls(bi, 8), AF.Exp,
                                      scale=-0.5))

        @block.sync
        def _(sy):
            for i in range(32):
                bi = i % 2
                sy.wait_ge(nrm.h, i + 1)
                row = (i % 16) * 128 + (0 if i < 16 else OWN)
                stq.inc(sy.dma_start(out=xn_dram[row:row + 128, :],
                                     in_=vb(a_xn[bi], 2048)), i)

        nc.leave_named_scope("ph1", sc1, False)
        sc2, _ = nc.enter_named_scope("ph2", False)
        A_Q5, A_CKV4 = 32768, 44032
        HG = [(0, 0, 11), (1, 11, 11), (2, 22, 10)]

        @block.sync
        def _(sy):
            stq.wait_total(sy, 32)
            for ci in range(32):
                if ci >= 2:
                    sy.wait_ge(tpj.h, ci - 1)
                txn.inc(sy.dma_start(out=vb(2048 * (ci % 2), 2048),
                                     in_=xn_dram[:, ci * 128:(ci + 1) * 128],
                                     transpose=True), ci)

        @block.tensor
        def _(t):
            for ci in range(32):
                txn.wait_done(t, ci)
                if ci == 0:
                    t.wait_ge(c_ld.h, c_ld.n)
                for nb in range(8):
                    ins = t.matmul(PS[0:36, nb * 512:(nb + 1) * 512],
                                   s_wpack[:, ci * 36:(ci + 1) * 36],
                                   vb(2048 * (ci % 2), 2048)[:, nb * 512:(nb + 1) * 512],
                                   start=(ci == 0), stop=(ci == 31))
                tpj.inc(ins)

        @block.scalar
        def _(s):
            s.wait_ge(tpj.h, 32)
            s.wait_ge(vst.h, 34)
            for nb in range(8):
                spj.inc(s.activation(vb(4096, 2048, 0, 12)[:, nb * 512:(nb + 1) * 512],
                                     PS[0:12, nb * 512:(nb + 1) * 512], AF.Identity,
                                     bias=s_b36[0:12, :]))
                spj.inc(s.activation(vb(44032, 2048, 0, 4)[:, nb * 512:(nb + 1) * 512],
                                     PS[32:36, nb * 512:(nb + 1) * 512], AF.Identity,
                                     bias=s_b36[32:36, :]))

        @block.vector
        def _(v):
            v.wait_ge(c_ld.h, c_ld.n)
            vst.inc(v.memset(vb(44032, 2048, 0, 8), 1.0))

        @block.tensor
        def _(t):
            t.wait_ge(spj.h, 16)
            t.wait_ge(vst.h, 1)
            for nb in range(4):
                tqt.inc(t.matmul(bank(nb), s_wqk05[:], 
                                 vb(4096, 2048, 0, 12)[:, nb * 512:(nb + 1) * 512],
                                 start=True, stop=True))

        @block.scalar
        def _(s):
            for nb in range(4):
                s.wait_ge(tqt.h, nb + 1)
                sqt.inc(s.activation(vb(6144, 1024)[:, nb * 512:(nb + 1) * 512],
                                     bank(nb), AF.Identity, bias=s_bqk05[:]))
            s.wait_ge(sqt.h, 4)
            sab.inc(s.activation(vb(7168, 1024), vb(6144, 1024), AF.Abs))

        @block.tensor
        def _(t):
            t.wait_ge(sab.h, 1)
            for nb in range(4):
                tbs.inc(t.matmul(PS[0:32, 2048 + nb * 512: 2048 + (nb + 1) * 512],
                                 s_headmask[:], vb(7168, 1024)[:, nb * 512:(nb + 1) * 512],
                                 start=True, stop=True))

        @block.vector
        def _(v):
            v.wait_ge(spj.h, 16)
            vmc.inc(v.reduce_max(vf(9216, 1, 0, 4), vb(44032, 2048, 0, 4), axis=AX.X,
                                 apply_absolute_value=True))

        @block.sync
        def _(sy):
            sy.wait_ge(vmc.h, 1)
            dmc.inc(sy.dma_start(out=mx_dram, in_=vf(9216, 1, 0, 4)), 16)
            sy.wait_ge(dmc.h, 16)
            dmc.inc(sy.dma_start(out=vf(9220, 4, 0, 1), in_=mx_dram.rearrange("a b -> b a")), 16)

        @block.vector
        def _(v):
            v.wait_ge(dmc.h, 32)
            vsync(v, v.reduce_max(vf(9232, 1, 0, 1), vf(9220, 4, 0, 1), axis=AX.X))
            vmc.inc(v.tensor_scalar(vf(9236, 1, 0, 1), vf(9232, 1, 0, 1), -1.0, None,
                                    op0=ALU.mult))

        @block.sync
        def _(sy):
            sy.wait_ge(vmc.h, 2)
            dmc.inc(sy.dma_start(out=mx2_dram, in_=vf(9236, 1, 0, 1)), 16)
            sy.wait_ge(dmc.h, 48)
            dmc.inc(sy.dma_start(out=vf(9240, 1, 0, 32),
                                 in_=mx2_dram.to_broadcast([32, 1])), 16)

        @block.vector
        def _(v):
            v.wait_ge(dmc.h, 64)
            v.wait_ge(tbs.h, 4)
            for nb in range(4):
                vnb.inc(v.tensor_scalar(vb(8192, 1024, 0, 32)[:, nb * 512:(nb + 1) * 512],
                                        PS[0:32, 2048 + nb * 512:2048 + (nb + 1) * 512],
                                        vf(9240, 1, 0, 32), None, op0=ALU.mult))

        @block.sync
        def _(sy):
            sy.wait_ge(sqt.h, 4)
            dq5.inc(sy.dma_start(out=qt_dram, in_=vb(6144, 1024)), 16)
            sy.wait_ge(vnb.h, 4)
            dq5.inc(sy.dma_start(out=nb_dram, in_=vb(8192, 1024, 0, 32)), 16)
            sy.wait_ge(spj.h, 16)
            sy.wait_ge(vst.h, 34)
            dq5.inc(sy.dma_start(out=ckv_dram, in_=vb(44032, 2048, 0, 5)), 16)
            sy.wait_ge(dq5.h, 48)
            for grp, h0, nh in HG:
                p0 = 32 * grp
                dq5.inc(sy.dma_start(
                    out=vb(A_Q5, nh * 1024, p0, p0 + 4).rearrange("d (h t) -> d h t", h=nh),
                    in_=qt_dram.rearrange("(h d) t -> d h t", d=4)[:, h0:h0 + nh, :]), 16)
                dq5.inc(sy.dma_start(
                    out=vb(A_Q5, nh * 1024, p0 + 4, p0 + 5).rearrange("a (h t) -> a h t", h=nh),
                    in_=nb_dram.unsqueeze(0)[:, h0:h0 + nh, :]), 16)
                dck.inc(sy.dma_start(out=vb(A_CKV4, 2048, p0, p0 + 5), in_=ckv_dram), 16)

        def _ph3():
            A_AT, A_PB, A_ZT, A_VH = 0, 16384, 24576, 28672
            A_Z128, A_RZH, A_RZ, A_RZBF, A_WOUT = 46080, 46336, 46592, 47104, 20480
            A_WOC = 16384

            def q5slice(h, qsi_g, w=512):
                g = min(h // 11, 2)
                p0 = 32 * g
                nh = 11 if g < 2 else 10
                base = (h - 11 * g) * 2048 + qsi_g * 512
                return vb(A_Q5, nh * 1024, p0, p0 + 5)[:, base:base + w]

            def ckv4s(kb, p0):
                return vb(A_CKV4, 2048, p0, p0 + 5)[:, kb * 128:(kb + 1) * 128]

            def z_mm(t, ITz):
                t.wait_ge(v_zt.h, 2 * (ITz + 1))
                if ITz >= 1:
                    t.wait_ge(v_at.h, 2 * ITz)
                t.matmul(PS[0:1, 3072:3584], s_ones128[:],
                         vb(A_ZT, 256), start=True, stop=False)
                t_zm.inc(t.matmul(PS[0:1, 3072:3584], s_ones128[:],
                                  vb(A_ZT + 2048, 256), start=False, stop=True))

            def z_bc(t, ITz):
                t.wait_ge(v_rz.h, ITz + 1)
                t_zb.inc(t.matmul(PS[:, 3072:3584], s_ones1[:],
                                  vb(A_RZH, 256, 0, 1), start=True, stop=True))

            def attnv_half(t, H):
                ITp, hb = H // 2, H % 2
                HIp = ITp // 2
                t.wait_ge(s_ex.h, 8 * H + 8)
                t.wait_ge(v_ve.h, 8 * HIp + 4 + 4 * hb)
                if hb == 0 and ITp >= 1:
                    t.wait_ge(v_at.h, 2 * ITp)
                bc = 2048 + hb * 512
                for i in range(16):
                    kb = 16 * hb + i
                    ins = t.matmul(
                        PS[:, bc:bc + 512],
                        vb(A_VH + (HIp % 2) * 2048, 2048)[:, kb * 128:(kb + 1) * 128],
                        vb(A_PB, 8192)[:, kb * 512:(kb + 1) * 512],
                        start=(i == 0), stop=(i == 15))
                t_av.inc(ins, 1)

            @block.tensor
            def _(t):
                for qcg in range(2):
                    for h in range(32):
                        HI = qcg * 32 + h
                        for g8 in range(8):
                            if HI == 0 and g8 == 0:
                                t.wait_ge(vnb.h, 4)
                                t.wait_ge(dck.h, dck.n)
                                t.wait_ge(dq5.h, dq5.n)
                            if 8 * HI + g8 >= 1:
                                t.wait_ge(v_ve.h, 8 * HI + g8)
                            for j in range(4):
                                kb = g8 * 4 + j
                                ins = t.matmul(PS[:, 3584 + j * 128:3584 + (j + 1) * 128],
                                               ckv4s(kb, 0), s_wv5[:, h * 128:(h + 1) * 128],
                                               start=True, stop=True)
                            t_v.inc(ins, 4)
                        for qsi in range(2):
                            if ph3sub < 2:
                                continue
                            IT = qcg * 64 + h * 2 + qsi
                            QSI = qcg * 2 + qsi
                            for p in range(16):
                                PI = 16 * IT + p
                                if p == 0 and IT == 64 and ph3sub >= 4:
                                    t.wait_ge(s_wo.h, 64)
                                if PI >= 2:
                                    t.wait_ge(s_ex.h, PI - 1)
                                cb = (PI % 2) * 1024
                                t.matmul(PS[:, cb:cb + 512], ckv4s(2 * p, 32 * min(h // 11, 2)),
                                         q5slice(h, QSI), start=True, stop=True)
                                ins = t.matmul(PS[:, cb + 512:cb + 1024],
                                               ckv4s(2 * p + 1, 32 * min(h // 11, 2)),
                                               q5slice(h, QSI), start=True, stop=True)
                                t_sc.inc(ins, 2)
                                if p == 7 and ph3sub >= 3 and 2 * IT - 1 >= 2 * qcg * 64:
                                    z_mm(t, IT - 1)
                                    attnv_half(t, 2 * IT - 1)
                                if p == 10 and ph3sub >= 3 and IT - 1 >= qcg * 64:
                                    z_bc(t, IT - 1)
                            if ph3sub < 3:
                                continue
                            attnv_half(t, 2 * IT)
                    if ph3sub >= 3:
                        z_mm(t, qcg * 64 + 63)
                        attnv_half(t, 2 * (qcg * 64 + 63) + 1)
                        z_bc(t, qcg * 64 + 63)
                    if ph3sub < 4:
                        continue
                    t.wait_ge(v_at.h, 2 * 64 * (qcg + 1))
                    for dmt in range(32):
                        g_wo.wait_done(t, qcg * 32 + dmt)
                        for qc4 in range(2):
                            k = qcg * 64 + dmt * 2 + qc4
                            if k >= 2:
                                t.wait_ge(s_wo.h, k - 1)
                            for hd in range(32):
                                ins = t.matmul(
                                    PS[:, (k % 2) * 512:(k % 2) * 512 + 512],
                                    vb(A_WOC + (dmt % 2) * 2048, 2048)[:, hd * 128:(hd + 1) * 128],
                                    vb(A_AT, 16384)[:, hd * 1024 + qc4 * 512: hd * 1024 + qc4 * 512 + 512],
                                    start=(hd == 0), stop=(hd == 31))
                            t_wo.inc(ins, 1)

            def z_inv(s, ITz):
                s.wait_ge(t_zm.h, ITz + 1)
                ssync(s, s.activation(vf(A_RZBF, 512, 0, 1), PS[0:1, 3072:3584], AF.Ln))
                v_rz.inc(s.activation(vb(A_RZH, 256, 0, 1), vf(A_RZBF, 512, 0, 1),
                                      AF.Exp, scale=-1.0))

            @block.scalar
            def _(s):
                for qcg in range(2):
                    for h in range(32):
                        for qsi in range(2):
                            IT = qcg * 64 + h * 2 + qsi
                            if ph3sub < 2:
                                continue
                            for p in range(16):
                                PI = 16 * IT + p
                                s.wait_ge(t_sc.h, 2 * (PI + 1))
                                if IT >= 1 and ph3sub >= 3:
                                    s.wait_ge(t_av.h, 2 * (IT - 1) + p // 8 + 1)
                                if p % 8 == 0 and IT >= 1 and ph3sub >= 3:
                                    s.wait_ge(v_zt.h, 2 * (IT - 1) + p // 8 + 1)
                                if p == 0 and IT == 64 and ph3sub >= 4:
                                    s.wait_ge(t_wo.h, 64)
                                    s.wait_ge(s_wo.h, 64)
                                    st_wo.wait_total(s, 64)
                                cb = (PI % 2) * 1024
                                s_ex.inc(s.activation(
                                    vb(A_PB, 8192)[:, p * 1024:(p + 1) * 1024],
                                    PS[:, cb:cb + 1024], AF.Exp))
                                if p == 7 and IT - 1 >= qcg * 64 and ph3sub >= 3:
                                    z_inv(s, IT - 1)
                    if ph3sub >= 3:
                        z_inv(s, qcg * 64 + 63)
                    for dmt in range(32):
                        if ph3sub < 4:
                            continue
                        for qc4 in range(2):
                            k = qcg * 64 + dmt * 2 + qc4
                            s.wait_ge(t_wo.h, k + 1)
                            if k >= 2:
                                st_wo.wait(s, k, k // 2)
                            s_wo.inc(s.activation(
                                vb(A_WOUT + (k % 2) * 256, 256), PS[:, (k % 2) * 512:(k % 2) * 512 + 512],
                                AF.Identity, bias=s_bo[:, dmt:dmt + 1]))

            def att_evac(v, ITE):
                hE, qsE = (ITE % 64) // 2, ITE % 2
                v.wait_ge(t_zb.h, ITE + 1)
                v_at.inc(v.engine_nop())
                v.wait_ge(t_av.h, 2 * (ITE + 1))
                if ITE == 64 and ph3sub >= 4:
                    v.wait_ge(t_wo.h, 64)
                vsync(v, v.tensor_copy(vb(A_RZ, 256), PS[:, 2048:2560]))
                vsync(v, v.tensor_tensor(vb(A_RZ + 256, 256), PS[:, 2560:3072],
                                         vb(A_RZ, 256), op=ALU.add))
                v_at.inc(v.tensor_tensor(
                    vb(A_AT, 16384)[:, hE * 1024 + qsE * 512: hE * 1024 + qsE * 512 + 512],
                    PS[:, 3072:3584], vb(A_RZ + 256, 256), op=ALU.mult))

            @block.vector
            def _(v):
                for qcg in range(2):
                    for h in range(32):
                        HI = qcg * 32 + h
                        for g8 in range(8):
                            v.wait_ge(t_v.h, 32 * HI + 4 * (g8 + 1))
                            if g8 == 0 and HI >= 2 and ph3sub >= 3:
                                v.wait_ge(t_av.h, 2 * (2 * HI - 2))
                            v_ve.inc(v.tensor_copy(
                                vb(A_VH + (HI % 2) * 2048, 2048)[:, g8 * 512:(g8 + 1) * 512],
                                PS[:, 3584:4096]))
                        for qsi in range(2):
                            if ph3sub < 3:
                                continue
                            IT = qcg * 64 + h * 2 + qsi
                            zt = vb(A_ZT, 4096)
                            pb = vb(A_PB, 8192)
                            v.wait_ge(s_ex.h, 16 * IT + 8)
                            v.wait_ge(t_zm.h, IT)
                            vsync(v, v.tensor_tensor(zt[:, 0:4096], pb[:, 0:4096],
                                                     pb[:, 4096:8192], op=ALU.add))
                            vsync(v, v.tensor_tensor(zt[:, 0:2048], zt[:, 0:2048],
                                                     zt[:, 2048:4096], op=ALU.add))
                            vsync(v, v.tensor_tensor(zt[:, 0:1024], zt[:, 0:1024],
                                                     zt[:, 1024:2048], op=ALU.add))
                            v_zt.inc(v.tensor_tensor(zt[:, 0:512], zt[:, 0:512],
                                                     zt[:, 512:1024], op=ALU.add))
                            if IT - 1 >= qcg * 64:
                                att_evac(v, IT - 1)
                                v.wait_ge(v_at.h, v_at.n)
                            v.wait_ge(s_ex.h, 16 * (IT + 1))
                            vsync(v, v.tensor_tensor(zt[:, 4096:8192], pb[:, 8192:12288],
                                                     pb[:, 12288:16384], op=ALU.add))
                            vsync(v, v.tensor_tensor(zt[:, 4096:6144], zt[:, 4096:6144],
                                                     zt[:, 6144:8192], op=ALU.add))
                            vsync(v, v.tensor_tensor(zt[:, 4096:5120], zt[:, 4096:5120],
                                                     zt[:, 5120:6144], op=ALU.add))
                            v_zt.inc(v.tensor_tensor(zt[:, 4096:4608], zt[:, 4096:4608],
                                                     zt[:, 4608:5120], op=ALU.add))
                    if ph3sub >= 3:
                        att_evac(v, qcg * 64 + 63)


            @block.sync
            def _(sy):
                for qcg in range(2):
                    for dmt in range(32):
                        if ph3sub < 4:
                            continue
                        for qc4 in range(2):
                            k = qcg * 64 + dmt * 2 + qc4
                            sy.wait_ge(s_wo.h, k + 1)
                            st_wo.inc(sy.dma_start(
                                out=aot_dram[dmt * 128:(dmt + 1) * 128,
                                             qcg * 1024 + qc4 * 512: qcg * 1024 + qc4 * 512 + 512],
                                in_=vb(A_WOUT + (k % 2) * 256, 256)), k)

            @block.gpsimd
            def _(g):
                for qcg in range(2):
                    for dmt in range(32):
                        if ph3sub < 4:
                            continue
                        if dmt < 2:
                            g.wait_ge(t_av.h, 2 * 64 * (qcg + 1))
                            g.wait_ge(v_zt.h, 128 * (qcg + 1))
                        else:
                            g.wait_ge(t_wo.h, qcg * 64 + 2 * dmt - 2)
                        g_wo.inc(g.dma_start(
                            out=vb(A_WOC + (dmt % 2) * 2048, 2048).rearrange("p (hd c) -> p hd c", c=128),
                            in_=d_wo.rearrange("(hd p) c -> p hd c", p=128)[:, :, dmt * 128:(dmt + 1) * 128]),
                            qcg * 32 + dmt)

        nc.leave_named_scope("ph2", sc2, False)

        @block.gpsimd
        def _(g):
            g.dma_start(out=barr_bf, in_=d_barr).then_inc(c_ld.h, 16)
        c_ld.n += 16

        sc3, _ = nc.enter_named_scope("ph3", False)
        if phases >= 3:
            _ph3()
        nc.leave_named_scope("ph3", sc3, False)

        M_AB, M_HGT, M_GX = 0, 16384, 24576
        M_XT1, M_XT2 = 26624, 27648
        M_MX = [28672, 32768]
        M_SQS = 36864
        M_MAT = [40960, 43008]
        M_SM = 45056
        G_SCR, G_SCR2, G_M8 = M_SM, M_SM + 64, M_SM + 128
        G_NEGM, G_ZD, G_Z16, G_RZ, G_RZ2 = M_SM + 144, M_SM + 145, M_SM + 146, M_SM + 147, M_SM + 148
        G_MASK = M_SM + 152
        G_ME, G_E16O = M_SM + 184, M_SM + 248
        G_G, G_GT32, G_GTB = M_SM + 264, M_SM + 328, M_SM + 456
        GT4 = 46336
        M_MO = [45824, 46336]

        def msm(bi, j):
            return vf(46848 + bi * 16 + j, 1)

        def moe_call(call, d_src, d_dst, d_res):
            KB = call * 16
            RB = call * 128
            HB = call * 128
            GB = call * 16
            XB = call * 32

            @block.gpsimd
            def _(g):
                if call == 0:
                    g.wait_ge(t_wo.h, 128)
                else:
                    g.wait_ge(fs_x3.h, 256)
                    g.wait_ge(fv_x3.h, 16)
                    g.wait_ge(ft_B.h, 1024)
                    g.wait_ge(fv_ev.h, 1024)
                    g.wait_ge(fv_g.h, 48)
                    g.wait_ge(fv_pe.h, 256)
                    g.wait_ge(fv_gw.h, 256)
                    g.wait_ge(fv_he.h, 256)
                mg_aA.inc(g.dma_start(
                    out=vb(M_AB, 16384).rearrange("p (c e) -> p c e", e=1024),
                    in_=d_at.rearrange("(c p) e -> p c e", p=128)), 16)
                for k in range(16):
                    kk = KB + k
                    if k == 0 and call == 0:
                        g.wait_ge(t_av.h, 256)
                        g.wait_ge(t_sc.h, 4096)
                    if k >= 2:
                        g.wait_ge(mv_1.h, 3 * (kk - 2) + 3)
                        g.wait_ge(mc_s.h, 2 * (kk - 2) + 1)
                        if call == 0:
                            ms_x1.wait(g, k, k // 2)
                    if call == 1:
                        g.wait_ge(fs_x3.h, 16 * (k + 1))
                        if k == 0:
                            g.wait_ge(fs_xn.h, 256)
                            g.wait_ge(fs_g.h, 256)
                    row = k * 128
                    mg_x.inc(g.dma_start(out=vf(M_MX[k % 2], 4096),
                                         in_=d_src[row:row + 128, :]), kk)
                g.wait_ge(tm_h.h, HB + 128)
                g.wait_ge(c_ld.h, 272)
                mg_aB.inc(g.dma_start(
                    out=vb(M_AB, 16384).rearrange("p (e d) -> p e d", d=4096),
                    in_=barr_bf.rearrange("(e p) d -> p e d", p=128)), 16)
                for k in range(16):
                    kk = KB + k
                    if k >= 2:
                        g.wait_ge(mv_oe.h, call * 128 + (k - 2) * 8 + 8)
                        g.wait_ge(mc_s.h, 2 * KB + 32)
                    else:
                        if call == 0:
                            ms_x1.wait_total(g, 16)
                    if call == 1:
                        g.wait_ge(fs_x3.h, 16 * (k + 1))
                    row = k * 128
                    mg_r.inc(g.dma_start(out=vf(M_MX[k % 2], 4096),
                                         in_=d_res[row:row + 128, :]), kk)

            @block.sync
            def _(sy):
                if call == 0:
                    st_wo.wait_total(sy, 128)
                    sy.wait_ge(t_sc.h, 4096)
                    sy.wait_ge(t_v.h, 2048)
                for k in range(16):
                    kk = KB + k
                    if call == 0:
                        if k >= 2:
                            sy.wait_ge(mv_1.h, 3 * (k - 2) + 3)
                            ms_xn.wait(sy, k, k // 2)
                        ms_t.inc(sy.dma_start(out=vb(M_MAT[k % 2], 2048),
                                              in_=aot_dram[:, k * 128:(k + 1) * 128],
                                              transpose=True), k)
                        sy.wait_ge(mv_1.h, 3 * k + 1)
                        ms_x1.inc(sy.dma_start(out=x1_dram[k * 128:(k + 1) * 128, :],
                                               in_=vf(M_MX[k % 2], 4096)), k)
                    sy.wait_ge(mv_1.h, 3 * kk + 3)
                    ms_xn.inc(sy.dma_start(out=xnm_dram[k * 128:(k + 1) * 128, :],
                                           in_=vb(M_MAT[k % 2], 2048)), kk)
                for mc in range(4):
                    ms_xn.wait_total(sy, KB + mc * 4 + 4)
                    for ci in range(32):
                        R = RB + mc * 32 + ci
                        if call == 0 and mc == 0 and ci == 0:
                            sy.wait_ge(v_zt.h, 256)
                        if R >= 4:
                            sy.wait_ge(tm_r.h, R - 3)
                        ms_T1.inc(sy.dma_start(out=vb(M_XT1 + (ci % 4) * 256, 256),
                                               in_=xnm_dram[mc * 512:(mc + 1) * 512,
                                                            ci * 128:(ci + 1) * 128],
                                               transpose=True), R)
                for k in range(16):
                    for dc in range(8):
                        kk = (KB + k) * 8 + dc
                        sy.wait_ge(mv_oe.h, kk + 1)
                        row = k * 128
                        ms_o.inc(sy.dma_start(out=d_dst[row:row + 128, dc * 512:(dc + 1) * 512],
                                              in_=vf(M_MO[kk % 2], 512)), kk)

            @block.vector
            def _(v):
                for k in range(16):
                    kk = KB + k
                    bi = k % 2
                    mg_x.wait_done(v, kk)
                    if k == 0 and call == 0:
                        v.wait_ge(t_sc.h, 4096)
                        v.wait_ge(t_v.h, 2048)
                    if call == 0:
                        ms_t.wait_done(v, k)
                        vsync(v, v.tensor_tensor(vf(M_MX[bi], 4096), vf(M_MX[bi], 4096),
                                                 vb(M_MAT[bi], 2048), op=ALU.add))
                    vsync(v, v.reduce_sum(msm(bi, 0), vf(M_MX[bi], 4096), axis=AX.X))
                    mv_1.inc(v.tensor_scalar(msm(bi, 1), msm(bi, 0), -1.0 / D, None,
                                             op0=ALU.mult))
                    vsync(v, v.engine_nop())
                    v.wait_ge(mc_s.h, 2 * kk + 1)
                    vsync(v, v.tensor_tensor(msm(bi, 2), msm(bi, 1), msm(bi, 1),
                                             op=ALU.mult))
                    vsync(v, v.tensor_scalar(msm(bi, 3), msm(bi, 4), 1.0 / D, None,
                                             op0=ALU.mult))
                    mv_1.inc(v.tensor_tensor(msm(bi, 5), msm(bi, 3), msm(bi, 2),
                                             op=ALU.subtract))
                    v.wait_ge(mc_s.h, 2 * kk + 2)
                    vsync(v, v.tensor_tensor(msm(bi, 6), msm(bi, 1), msm(bi, 7),
                                             op=ALU.mult))
                    if kk >= 2:
                        ms_xn.wait(v, kk, kk // 2)
                    mv_1.inc(v.tensor_scalar(vb(M_MAT[bi], 2048), vf(M_MX[bi], 4096),
                                             msm(bi, 7), msm(bi, 6),
                                             op0=ALU.mult, op1=ALU.add))
                def hg_evac(v, hp):
                    v.wait_ge(tm_h.h, HB + (hp + 1) * 32)
                    for er in range(8):
                        mv_hg.inc(v.tensor_tensor(
                            vb(M_HGT, 8192)[:, hp * 4096 + er * 512:hp * 4096 + (er + 1) * 512],
                            PS[:, er * 512:(er + 1) * 512],
                            vb(M_GX, 2048)[:, er * 512:(er + 1) * 512], op=ALU.mult))

                for mc in range(4):
                    v.wait_ge(tm_r.h, RB + (mc + 1) * 32)
                    if call == 1 and mc == 0:
                        ms_o.wait_total(v, 128)
                    for tt in range(4):
                        ins = v.tensor_copy(vf(GT4 + tt * 64, 64),
                                            PS[:, tt * 512:tt * 512 + 64])
                        if tt < 3:
                            vsync(v, ins)
                    mv_g.inc(ins)
                    for tt in range(4):
                        Gi = GB + mc * 4 + tt
                        scr = vf(GT4 + tt * 64, 64)
                        vsync(v, v.max(out=vf(G_M8, 8), in_=scr))
                        vsync(v, v.match_replace(out=vf(G_SCR2, 64), in_to_replace=vf(G_M8, 8),
                                                 in_values=scr, imm_value=NEG))
                        vsync(v, v.max(out=vf(G_M8 + 8, 8), in_=vf(G_SCR2, 64)))
                        mv_g.inc(v.tensor_scalar(vf(G_NEGM, 1), vf(G_M8, 1), -1.0,
                                                 None, op0=ALU.mult))
                        v.wait_ge(mc_e.h, Gi + 1)
                        vsync(v, v.reciprocal(vf(G_RZ, 1), vf(G_Z16, 1)))
                        vsync(v, v.tensor_scalar(vf(G_RZ2, 1), vf(G_RZ, 1), SCALING, None,
                                                 op0=ALU.mult))
                        vsync(v, v.tensor_scalar(vb(G_MASK, 32), scr,
                                                 vf(G_M8 + 15, 1), None, op0=ALU.is_ge))
                        vsync(v, v.scalar_tensor_tensor(vf(G_G, 64), vf(G_ME, 64),
                                                        vf(G_RZ2, 1), vb(G_MASK, 32),
                                                        op0=ALU.mult, op1=ALU.mult))
                        for bi4 in range(4):
                            for bj in range(2):
                                ins = v.transpose(
                                    vf(G_GT32, 128, 32 * bj, 32 * bj + 32)[:, 32 * bi4:32 * bi4 + 32],
                                    vf(G_G, 64, 32 * bi4, 32 * bi4 + 32)[:, 32 * bj:32 * bj + 32])
                        vsync(v, ins)
                        mv_g.inc(v.tensor_copy(
                            vb(G_GTB, 256, 0, 64)[:, tt * 128:(tt + 1) * 128],
                            vf(G_GT32, 128, 0, 64)))
                    if mc >= 1:
                        hg_evac(v, mc - 1)
                    for er in range(8):
                        gx = XB + mc * 8 + er
                        v.wait_ge(tm_g.h, gx + 1)
                        mv_gx.inc(v.tensor_copy(vb(M_GX, 2048)[:, er * 512:(er + 1) * 512],
                                                PS[:, 4 * 512:4 * 512 + 512]))
                hg_evac(v, 3)
                for k in range(16):
                    for dc in range(8):
                        kk = (KB + k) * 8 + dc
                        v.wait_ge(tm_b.h, kk + 1)
                        mg_r.wait_done(v, KB + k)
                        if kk >= 2:
                            ms_o.wait(v, kk, kk // 2)
                        mv_oe.inc(v.tensor_tensor(
                            vf(M_MO[kk % 2], 512),
                            PS[:, (kk % 2) * 512:(kk % 2) * 512 + 512],
                            vf(M_MX[k % 2], 4096)[:, dc * 512:(dc + 1) * 512], op=ALU.add))

            @block.scalar
            def _(s):
                for k in range(16):
                    kk = KB + k
                    bi = k % 2
                    s.wait_ge(mv_1.h, 3 * kk + 1)
                    ssync(s, s.memzero(msm(bi, 4)))
                    mc_s.inc(s.activation(vf(M_SQS, 4096), vf(M_MX[bi], 4096),
                                          AF.Square, accum_out=msm(bi, 4)))
                    s.wait_ge(mv_1.h, 3 * kk + 2)
                    ssync(s, s.activation(msm(bi, 8), msm(bi, 5), AF.Ln, bias=s_eps[:]))
                    mc_s.inc(s.activation(msm(bi, 7), msm(bi, 8), AF.Exp, scale=-0.5))
                def t2_loads(hp):
                    for ci in range(32):
                        H = HB + hp * 32 + ci
                        if H >= 4:
                            s.wait_ge(tm_h.h, H - 3)
                        if call == 0 and hp == 0 and ci == 0:
                            s.wait_ge(v_zt.h, 256)
                        ms_T2.inc(s.dma_start(out=vb(M_XT2 + (ci % 4) * 256, 256),
                                              in_=xnm_dram[hp * 512:(hp + 1) * 512,
                                                           ci * 128:(ci + 1) * 128],
                                              transpose=True), H)

                for mc in range(4):
                    for tt in range(4):
                        Gi = GB + mc * 4 + tt
                        s.wait_ge(mv_g.h, call * 36 + mc * 9 + 2 * tt + 2)
                        s.memzero(vf(G_ZD, 1))
                        ssync(s, s.memzero(vf(G_Z16, 1)))
                        s.activation(vf(G_ME, 64), vf(GT4 + tt * 64, 64), AF.Exp,
                                     bias=vf(G_NEGM, 1), accum_out=vf(G_ZD, 1))
                        mc_e.inc(s.activation(vf(G_E16O, 16), vf(G_M8, 16), AF.Exp,
                                              bias=vf(G_NEGM, 1), accum_out=vf(G_Z16, 1)))
                    if mc >= 1:
                        t2_loads(mc - 1)
                t2_loads(3)

            @block.tensor
            def _(t):
                def hpass(t, hp):
                    for ci in range(32):
                        H = HB + hp * 32 + ci
                        ms_T2.wait_done(t, H)
                        if ci == 0:
                            t.wait_ge(mg_aA.h, 16 * (call + 1))
                            t.wait_ge(mv_gx.h, XB + hp * 8 + 8)
                            if hp <= 2:
                                t.wait_ge(mv_g.h, call * 36 + (hp + 1) * 9 + 1)
                            if call == 0 and hp == 0:
                                t.wait_ge(v_rz.h, 128)
                                t.wait_ge(v_at.h, 256)
                                t.wait_ge(v_ve.h, 512)
                            if hp > 0 or call == 1:
                                t.wait_ge(mv_hg.h, call * 32 + hp * 8)
                        for er in range(8):
                            ins = t.matmul(PS[:, er * 512:(er + 1) * 512],
                                           vb(M_AB, 16384)
                                           [:, ci * 1024 + er * 128:ci * 1024 + (er + 1) * 128],
                                           vb(M_XT2 + (ci % 4) * 256, 256),
                                           start=(ci == 0), stop=(ci == 31))
                        tm_h.inc(ins)

                for mc in range(4):
                    for ci in range(32):
                        R = RB + mc * 32 + ci
                        ms_T1.wait_done(t, R)
                        if ci == 0 and mc >= 1:
                            t.wait_ge(mv_hg.h, call * 32 + (mc - 1) * 8)
                        if R == 0:
                            t.wait_ge(s_wo.h, 128)
                            t.wait_ge(s_ex.h, 2048)
                        if call == 1 and R == RB:
                            t.wait_ge(mv_oe.h, 128)
                            t.wait_ge(fv_ev.h, fv_ev.n)
                            t.wait_ge(fse_A.h, fse_A.n)
                            t.wait_ge(fve_A.h, fve_A.n)
                            t.wait_ge(fv_pe.h, fv_pe.n)
                            t.wait_ge(fv_gw.h, fv_gw.n)
                            t.wait_ge(fv_he.h, fv_he.n)
                        for tt in range(4):
                            ins = t.matmul(PS[:, tt * 512:tt * 512 + 64],
                                           vb(M_XT1 + (ci % 4) * 256, 256)
                                           [:, tt * 128:(tt + 1) * 128],
                                           s_wrt[:, ci * 64:(ci + 1) * 64],
                                           start=(ci == 0), stop=(ci == 31))
                        tm_r.inc(ins)
                    if mc >= 1:
                        hpass(t, mc - 1)
                    for er in range(8):
                        gx = XB + mc * 8 + er
                        t.wait_ge(mv_g.h, call * 36 + mc * 9 + 9)
                        if er == 0 and mc >= 1:
                            t.wait_ge(mv_hg.h, call * 32 + (mc - 1) * 8 + 5)
                        if call == 0 and mc == 0 and er == 0:
                            t.wait_ge(v_at.h, 256)
                        if gx >= 1:
                            t.wait_ge(mv_gx.h, gx)
                        tm_g.inc(t.matmul(PS[:, 4 * 512:4 * 512 + 512],
                                          s_efull[:, er * 128:(er + 1) * 128],
                                          vb(G_GTB, 256, 0, 64), start=True, stop=True))
                hpass(t, 3)
                for k in range(16):
                    mc, tt = k // 4, k % 4
                    for dc in range(8):
                        kk = (KB + k) * 8 + dc
                        t.wait_ge(mg_aB.h, 16 * (call + 1))
                        if k == 0 and dc == 0:
                            t.wait_ge(mv_hg.h, call * 32 + 32)
                        if kk >= 2:
                            t.wait_ge(mv_oe.h, kk - 1)
                        for er in range(8):
                            ins = t.matmul(
                                PS[:, (kk % 2) * 512:(kk % 2) * 512 + 512],
                                vb(M_HGT, 8192)[:, mc * 4096 + er * 512 + tt * 128:
                                                mc * 4096 + er * 512 + (tt + 1) * 128],
                                vb(M_AB, 16384)[:, er * 4096 + dc * 512:er * 4096 + (dc + 1) * 512],
                                start=(er == 0), stop=(er == 7))
                        tm_b.inc(ins)

        sc4, _ = nc.enter_named_scope("ph4", False)
        if phases >= 4:
            moe_call(0, d_xq, x2_dram, x1_dram)
        nc.leave_named_scope("ph4", sc4, False)

        if phases <= 4:
            @block.gpsimd
            def _(g):
                if phases == 4:
                    ms_o.wait_total(g, 128)
                elif phases == 3:
                    if ph3sub >= 4:
                        st_wo.wait_total(g, 128)
                    elif ph3sub == 3:
                        g.wait_ge(v_at.h, 256)
                    elif ph3sub == 2:
                        g.wait_ge(s_ex.h, 2048)
                    else:
                        g.wait_ge(v_ve.h, 512)
                elif phases == 2:
                    g.wait_ge(dq5.h, dq5.n)
                else:
                    stq.wait_total(g, 32)
            return nc
        F_X2 = [0, 4096]
        F_V, F_ST = 8192, 12288
        F_W1, F_W2, F_W3, F_W4 = 20480, 24576, 28672, 32768
        F_XN, F_SM, F_SLAB = 36864, 38912, 39424
        F_E, F_MASK, F_G = 30720, 24576, 36864

        def fsm5(j):
            return vf(F_SM + j, 1)

        NCH = OWN // FCH
        fse_A, fve_A = sqt, vmc
        FSE0, FVE0, FCS0 = fse_A.n, fve_A.n, 0
        s_slabF = vb(F_SLAB, 4096, 0, 64)
        s_slabF2 = vb(F_SLAB + 4096, 4096, 0, 64)
        st4 = vb(F_ST, 8192, 0, 64).rearrange("y (r c j) -> y r c j", r=128, c=2, j=64)
        P4 = vb(F_W1, 4096, 0, 64).rearrange("k (j r) -> k j r", r=128)
        H4 = vb(F_W4, 4096).rearrange("p (j r) -> p j r", r=128)

        sc5, _ = nc.enter_named_scope("ph5", False)
        @block.tensor
        def _(t):
            for fc in range(NCH):
                for tf in range(4):
                    T = 4 * fc + tf
                    bmap = [0, 1, 6, 7] if tf < 3 else [0, 1, 2, 3]
                    for g4 in range(32):
                        if T == 0 and g4 == 0:
                            t.wait_ge(mv_oe.h, 128)
                        if g4 == 0:
                            if tf == 0:
                                t.wait_ge(fs_D.h, 16 * (fc + 1))
                            elif tf == 1:
                                t.wait_ge(fv_pe.h, 16 * (fc + 1))
                            elif tf == 2:
                                t.wait_ge(fs_gD.h, 16 * (fc + 1))
                            else:
                                t.wait_ge(fv_he.h, 16 * (fc + 1))
                            if T >= 1:
                                t.wait_ge(fse_A.h, FSE0 + 8 * T)
                                t.wait_ge(fve_A.h, FVE0 + 8 * T)
                        if g4 == 2:
                            if tf == 0 and fc >= 1:
                                t.wait_ge(fv_he.h, 16 * fc)
                            if tf == 3:
                                t.wait_ge(fv_ev.h, 16 * (4 * fc + 3))
                        if g4 >= 4:
                            q2 = g4 // 2 - 2
                            if q2 % 2 == 0:
                                t.wait_ge(fse_A.h, FSE0 + 8 * T + q2 // 2 + 1)
                            else:
                                t.wait_ge(fve_A.h, FVE0 + 8 * T + q2 // 2 + 1)
                        bk = bmap[g4 % 4]
                        for j in range(4):
                            r = g4 * 4 + j
                            ob = PS[0:64, bk * 512 + j * 128:bk * 512 + (j + 1) * 128]
                            if tf == 0:
                                ins = t.matmul(ob, vb(F_W4, 4096, 0, 64)[:, r * 64:(r + 1) * 64],
                                               s_fa[:], start=True, stop=True)
                            elif tf == 2:
                                ins = t.matmul(ob, vb(F_W1, 4096, 0, 64)[:, r * 64:(r + 1) * 64],
                                               s_fa[:], start=True, stop=True)
                            elif tf == 1:
                                ins = t.matmul(ob, P4[:, :, r], s_fa[:], start=True, stop=True)
                            else:
                                ins = t.matmul(ob, H4[:, :, r], s_fha[:], start=True, stop=True)
                        ft_A.inc(ins)
                    for jg in range(16):
                        GP = 64 * fc + 16 * tf + jg
                        if jg == 0:
                            t.wait_ge(fse_A.h, FSE0 + 8 * (T + 1))
                            t.wait_ge(fve_A.h, FVE0 + 8 * (T + 1))
                            if T == 0:
                                t.wait_ge(fs_cs.h, FCS0 + 16)
                            if tf == 3:
                                t.wait_ge(fv_gw.h, 16 * (fc + 1))
                        if GP >= 2:
                            t.wait_ge(fv_ev.h, GP - 1)
                        bb = (2 + jg % 2) if tf < 3 else (4 + jg % 2)
                        for q in range(4):
                            j1 = 4 * jg + q
                            if tf in (0, 2):
                                ob = PS[0:128, bb * 512 + q * 128:bb * 512 + (q + 1) * 128]
                                t.matmul(ob, s_slabF[:, j1 * 128:(j1 + 1) * 128],
                                         st4[:, :, 0, j1], start=True, stop=False)
                                ins = t.matmul(ob, s_slabF2[:, j1 * 128:(j1 + 1) * 128],
                                               st4[:, :, 1, j1], start=False, stop=True)
                            else:
                                ob = PS[0:64, bb * 512 + q * 128:bb * 512 + (q + 1) * 128]
                                t.matmul(ob, s_slabF[:, j1 * 128:j1 * 128 + 64],
                                         st4[:, :, 0, j1], start=True, stop=False)
                                ins = t.matmul(ob, s_slabF[:, j1 * 128 + 64:(j1 + 1) * 128],
                                               st4[:, :, 1, j1], start=False, stop=True)
                        ft_B.inc(ins)
                    if tf == 0:
                        for i in range(16):
                            if i == 0:
                                t.wait_ge(fc_s.h, 3 * fc + 1)
                                t.wait_ge(fv_ev.h, 64 * fc)
                                if fc == 0:
                                    t.wait_ge(mv_gx.h, 32)
                                    t.wait_ge(mv_hg.h, 32)
                            if 16 * fc + i >= 2:
                                t.wait_ge(fv_pe.h, 16 * fc + i - 1)
                            ft_P.inc(t.matmul(
                                PS[0:64, 2048 + (i % 2) * 512:2048 + (i % 2) * 512 + 512],
                                s_lh1[:, 0:64], vb(F_W2, 4096)[:, i * 512:(i + 1) * 512],
                                start=True, stop=True))
                    if tf == 2:
                        for i in range(16):
                            t.wait_ge(fv_ev.h, 64 * fc + 32 + i + 1)
                            if 16 * fc + i >= 2:
                                t.wait_ge(fv_gw.h, 16 * fc + i - 1)
                            ft_G.inc(t.matmul(
                                PS[:, 2048 + (i % 2) * 512:2048 + (i % 2) * 512 + 512],
                                s_sw[:], vb(F_W2, 4096)[:, i * 512:(i + 1) * 512],
                                start=True, stop=True))
                        for i in range(16):
                            if i == 0:
                                t.wait_ge(fv_g.h, 3 * fc + 3)
                            if 16 * fc + i >= 2:
                                t.wait_ge(fv_he.h, 16 * fc + i - 1)
                            pb = PS[:, 3072 + (i % 2) * 512:3072 + (i % 2) * 512 + 512]
                            t.matmul(pb, s_lh1[:], vb(F_W1, 4096)[:, i * 512:(i + 1) * 512],
                                     start=True, stop=False)
                            ft_H.inc(t.matmul(pb, s_lh2[:], vb(F_W3, 4096)[:, i * 512:(i + 1) * 512],
                                              start=False, stop=True))

        def ph5_ln(v, c):
            fg_x.wait_done(v, c)
            fx2 = vf(F_X2[c % 2], 4096)
            vsync(v, v.reduce_sum(fsm5(0), fx2, axis=AX.X))
            fv_ln.inc(v.tensor_scalar(fsm5(1), fsm5(0), -1.0 / D, None, op0=ALU.mult))
            vsync(v, v.engine_nop())
            v.wait_ge(fc_ln.h, 2 * c + 1)
            vsync(v, v.tensor_tensor(fsm5(48), fsm5(40), fsm5(41), op=ALU.add))
            vsync(v, v.tensor_tensor(fsm5(49), fsm5(42), fsm5(43), op=ALU.add))
            vsync(v, v.tensor_tensor(fsm5(50), fsm5(44), fsm5(45), op=ALU.add))
            vsync(v, v.tensor_tensor(fsm5(51), fsm5(46), fsm5(47), op=ALU.add))
            vsync(v, v.tensor_tensor(fsm5(48), fsm5(48), fsm5(49), op=ALU.add))
            vsync(v, v.tensor_tensor(fsm5(50), fsm5(50), fsm5(51), op=ALU.add))
            vsync(v, v.tensor_tensor(fsm5(4), fsm5(48), fsm5(50), op=ALU.add))
            vsync(v, v.tensor_tensor(fsm5(2), fsm5(1), fsm5(1), op=ALU.mult))
            vsync(v, v.tensor_scalar(fsm5(3), fsm5(4), 1.0 / D, None, op0=ALU.mult))
            fv_ln.inc(v.tensor_tensor(fsm5(5), fsm5(3), fsm5(2), op=ALU.subtract))
            v.wait_ge(fc_ln.h, 2 * c + 2)
            vsync(v, v.tensor_tensor(fsm5(6), fsm5(1), fsm5(7), op=ALU.mult))
            if c >= 1:
                v.wait_ge(fs_xn.h, 16 * c)
                v.wait_ge(fs_g.h, 16 * c)
            fv_ln.inc(v.tensor_scalar(vb(F_XN, 2048), fx2, fsm5(7), fsm5(6),
                                      op0=ALU.mult, op1=ALU.add))

        @block.vector
        def _(v):
            for fc in range(NCH):
                if fc == 0:
                    v.wait_ge(mv_oe.h, 128)
                    ph5_ln(v, 0)
                for tf in range(4):
                    T = 4 * fc + tf
                    for q in range(1, 16, 2):
                        v.wait_ge(ft_A.h, 32 * T + 2 * q + 2)
                        if q == 1:
                            v.wait_ge(ft_B.h, 16 * T)
                            if tf == 3:
                                v.wait_ge(fv_g.h, 3 * fc + 3)
                        base = 3072 if tf < 3 else 1024
                        fve_A.inc(v.tensor_copy(
                            vb(F_ST, 8192, 0, 64)[:, q * 1024:(q + 1) * 1024],
                            PS[0:64, base:base + 1024]))
                    for jg in range(16):
                        GP = 64 * fc + 16 * tf + jg
                        v.wait_ge(ft_B.h, GP + 1)
                        bb = (2 + jg % 2) if tf < 3 else (4 + jg % 2)
                        if tf == 0:
                            fv_ev.inc(v.tensor_copy(
                                vb(F_V, 4096)[:, jg * 512:(jg + 1) * 512],
                                PS[:, bb * 512:bb * 512 + 512]))
                        elif tf == 2:
                            if jg == 0:
                                v.wait_ge(fv_g.h, 3 * fc + 2)
                                v.wait_ge(fs_c.h, 16 * (fc + 1))
                            fv_ev.inc(v.tensor_copy(
                                vb(F_W2, 4096)[:, jg * 512:(jg + 1) * 512],
                                PS[:, bb * 512:bb * 512 + 512]))
                        else:
                            wreg = F_W2 if tf == 1 else F_W1
                            dst = vb(wreg, 4096, 0, 64).rearrange(
                                "p (r j) -> p r j", j=64)[:, :, 4 * jg:4 * jg + 4]
                            srcp = PS[0:64, bb * 512:bb * 512 + 512].rearrange(
                                "p (j r) -> p r j", j=4)
                            fv_ev.inc(v.tensor_copy(dst, srcp))
                    if tf == 0:
                        if fc >= 1:
                            v.wait_ge(fs_or.h, 16 * fc)
                            fv_x3.inc(v.tensor_tensor(vf(F_W3, 4096), vf(F_X2[(fc - 1) % 2], 4096),
                                                      vb(F_W2, 2048), op=ALU.add))
                        for i in range(16):
                            v.wait_ge(ft_P.h, 16 * fc + i + 1)
                            if i == 0 and fc >= 1:
                                v.wait_ge(fs_o.h, 16 * fc)
                            fv_pe.inc(v.tensor_copy(
                                vb(F_W1, 4096, 0, 64)[:, i * 512:(i + 1) * 512],
                                PS[0:64, 2048 + (i % 2) * 512:2048 + (i % 2) * 512 + 512]))
                    elif tf == 1:
                        v.wait_ge(fs_cr.h, 16 * (fc + 1))
                        vsync(v, v.max(out=vb(F_SM + 16, 4), in_=vb(F_W3, 2048)))
                        fv_g.inc(v.tensor_scalar(fsm5(24), vb(F_SM + 16, 4)[:, 0:1], -1.0,
                                                 None, op0=ALU.mult))
                        v.wait_ge(fc_s.h, 3 * fc + 3)
                        vsync(v, v.reciprocal(fsm5(25), fsm5(26)))
                        vsync(v, v.tensor_copy(fsm5(27), vb(F_SM + 16, 4)[:, 7:8]))
                        vsync(v, v.tensor_scalar(vb(F_MASK, 2048), vb(F_W3, 2048),
                                                 fsm5(27), None, op0=ALU.is_ge))
                        v.wait_ge(fs_xn.h, 16 * (fc + 1))
                        fv_g.inc(v.scalar_tensor_tensor(vb(F_G, 2048), vb(F_E, 2048),
                                                        fsm5(25), vb(F_MASK, 2048),
                                                        op0=ALU.mult, op1=ALU.mult))
                    elif tf == 2:
                        for i in range(16):
                            v.wait_ge(ft_G.h, 16 * fc + i + 1)
                            fv_gw.inc(v.tensor_copy(
                                vb(F_ST, 4096)[:, i * 512:(i + 1) * 512],
                                PS[:, 2048 + (i % 2) * 512:2048 + (i % 2) * 512 + 512]))
                        vsync(v, v.engine_nop())
                        vsync(v, v.tensor_tensor(vb(F_W1, 4096), vb(F_V, 4096),
                                                 vb(F_W2, 4096), op=ALU.mult))
                        fv_g.inc(v.tensor_tensor(vb(F_W3, 4096), vb(F_V, 4096),
                                                 vb(F_ST, 4096), op=ALU.mult))
                        for i in range(16):
                            v.wait_ge(ft_H.h, 16 * fc + i + 1)
                            fv_he.inc(v.tensor_copy(
                                vb(F_W4, 4096)[:, i * 512:(i + 1) * 512],
                                PS[:, 3072 + (i % 2) * 512:3072 + (i % 2) * 512 + 512]))
                        if fc + 1 < NCH:
                            ph5_ln(v, fc + 1)

            v.wait_ge(fs_or.h, 16 * NCH)
            fv_x3.inc(v.tensor_tensor(vf(F_W3, 4096), vf(F_X2[(NCH - 1) % 2], 4096),
                                      vb(F_W2, 2048), op=ALU.add))

        def ph5_lns(s, c):
            s.wait_ge(fv_ln.h, 3 * c + 1)
            for i in range(8):
                ssync(s, s.memzero(fsm5(40 + i)))
                ins = s.activation(vb(47616, 256),
                                   vf(F_X2[c % 2], 4096)[:, i * 512:(i + 1) * 512],
                                   AF.Square, accum_out=fsm5(40 + i))
                if i < 7:
                    ssync(s, ins)
            fc_ln.inc(ins)
            s.wait_ge(fv_ln.h, 3 * c + 2)
            ssync(s, s.activation(fsm5(8), fsm5(5), AF.Ln, bias=s_eps[:]))
            fc_ln.inc(s.activation(fsm5(7), fsm5(8), AF.Exp, scale=-0.5))

        @block.scalar
        def _(s):
            for fc in range(NCH):
                if fc == 0:
                    ph5_lns(s, 0)
                for tf in range(4):
                    T = 4 * fc + tf
                    for q in range(0, 16, 2):
                        s.wait_ge(ft_A.h, 32 * T + 2 * q + 2)
                        if q == 0:
                            s.wait_ge(ft_B.h, 16 * T)
                            if tf == 3:
                                s.wait_ge(fv_g.h, 3 * fc + 3)
                        fse_A.inc(s.activation(
                            vb(F_ST, 8192, 0, 64)[:, q * 1024:(q + 1) * 1024],
                            PS[0:64, 0:1024], AF.Identity))
                    if tf == 0:
                        s.wait_ge(fv_ev.h, 64 * fc + 16)
                        s.wait_ge(fv_x3.h, fc)
                        fc_s.inc(s.activation(vb(F_W2, 4096), vb(F_V, 4096), AF.Square))
                    elif tf == 1:
                        s.wait_ge(fv_g.h, 3 * fc + 1)
                        s.wait_ge(fs_cr.h, 16 * (fc + 1))
                        fc_s.inc(s.activation(vb(F_E, 2048), vb(F_W3, 2048), AF.Exp,
                                              bias=fsm5(24)))
                        ssync(s, s.memzero(fsm5(26)))
                        fc_s.inc(s.activation(vf(F_SM + 32, 8), vb(F_SM + 16, 4), AF.Exp,
                                              bias=fsm5(24), accum_out=fsm5(26)))
                    elif tf == 2 and fc + 1 < NCH:
                        ph5_lns(s, fc + 1)

        @block.sync
        def _(sy):
            def do_xn_D(c):
                row2 = (c % 2) * FCH
                sy.wait_ge(fv_ln.h, 3 * c + 3)
                fs_xn.inc(sy.dma_start(out=fb_dram[row2:row2 + FCH, :], in_=vb(F_XN, 2048)), 16)
                sy.wait_ge(fs_xn.h, 16 * (c + 1))
                if c == 0:
                    sy.wait_ge(mv_oe.h, 128)
                else:
                    sy.wait_ge(ft_A.h, 128 * c)
                fs_D.inc(sy.dma_start(
                    out=vb(F_W4, 4096, 0, 64).rearrange("x (r y) -> x r y", r=FCH),
                    in_=fb_dram[row2:row2 + FCH, :].rearrange("r (x y) -> x r y", x=64, y=64)), 16)

            do_xn_D(0)
            for fc in range(NCH):
                row2 = (fc % 2) * FCH
                if fc >= 1:
                    sy.wait_ge(fv_x3.h, fc)
                    fs_x3.inc(sy.dma_start(out=x3_dram[(fc - 1) * FCH:fc * FCH, :],
                                           in_=vf(F_W3, 4096)), 16)
                sy.wait_ge(fv_ev.h, 64 * fc + 32)
                fs_c.inc(sy.dma_start(out=fc_dram[(fc % 2) * 64:(fc % 2) * 64 + 64, :],
                                      in_=vb(F_W2, 4096, 0, 64)), 16)
                sy.wait_ge(fs_c.h, 16 * (fc + 1))
                sy.wait_ge(fs_x3.h, 16 * fc)
                fs_cr.inc(sy.dma_start(
                    out=vb(F_W3, 2048).rearrange("r (l m) -> r l m", l=64),
                    in_=fc_dram[(fc % 2) * 64:(fc % 2) * 64 + 64, :]
                    .rearrange("l (r m) -> r l m", r=FCH)), 16)
                sy.wait_ge(fv_g.h, 3 * fc + 2)
                fs_g.inc(sy.dma_start(out=fg_dram[row2:row2 + FCH, :], in_=vb(F_G, 2048)), 16)
                sy.wait_ge(fs_g.h, 16 * (fc + 1))
                fs_gD.inc(sy.dma_start(
                    out=vb(F_W1, 4096, 0, 64).rearrange("x (r y) -> x r y", r=FCH),
                    in_=fg_dram[row2:row2 + FCH, :].rearrange("r (x y) -> x r y", x=64, y=64)), 16)
                if fc + 1 < NCH:
                    do_xn_D(fc + 1)
                sy.wait_ge(fv_ev.h, 64 * fc + 64)
                fs_o.inc(sy.dma_start(out=fo_dram[(fc % 2) * 64:(fc % 2) * 64 + 64, :],
                                      in_=vb(F_W1, 4096, 0, 64)), 16)
                sy.wait_ge(fs_o.h, 16 * (fc + 1))
                sy.wait_ge(fv_g.h, 3 * fc + 3)
                fs_or.inc(sy.dma_start(
                    out=vb(F_W2, 2048).rearrange("r (l m) -> r l m", l=64),
                    in_=fo_dram[(fc % 2) * 64:(fc % 2) * 64 + 64, :]
                    .rearrange("l (r m) -> r l m", r=FCH)), 16)
            sy.wait_ge(fv_x3.h, NCH)
            fs_x3.inc(sy.dma_start(out=x3_dram[(NCH - 1) * FCH:NCH * FCH, :],
                                   in_=vf(F_W3, 4096)), 16)

        @block.gpsimd
        def _(g):
            g.wait_ge(tm_b.h, 128)
            ms_xn.wait_total(g, 16)
            ms_o.wait_total(g, 128)
            fs_cs.inc(g.dma_start(out=vb(F_SLAB, 8192, 0, 64), in_=d_slab), 16)
            for fc in range(NCH):
                if fc == 0:
                    g.wait_ge(mv_oe.h, 128)
                ms_o.wait_total(g, 8 * (fc + 1))
                if fc >= 2:
                    g.wait_ge(fv_x3.h, fc - 1)
                fg_x.inc(g.dma_start(out=vf(F_X2[fc % 2], 4096),
                                     in_=x2_dram[fc * FCH:(fc + 1) * FCH, :]), fc)

        nc.leave_named_scope("ph5", sc5, False)
        if phases <= 5:
            @block.gpsimd
            def _(g):
                g.wait_ge(fs_x3.h, fs_x3.n)
            return nc
        sc6, _ = nc.enter_named_scope("ph6", False)
        moe_call(1, x3_dram, d_out, x3_dram)
        nc.leave_named_scope("ph6", sc6, False)

        @block.gpsimd
        def _(g):
            ms_o.wait_total(g, 256)
    return nc


# version 13
# speedup vs baseline: 1.0071x; 1.0021x over previous
import numpy as np
import ml_dtypes
from contextlib import ExitStack

import concourse.bass as bass
import concourse.mybir as mybir
from concourse.bass_utils import run_bass_kernel_spmd

F32 = mybir.dt.float32
BF16 = mybir.dt.bfloat16
AF = mybir.ActivationFunctionType
ALU = mybir.AluOpType
AX = mybir.AxisListType

D = 4096
SEQ = 4096
OWN = 2048
NH, KVL, QL, HD = 32, 4, 12, 128
NEXP, RANK = 64, 16
SCALING = 2.0
EPS = 1e-5
NEG = -1.0e30

QCG = 1024
QS = 512
MCH = 512
FCH = 128


def _consts():
    jj = np.arange(64)
    c64 = np.cos(2 * np.pi * np.outer(jj, jj) / 64).astype(np.float32)
    s64 = np.sin(2 * np.pi * np.outer(jj, jj) / 64).astype(np.float32)
    C = {}
    C["FA"] = np.concatenate([c64, -s64], axis=1)
    C["FHA"] = np.block([[c64, -s64], [s64, c64]])
    f = np.arange(D)
    ang = 2 * np.pi * np.outer(jj, f) / D
    Cb = np.cos(ang).astype(np.float32)
    Sb = np.sin(ang).astype(np.float32)
    slab = np.zeros((64, 16384), np.float32)
    for j1 in range(64):
        slab[:, j1 * 128:j1 * 128 + 64] = Cb[:, j1::64]
        slab[:, j1 * 128 + 64:(j1 + 1) * 128] = Sb[:, j1::64]
        slab[:, 8192 + j1 * 128:8192 + j1 * 128 + 64] = Sb[:, j1::64]
        slab[:, 8192 + j1 * 128 + 64:8192 + (j1 + 1) * 128] = -Cb[:, j1::64]
    C["SLAB"] = slab
    C["ONES128"] = np.ones((128, 1), np.float32)
    C["ONES1"] = np.ones((1, 128), np.float32)
    hm = np.zeros((128, 32), np.float32)
    for h in range(NH):
        hm[4 * h:4 * h + 4, h] = 1.0
    C["HEADMASK"] = hm
    ef = np.zeros((64, NEXP * RANK), np.float32)
    for e in range(NEXP):
        ef[e, RANK * e:RANK * (e + 1)] = 1.0
    C["EFULL"] = ef
    I64 = np.eye(64, dtype=np.float32)
    L1 = np.zeros((128, 128), np.float32)
    L1[0:64, 0:64] = I64
    L1[64:128, 0:64] = I64
    L2 = np.zeros((128, 128), np.float32)
    L2[0:64, 64:128] = -I64
    L2[64:128, 64:128] = I64
    C["LH1"] = L1 / D
    C["LH2"] = L2 / D
    SW = np.zeros((128, 128), np.float32)
    SW[0:64, 64:128] = I64
    SW[64:128, 0:64] = I64
    C["SW"] = SW
    return C


def _shard_inputs(inp):
    x = np.ascontiguousarray(np.asarray(inp["L0_attn_a_x"], np.float32))
    Wq = np.asarray(inp["L0_attn_a_Wq_d"], np.float32)
    bq = np.asarray(inp["L0_attn_a_bq_d"], np.float32)
    Wqk = np.asarray(inp["L0_attn_a_W_qk"], np.float32)
    bqk = np.asarray(inp["L0_attn_a_b_qk"], np.float32)
    Wkv = np.asarray(inp["L0_attn_a_Wkv_d"], np.float32)
    bkv = np.asarray(inp["L0_attn_a_bkv_d"], np.float32)
    Wvu = np.asarray(inp["L0_attn_a_Wv_u"], np.float32)
    bvu = np.asarray(inp["L0_attn_a_bv_u"], np.float32)
    Wo = np.ascontiguousarray(np.asarray(inp["L0_attn_a_Wo"], np.float32))
    bo = np.asarray(inp["L0_attn_a_bo"], np.float32)
    A = np.asarray(inp["L0_ffn_A"], np.float32)
    Bm = np.asarray(inp["L0_ffn_B"], np.float32)
    Wr = np.asarray(inp["L0_ffn_Wr"], np.float32)

    wpack = np.zeros((D, 36), np.float32)
    wpack[:, 0:12] = Wq
    wpack[:, 32:36] = Wkv
    b36 = np.zeros((36, 1), np.float32)
    b36[0:12, 0] = bq
    b36[32:36, 0] = bkv
    wv5 = np.zeros((5, D), np.float32)
    wv5[0:4] = Wvu
    wv5[4] = bvu
    AT = np.ascontiguousarray(A.reshape(NEXP * RANK, D).T)
    Barr = np.ascontiguousarray(Bm.transpose(0, 2, 1).reshape(NEXP * RANK, D))
    WrT = np.ascontiguousarray(Wr.T)

    C = _consts()
    shared = dict(wpack=wpack, b36=b36, wqk=Wqk, bqk=bqk.reshape(128, 1),
                  wv5=wv5, wo=Wo,
                  bo=np.ascontiguousarray(bo.reshape(32, 128).T),
                  at=AT, barr=Barr, wrt=WrT, **{k.lower(): v for k, v in C.items()})
    maps = []
    for c in range(8):
        b, half = c // 2, c % 2
        m = dict(shared)
        m["xq"] = np.ascontiguousarray(x[b, half * OWN:(half + 1) * OWN])
        m["xf"] = np.ascontiguousarray(x[b, (1 - half) * OWN:(2 - half) * OWN])
        maps.append(m)
    return maps


class Sem:
    def __init__(self, ctx, nc, name):
        self.h = ctx.enter_context(nc.semaphore(name))
        self.n = 0

    def inc(self, instr, k=1):
        instr.then_inc(self.h, k)
        self.n += k
        return self.n


class SemPair:
    def __init__(self, ctx, nc, name):
        self.s = [Sem(ctx, nc, name + "a"), Sem(ctx, nc, name + "b")]

    def inc(self, instr, i, k=16):
        return self.s[i % 2].inc(instr, k)

    def wait(self, eng, i, cnt_of_parity):
        eng.wait_ge(self.s[i % 2].h, 16 * cnt_of_parity)

    def wait_done(self, eng, i):
        eng.wait_ge(self.s[i % 2].h, 16 * (i // 2 + 1))

    def wait_total(self, eng, total):
        eng.wait_ge(self.s[0].h, 16 * ((total + 1) // 2))
        eng.wait_ge(self.s[1].h, 16 * (total // 2))


class SemQuad:
    def __init__(self, ctx, nc, name):
        self.s = [Sem(ctx, nc, name + chr(97 + i)) for i in range(4)]

    def inc(self, instr, i, k=16):
        return self.s[i % 4].inc(instr, k)

    def wait_done(self, eng, i):
        eng.wait_ge(self.s[i % 4].h, 16 * (i // 4 + 1))


def kernel(**inputs):
    maps = _shard_inputs(inputs)
    nc = build_program()
    res = run_bass_kernel_spmd(nc, maps, core_ids=list(range(8)))
    out = np.empty((4, SEQ, D), np.float32)
    for c in range(8):
        b, half = c // 2, c % 2
        out[b, half * OWN:(half + 1) * OWN] = res.results[c]["out"]
    return out


def build_program(phases=6, dbg=False, ph3sub=4):
    nc = bass.Bass("TRN2", target_bir_lowering=False, debug=False)

    d_xq = nc.dram_tensor("xq", [OWN, D], F32, kind="ExternalInput").ap()
    d_xf = nc.dram_tensor("xf", [OWN, D], F32, kind="ExternalInput").ap()
    d_wpack = nc.dram_tensor("wpack", [D, 36], F32, kind="ExternalInput").ap()
    d_b36 = nc.dram_tensor("b36", [36, 1], F32, kind="ExternalInput").ap()
    d_wqk = nc.dram_tensor("wqk", [QL, 128], F32, kind="ExternalInput").ap()
    d_bqk = nc.dram_tensor("bqk", [128, 1], F32, kind="ExternalInput").ap()
    d_wv5 = nc.dram_tensor("wv5", [5, D], F32, kind="ExternalInput").ap()
    d_wo = nc.dram_tensor("wo", [D, D], F32, kind="ExternalInput").ap()
    d_bo = nc.dram_tensor("bo", [128, 32], F32, kind="ExternalInput").ap()
    d_at = nc.dram_tensor("at", [D, NEXP * RANK], F32, kind="ExternalInput").ap()
    d_barr = nc.dram_tensor("barr", [NEXP * RANK, D], F32, kind="ExternalInput").ap()
    d_wrt = nc.dram_tensor("wrt", [D, NEXP], F32, kind="ExternalInput").ap()
    d_fa = nc.dram_tensor("fa", [64, 128], F32, kind="ExternalInput").ap()
    d_fha = nc.dram_tensor("fha", [128, 128], F32, kind="ExternalInput").ap()
    d_slab = nc.dram_tensor("slab", [64, 16384], F32, kind="ExternalInput").ap()
    d_ones128 = nc.dram_tensor("ones128", [128, 1], F32, kind="ExternalInput").ap()
    d_ones1 = nc.dram_tensor("ones1", [1, 128], F32, kind="ExternalInput").ap()
    d_headmask = nc.dram_tensor("headmask", [128, 32], F32, kind="ExternalInput").ap()
    d_efull = nc.dram_tensor("efull", [64, NEXP * RANK], F32, kind="ExternalInput").ap()
    d_lh1 = nc.dram_tensor("lh1", [128, 128], F32, kind="ExternalInput").ap()
    d_lh2 = nc.dram_tensor("lh2", [128, 128], F32, kind="ExternalInput").ap()
    d_sw = nc.dram_tensor("sw", [128, 128], F32, kind="ExternalInput").ap()
    d_out = nc.dram_tensor("out", [OWN, D], F32, kind="ExternalOutput").ap()

    dk0 = dict(kind="ExternalOutput") if dbg else {}
    xn_dram = nc.dram_tensor("xn_s", [SEQ, D], BF16, **dk0).ap()
    qt_dram = nc.dram_tensor("qt_s", [128, OWN], BF16, **dk0).ap()
    nb_dram = nc.dram_tensor("nb_s", [32, OWN], BF16, **dk0).ap()
    ckv_dram = nc.dram_tensor("ckv_s", [5, SEQ], BF16, **dk0).ap()
    mx_dram = nc.dram_tensor("mx_s", [4, 1], F32).ap()
    mx2_dram = nc.dram_tensor("mx2_s", [1, 1], F32).ap()
    aot_dram = nc.dram_tensor("aot_s", [D, OWN], BF16, **({"kind": "ExternalOutput"} if dbg else {})).ap()
    dk = dict(kind="ExternalOutput") if dbg else {}
    x1_dram = nc.dram_tensor("x1_s", [OWN, D], F32, **dk).ap()
    x2_dram = nc.dram_tensor("x2_s", [OWN, D], F32, **dk).ap()
    x3_dram = nc.dram_tensor("x3_s", [OWN, D], F32, **dk).ap()
    xnm_dram = nc.dram_tensor("xnm_s", [OWN, D], BF16).ap()
    fb_dram = nc.dram_tensor("fb_s", [2 * FCH, D], BF16).ap()
    fc_dram = nc.dram_tensor("fc_s", [128, 8192], BF16).ap()
    fg_dram = nc.dram_tensor("fg_s", [2 * FCH, D], BF16).ap()
    barr_bf = nc.dram_tensor("barr_bf", [NEXP * RANK, D], BF16).ap()
    fo_dram = nc.dram_tensor("fo_s", [128, 8192], BF16).ap()

    with ExitStack() as ctx:
        sb = lambda n, s, d: ctx.enter_context(nc.sbuf_tensor(n, s, d))
        mksem = lambda n: Sem(ctx, nc, n)

        s_wpack = sb("c_wpack", [128, 32 * 36], BF16)
        s_b36 = sb("c_b36", [36, 1], F32)
        s_wqk_raw = sb("c_wqkr", [QL, 128], F32)
        s_wqk05 = sb("c_wqk05", [QL, 128], BF16)
        s_bqk = sb("c_bqk", [128, 1], F32)
        s_bqk05 = sb("c_bqk05", [128, 1], F32)
        s_wv5 = sb("c_wv5", [5, D], BF16)
        s_wrt = sb("c_wrt", [128, 32 * NEXP], BF16)
        s_bo = sb("c_bo", [128, 32], F32)
        s_fa = sb("c_fa", [64, 128], BF16)
        s_fha = sb("c_fha", [128, 128], BF16)
        s_ones128 = sb("c_ones128", [128, 1], BF16)
        s_ones1 = sb("c_ones1", [1, 128], BF16)
        s_headmask = sb("c_hm", [128, 32], BF16)
        s_efull = sb("c_efull", [64, NEXP * RANK], BF16)
        s_lh1 = sb("c_lh1", [128, 128], BF16)
        s_lh2 = sb("c_lh2", [128, 128], BF16)
        s_sw = sb("c_sw", [128, 128], BF16)
        s_eps = sb("c_eps", [128, 1], F32)

        AW = 47872
        arena = sb("arena", [128, AW], F32)
        PS = ctx.enter_context(nc.psum_tensor("PS", [128, 4096], F32))

        def vf(off, n, p0=0, p1=128):
            return arena.ap()[p0:p1, off:off + n]

        def vb(off, n, p0=0, p1=128):
            return arena.ap()[p0:p1, off:off + n].bitcast(BF16)

        def bank(i, w=512):
            return PS[:, i * 512:i * 512 + w]

        def vsync(v, instr):
            vch.inc(instr)
            v.wait_ge(vch.h, vch.n)

        def ssync(s, instr):
            sch.inc(instr)
            s.wait_ge(sch.h, sch.n)

        def gsync(g, instr):
            gch.inc(instr)
            g.wait_ge(gch.h, gch.n)

        c_ld = mksem("c_ld")
        q_ld = SemPair(ctx, nc, "q_ld")
        vst = mksem("vst"); sst = mksem("sst"); vst2 = mksem("vst2")
        vch = mksem("vch"); sch = mksem("sch"); gch = mksem("gch")
        sst2 = mksem("sst2"); nrm = mksem("nrm"); stq = SemPair(ctx, nc, "stq")
        txn = SemPair(ctx, nc, "txn"); tpj = mksem("tpj"); spj = mksem("spj")
        tqt = mksem("tqt"); sqt = mksem("sqt"); sab = mksem("sab")
        tbs = mksem("tbs"); vmc = mksem("vmc"); dmc = mksem("dmc")
        vnb = mksem("vnb"); dq5 = mksem("dq5"); dck = mksem("dck")
        t_v = mksem("t_v"); v_ve = mksem("v_ve")
        t_sc = mksem("t_sc"); s_ex = mksem("s_ex"); t_av = mksem("t_av")
        v_zt = mksem("v_zt"); t_zm = mksem("t_zm"); v_rz = mksem("v_rz")
        t_zb = mksem("t_zb"); v_at = mksem("v_at")
        g_wo = SemPair(ctx, nc, "g_wo"); t_wo = mksem("t_wo"); s_wo = mksem("s_wo")
        st_wo = SemPair(ctx, nc, "st_wo")
        mg_x = SemPair(ctx, nc, "mg_x"); mg_aA = mksem("mg_aA"); mg_aB = mksem("mg_aB"); mg_r = SemPair(ctx, nc, "mg_r")
        ms_t = SemPair(ctx, nc, "ms_t"); ms_xn = SemPair(ctx, nc, "ms_xn"); ms_T1 = SemQuad(ctx, nc, "msT1")
        ms_T2 = SemQuad(ctx, nc, "msT2"); ms_o = SemPair(ctx, nc, "ms_o"); ms_x1 = SemPair(ctx, nc, "ms_x1")
        mv_1 = mksem("mv_1"); mv_g = mksem("mv_g"); mv_gx = mksem("mv_gx")
        mv_hg = mksem("mv_hg"); mv_oe = mksem("mv_oe")
        mc_s = mksem("mc_s"); mc_e = mksem("mc_e")
        tm_r = mksem("tm_r"); tm_g = mksem("tm_g"); tm_h = mksem("tm_h")
        tm_b = mksem("tm_b")
        fg_x = SemPair(ctx, nc, "fg_x"); fs_xn = mksem("fs_xn"); fs_D = mksem("fs_D")
        ft_A = mksem("ft_A"); ft_B = mksem("ft_B"); fs_cs = mksem("fs_cs")
        fv_ev = mksem("fv_ev"); fs_c = mksem("fs_c"); fs_cr = mksem("fs_cr")
        fv_g = mksem("fv_g"); fc_s = mksem("fc_s"); fs_g = mksem("fs_g")
        fs_gD = mksem("fs_gD"); fs_o = mksem("fs_o")
        fs_or = mksem("fs_or"); fv_x3 = mksem("fv_x3"); fs_x3 = mksem("fs_x3")
        fv_ln = mksem("fv_ln"); fc_ln = mksem("fc_ln")
        ft_P = mksem("ft_P"); fv_pe = mksem("fv_pe")
        ft_G = mksem("ft_G"); fv_gw = mksem("fv_gw")
        ft_H = mksem("ft_H"); fv_he = mksem("fv_he")

        block = ctx.enter_context(nc.Block())

        sc0, _ = nc.enter_named_scope("ph0", False)

        @block.gpsimd
        def _(g):
            g.dma_start(out=s_wpack.ap().rearrange("p (c w) -> p c w", w=36),
                        in_=d_wpack.rearrange("(c p) w -> p c w", p=128)).then_inc(c_ld.h, 16)
            g.dma_start(out=s_b36[:], in_=d_b36).then_inc(c_ld.h, 16)
            g.dma_start(out=s_wqk_raw[:], in_=d_wqk).then_inc(c_ld.h, 16)
            g.dma_start(out=s_bqk[:], in_=d_bqk).then_inc(c_ld.h, 16)
            g.dma_start(out=s_wv5[:], in_=d_wv5).then_inc(c_ld.h, 16)
            g.dma_start(out=s_wrt.ap().rearrange("p (c e) -> p c e", e=NEXP),
                        in_=d_wrt.rearrange("(c p) e -> p c e", p=128)).then_inc(c_ld.h, 16)
            g.dma_start(out=s_bo.ap().unsqueeze(2),
                        in_=d_bo.rearrange("(c p) o -> p c o", p=128)).then_inc(c_ld.h, 16)
            for s_t, d_t in [(s_fa, d_fa), (s_fha, d_fha),
                             (s_ones128, d_ones128), (s_ones1, d_ones1),
                             (s_headmask, d_headmask), (s_efull, d_efull),
                             (s_lh1, d_lh1), (s_lh2, d_lh2), (s_sw, d_sw)]:
                g.dma_start(out=s_t[:], in_=d_t).then_inc(c_ld.h, 16)
        c_ld.n = 16 * 16

        @block.vector
        def _(v):
            v.wait_ge(c_ld.h, c_ld.n)
            v.memset(s_eps[:], EPS)
            v.tensor_scalar(s_wqk05[:], s_wqk_raw[:], 0.5, None, op0=ALU.mult)
            vst.inc(v.tensor_scalar(s_bqk05[:], s_bqk[:], 0.5, None, op0=ALU.mult))

        nc.leave_named_scope("ph0", sc0, False)
        sc1, _ = nc.enter_named_scope("ph1", False)
        a_x = [0, 4096]
        a_sqs = 8192
        a_xn = [12288, 14336]
        a_sm = 16384

        def ph1_smalls(bi, k):
            return vf(a_sm + bi * 16 + k, 1)

        @block.gpsimd
        def _(g):
            for i in range(32):
                if i >= 2:
                    g.wait_ge(nrm.h, i - 1)
                    g.wait_ge(sst.h, i - 1)
                src = d_xq if i < 16 else d_xf
                row = (i % 16) * 128
                q_ld.inc(g.dma_start(out=vf(a_x[i % 2], 4096),
                                     in_=src[row:row + 128, :]), i)

        @block.vector
        def _(v):
            for i in range(32):
                bi = i % 2
                q_ld.wait_done(v, i)
                vsync(v, v.reduce_sum(ph1_smalls(bi, 0), vf(a_x[bi], 4096), axis=AX.X))
                vst.inc(v.tensor_scalar(ph1_smalls(bi, 1), ph1_smalls(bi, 0),
                                        -1.0 / D, None, op0=ALU.mult))
                v.wait_ge(vch.h, vch.n)
                vsync(v, v.engine_nop())
                v.wait_ge(sst.h, i + 1)
                vsync(v, v.tensor_tensor(ph1_smalls(bi, 2), ph1_smalls(bi, 1),
                                         ph1_smalls(bi, 1), op=ALU.mult))
                vsync(v, v.tensor_scalar(ph1_smalls(bi, 3), ph1_smalls(bi, 4), 1.0 / D,
                                         None, op0=ALU.mult))
                vst2.inc(v.tensor_tensor(ph1_smalls(bi, 5), ph1_smalls(bi, 3),
                                         ph1_smalls(bi, 2), op=ALU.subtract))
                v.wait_ge(sst2.h, i + 1)
                v.tensor_tensor(ph1_smalls(bi, 6), ph1_smalls(bi, 1),
                                ph1_smalls(bi, 7), op=ALU.mult)
                vsync(v, v.engine_nop())
                if i >= 2:
                    stq.wait(v, i, i // 2)
                nrm.inc(v.tensor_scalar(vb(a_xn[bi], 2048), vf(a_x[bi], 4096),
                                        ph1_smalls(bi, 7), ph1_smalls(bi, 6),
                                        op0=ALU.mult, op1=ALU.add))

        @block.scalar
        def _(s):
            for i in range(32):
                bi = i % 2
                q_ld.wait_done(s, i)
                ssync(s, s.memzero(ph1_smalls(bi, 4)))
                sst.inc(s.activation(vf(a_sqs, 4096), vf(a_x[bi], 4096), AF.Square,
                                     accum_out=ph1_smalls(bi, 4)))
                s.wait_ge(vst2.h, i + 1)
                ssync(s, s.activation(ph1_smalls(bi, 8), ph1_smalls(bi, 5), AF.Ln,
                                      bias=s_eps[:]))
                sst2.inc(s.activation(ph1_smalls(bi, 7), ph1_smalls(bi, 8), AF.Exp,
                                      scale=-0.5))

        @block.sync
        def _(sy):
            for i in range(32):
                bi = i % 2
                sy.wait_ge(nrm.h, i + 1)
                row = (i % 16) * 128 + (0 if i < 16 else OWN)
                stq.inc(sy.dma_start(out=xn_dram[row:row + 128, :],
                                     in_=vb(a_xn[bi], 2048)), i)

        nc.leave_named_scope("ph1", sc1, False)
        sc2, _ = nc.enter_named_scope("ph2", False)
        A_Q5, A_CKV4 = 32768, 44032
        HG = [(0, 0, 11), (1, 11, 11), (2, 22, 10)]

        @block.sync
        def _(sy):
            stq.wait_total(sy, 32)
            for ci in range(32):
                if ci >= 2:
                    sy.wait_ge(tpj.h, ci - 1)
                txn.inc(sy.dma_start(out=vb(2048 * (ci % 2), 2048),
                                     in_=xn_dram[:, ci * 128:(ci + 1) * 128],
                                     transpose=True), ci)

        @block.tensor
        def _(t):
            for ci in range(32):
                txn.wait_done(t, ci)
                if ci == 0:
                    t.wait_ge(c_ld.h, c_ld.n)
                for nb in range(8):
                    ins = t.matmul(PS[0:36, nb * 512:(nb + 1) * 512],
                                   s_wpack[:, ci * 36:(ci + 1) * 36],
                                   vb(2048 * (ci % 2), 2048)[:, nb * 512:(nb + 1) * 512],
                                   start=(ci == 0), stop=(ci == 31))
                tpj.inc(ins)

        @block.scalar
        def _(s):
            s.wait_ge(tpj.h, 32)
            s.wait_ge(vst.h, 34)
            for nb in range(8):
                spj.inc(s.activation(vb(4096, 2048, 0, 12)[:, nb * 512:(nb + 1) * 512],
                                     PS[0:12, nb * 512:(nb + 1) * 512], AF.Identity,
                                     bias=s_b36[0:12, :]))
                spj.inc(s.activation(vb(44032, 2048, 0, 4)[:, nb * 512:(nb + 1) * 512],
                                     PS[32:36, nb * 512:(nb + 1) * 512], AF.Identity,
                                     bias=s_b36[32:36, :]))

        @block.vector
        def _(v):
            v.wait_ge(c_ld.h, c_ld.n)
            vst.inc(v.memset(vb(44032, 2048, 0, 8), 1.0))

        @block.tensor
        def _(t):
            t.wait_ge(spj.h, 16)
            t.wait_ge(vst.h, 1)
            for nb in range(4):
                tqt.inc(t.matmul(bank(nb), s_wqk05[:], 
                                 vb(4096, 2048, 0, 12)[:, nb * 512:(nb + 1) * 512],
                                 start=True, stop=True))

        @block.scalar
        def _(s):
            for nb in range(4):
                s.wait_ge(tqt.h, nb + 1)
                sqt.inc(s.activation(vb(6144, 1024)[:, nb * 512:(nb + 1) * 512],
                                     bank(nb), AF.Identity, bias=s_bqk05[:]))
            s.wait_ge(sqt.h, 4)
            sab.inc(s.activation(vb(7168, 1024), vb(6144, 1024), AF.Abs))

        @block.tensor
        def _(t):
            t.wait_ge(sab.h, 1)
            for nb in range(4):
                tbs.inc(t.matmul(PS[0:32, 2048 + nb * 512: 2048 + (nb + 1) * 512],
                                 s_headmask[:], vb(7168, 1024)[:, nb * 512:(nb + 1) * 512],
                                 start=True, stop=True))

        @block.vector
        def _(v):
            v.wait_ge(spj.h, 16)
            vmc.inc(v.reduce_max(vf(9216, 1, 0, 4), vb(44032, 2048, 0, 4), axis=AX.X,
                                 apply_absolute_value=True))

        @block.sync
        def _(sy):
            sy.wait_ge(vmc.h, 1)
            dmc.inc(sy.dma_start(out=mx_dram, in_=vf(9216, 1, 0, 4)), 16)
            sy.wait_ge(dmc.h, 16)
            dmc.inc(sy.dma_start(out=vf(9220, 4, 0, 1), in_=mx_dram.rearrange("a b -> b a")), 16)

        @block.vector
        def _(v):
            v.wait_ge(dmc.h, 32)
            vsync(v, v.reduce_max(vf(9232, 1, 0, 1), vf(9220, 4, 0, 1), axis=AX.X))
            vmc.inc(v.tensor_scalar(vf(9236, 1, 0, 1), vf(9232, 1, 0, 1), -1.0, None,
                                    op0=ALU.mult))

        @block.sync
        def _(sy):
            sy.wait_ge(vmc.h, 2)
            dmc.inc(sy.dma_start(out=mx2_dram, in_=vf(9236, 1, 0, 1)), 16)
            sy.wait_ge(dmc.h, 48)
            dmc.inc(sy.dma_start(out=vf(9240, 1, 0, 32),
                                 in_=mx2_dram.to_broadcast([32, 1])), 16)

        @block.vector
        def _(v):
            v.wait_ge(dmc.h, 64)
            v.wait_ge(tbs.h, 4)
            for nb in range(4):
                vnb.inc(v.tensor_scalar(vb(8192, 1024, 0, 32)[:, nb * 512:(nb + 1) * 512],
                                        PS[0:32, 2048 + nb * 512:2048 + (nb + 1) * 512],
                                        vf(9240, 1, 0, 32), None, op0=ALU.mult))

        @block.sync
        def _(sy):
            sy.wait_ge(sqt.h, 4)
            dq5.inc(sy.dma_start(out=qt_dram, in_=vb(6144, 1024)), 16)
            sy.wait_ge(vnb.h, 4)
            dq5.inc(sy.dma_start(out=nb_dram, in_=vb(8192, 1024, 0, 32)), 16)
            sy.wait_ge(spj.h, 16)
            sy.wait_ge(vst.h, 34)
            dq5.inc(sy.dma_start(out=ckv_dram, in_=vb(44032, 2048, 0, 5)), 16)
            sy.wait_ge(dq5.h, 48)
            for grp, h0, nh in HG:
                p0 = 32 * grp
                dq5.inc(sy.dma_start(
                    out=vb(A_Q5, nh * 1024, p0, p0 + 4).rearrange("d (h t) -> d h t", h=nh),
                    in_=qt_dram.rearrange("(h d) t -> d h t", d=4)[:, h0:h0 + nh, :]), 16)
                dq5.inc(sy.dma_start(
                    out=vb(A_Q5, nh * 1024, p0 + 4, p0 + 5).rearrange("a (h t) -> a h t", h=nh),
                    in_=nb_dram.unsqueeze(0)[:, h0:h0 + nh, :]), 16)
                dck.inc(sy.dma_start(out=vb(A_CKV4, 2048, p0, p0 + 5), in_=ckv_dram), 16)

        def _ph3():
            A_AT, A_PB, A_ZT, A_VH = 0, 16384, 24576, 28672
            A_Z128, A_RZH, A_RZ, A_RZBF, A_WOUT = 46080, 46336, 46592, 47104, 20480
            A_WOC = 16384

            def q5slice(h, qsi_g, w=512):
                g = min(h // 11, 2)
                p0 = 32 * g
                nh = 11 if g < 2 else 10
                base = (h - 11 * g) * 2048 + qsi_g * 512
                return vb(A_Q5, nh * 1024, p0, p0 + 5)[:, base:base + w]

            def ckv4s(kb, p0):
                return vb(A_CKV4, 2048, p0, p0 + 5)[:, kb * 128:(kb + 1) * 128]

            def z_mm(t, ITz):
                t.wait_ge(v_zt.h, 2 * (ITz + 1))
                if ITz >= 1:
                    t.wait_ge(v_at.h, 2 * ITz)
                t.matmul(PS[0:1, 3072:3584], s_ones128[:],
                         vb(A_ZT, 256), start=True, stop=False)
                t_zm.inc(t.matmul(PS[0:1, 3072:3584], s_ones128[:],
                                  vb(A_ZT + 2048, 256), start=False, stop=True))

            def z_bc(t, ITz):
                t.wait_ge(v_rz.h, ITz + 1)
                t_zb.inc(t.matmul(PS[:, 3072:3584], s_ones1[:],
                                  vb(A_RZH, 256, 0, 1), start=True, stop=True))

            def attnv_half(t, H):
                ITp, hb = H // 2, H % 2
                HIp = ITp // 2
                t.wait_ge(s_ex.h, 8 * H + 8)
                t.wait_ge(v_ve.h, 8 * HIp + 4 + 4 * hb)
                if hb == 0 and ITp >= 1:
                    t.wait_ge(v_at.h, 2 * ITp)
                bc = 2048 + hb * 512
                for i in range(16):
                    kb = 16 * hb + i
                    ins = t.matmul(
                        PS[:, bc:bc + 512],
                        vb(A_VH + (HIp % 2) * 2048, 2048)[:, kb * 128:(kb + 1) * 128],
                        vb(A_PB, 8192)[:, kb * 512:(kb + 1) * 512],
                        start=(i == 0), stop=(i == 15))
                t_av.inc(ins, 1)

            @block.tensor
            def _(t):
                for qcg in range(2):
                    for h in range(32):
                        HI = qcg * 32 + h
                        for g8 in range(8):
                            if HI == 0 and g8 == 0:
                                t.wait_ge(vnb.h, 4)
                                t.wait_ge(dck.h, dck.n)
                                t.wait_ge(dq5.h, dq5.n)
                            if 8 * HI + g8 >= 1:
                                t.wait_ge(v_ve.h, 8 * HI + g8)
                            for j in range(4):
                                kb = g8 * 4 + j
                                ins = t.matmul(PS[:, 3584 + j * 128:3584 + (j + 1) * 128],
                                               ckv4s(kb, 0), s_wv5[:, h * 128:(h + 1) * 128],
                                               start=True, stop=True)
                            t_v.inc(ins, 4)
                        for qsi in range(2):
                            if ph3sub < 2:
                                continue
                            IT = qcg * 64 + h * 2 + qsi
                            QSI = qcg * 2 + qsi
                            for p in range(16):
                                PI = 16 * IT + p
                                if p == 0 and IT == 64 and ph3sub >= 4:
                                    t.wait_ge(s_wo.h, 64)
                                if PI >= 2:
                                    t.wait_ge(s_ex.h, PI - 1)
                                cb = (PI % 2) * 1024
                                t.matmul(PS[:, cb:cb + 512], ckv4s(2 * p, 32 * min(h // 11, 2)),
                                         q5slice(h, QSI), start=True, stop=True)
                                ins = t.matmul(PS[:, cb + 512:cb + 1024],
                                               ckv4s(2 * p + 1, 32 * min(h // 11, 2)),
                                               q5slice(h, QSI), start=True, stop=True)
                                t_sc.inc(ins, 2)
                                if p == 7 and ph3sub >= 3 and 2 * IT - 1 >= 2 * qcg * 64:
                                    z_mm(t, IT - 1)
                                    attnv_half(t, 2 * IT - 1)
                                if p == 10 and ph3sub >= 3 and IT - 1 >= qcg * 64:
                                    z_bc(t, IT - 1)
                            if ph3sub < 3:
                                continue
                            attnv_half(t, 2 * IT)
                    if ph3sub >= 3:
                        z_mm(t, qcg * 64 + 63)
                        attnv_half(t, 2 * (qcg * 64 + 63) + 1)
                        z_bc(t, qcg * 64 + 63)
                    if ph3sub < 4:
                        continue
                    t.wait_ge(v_at.h, 2 * 64 * (qcg + 1))
                    for dmt in range(32):
                        g_wo.wait_done(t, qcg * 32 + dmt)
                        for qc4 in range(2):
                            k = qcg * 64 + dmt * 2 + qc4
                            if k >= 2:
                                t.wait_ge(s_wo.h, k - 1)
                            for hd in range(32):
                                ins = t.matmul(
                                    PS[:, (k % 2) * 512:(k % 2) * 512 + 512],
                                    vb(A_WOC + (dmt % 2) * 2048, 2048)[:, hd * 128:(hd + 1) * 128],
                                    vb(A_AT, 16384)[:, hd * 1024 + qc4 * 512: hd * 1024 + qc4 * 512 + 512],
                                    start=(hd == 0), stop=(hd == 31))
                            t_wo.inc(ins, 1)

            def z_inv(s, ITz):
                s.wait_ge(t_zm.h, ITz + 1)
                ssync(s, s.activation(vf(A_RZBF, 512, 0, 1), PS[0:1, 3072:3584], AF.Ln))
                v_rz.inc(s.activation(vb(A_RZH, 256, 0, 1), vf(A_RZBF, 512, 0, 1),
                                      AF.Exp, scale=-1.0))

            @block.scalar
            def _(s):
                for qcg in range(2):
                    for h in range(32):
                        for qsi in range(2):
                            IT = qcg * 64 + h * 2 + qsi
                            if ph3sub < 2:
                                continue
                            for p in range(16):
                                PI = 16 * IT + p
                                s.wait_ge(t_sc.h, 2 * (PI + 1))
                                if IT >= 1 and ph3sub >= 3:
                                    s.wait_ge(t_av.h, 2 * (IT - 1) + p // 8 + 1)
                                if p % 8 == 0 and IT >= 1 and ph3sub >= 3:
                                    s.wait_ge(v_zt.h, 2 * (IT - 1) + p // 8 + 1)
                                if p == 0 and IT == 64 and ph3sub >= 4:
                                    s.wait_ge(t_wo.h, 64)
                                    s.wait_ge(s_wo.h, 64)
                                    st_wo.wait_total(s, 64)
                                cb = (PI % 2) * 1024
                                s_ex.inc(s.activation(
                                    vb(A_PB, 8192)[:, p * 1024:(p + 1) * 1024],
                                    PS[:, cb:cb + 1024], AF.Exp))
                                if p == 7 and IT - 1 >= qcg * 64 and ph3sub >= 3:
                                    z_inv(s, IT - 1)
                    if ph3sub >= 3:
                        z_inv(s, qcg * 64 + 63)
                    for dmt in range(32):
                        if ph3sub < 4:
                            continue
                        for qc4 in range(2):
                            k = qcg * 64 + dmt * 2 + qc4
                            s.wait_ge(t_wo.h, k + 1)
                            if k >= 2:
                                st_wo.wait(s, k, k // 2)
                            s_wo.inc(s.activation(
                                vb(A_WOUT + (k % 2) * 256, 256), PS[:, (k % 2) * 512:(k % 2) * 512 + 512],
                                AF.Identity, bias=s_bo[:, dmt:dmt + 1]))

            def att_evac(v, ITE):
                hE, qsE = (ITE % 64) // 2, ITE % 2
                v.wait_ge(t_zb.h, ITE + 1)
                v_at.inc(v.engine_nop())
                v.wait_ge(t_av.h, 2 * (ITE + 1))
                if ITE == 64 and ph3sub >= 4:
                    v.wait_ge(t_wo.h, 64)
                vsync(v, v.tensor_copy(vb(A_RZ, 256), PS[:, 2048:2560]))
                vsync(v, v.tensor_tensor(vb(A_RZ + 256, 256), PS[:, 2560:3072],
                                         vb(A_RZ, 256), op=ALU.add))
                v_at.inc(v.tensor_tensor(
                    vb(A_AT, 16384)[:, hE * 1024 + qsE * 512: hE * 1024 + qsE * 512 + 512],
                    PS[:, 3072:3584], vb(A_RZ + 256, 256), op=ALU.mult))

            @block.vector
            def _(v):
                for qcg in range(2):
                    for h in range(32):
                        HI = qcg * 32 + h
                        for g8 in range(8):
                            v.wait_ge(t_v.h, 32 * HI + 4 * (g8 + 1))
                            if g8 == 0 and HI >= 2 and ph3sub >= 3:
                                v.wait_ge(t_av.h, 2 * (2 * HI - 2))
                            v_ve.inc(v.tensor_copy(
                                vb(A_VH + (HI % 2) * 2048, 2048)[:, g8 * 512:(g8 + 1) * 512],
                                PS[:, 3584:4096]))
                        for qsi in range(2):
                            if ph3sub < 3:
                                continue
                            IT = qcg * 64 + h * 2 + qsi
                            zt = vb(A_ZT, 4096)
                            pb = vb(A_PB, 8192)
                            v.wait_ge(s_ex.h, 16 * IT + 8)
                            v.wait_ge(t_zm.h, IT)
                            vsync(v, v.tensor_tensor(zt[:, 0:4096], pb[:, 0:4096],
                                                     pb[:, 4096:8192], op=ALU.add))
                            vsync(v, v.tensor_tensor(zt[:, 0:2048], zt[:, 0:2048],
                                                     zt[:, 2048:4096], op=ALU.add))
                            vsync(v, v.tensor_tensor(zt[:, 0:1024], zt[:, 0:1024],
                                                     zt[:, 1024:2048], op=ALU.add))
                            v_zt.inc(v.tensor_tensor(zt[:, 0:512], zt[:, 0:512],
                                                     zt[:, 512:1024], op=ALU.add))
                            if IT - 1 >= qcg * 64:
                                att_evac(v, IT - 1)
                                v.wait_ge(v_at.h, v_at.n)
                            v.wait_ge(s_ex.h, 16 * (IT + 1))
                            vsync(v, v.tensor_tensor(zt[:, 4096:8192], pb[:, 8192:12288],
                                                     pb[:, 12288:16384], op=ALU.add))
                            vsync(v, v.tensor_tensor(zt[:, 4096:6144], zt[:, 4096:6144],
                                                     zt[:, 6144:8192], op=ALU.add))
                            vsync(v, v.tensor_tensor(zt[:, 4096:5120], zt[:, 4096:5120],
                                                     zt[:, 5120:6144], op=ALU.add))
                            v_zt.inc(v.tensor_tensor(zt[:, 4096:4608], zt[:, 4096:4608],
                                                     zt[:, 4608:5120], op=ALU.add))
                    if ph3sub >= 3:
                        att_evac(v, qcg * 64 + 63)


            @block.sync
            def _(sy):
                for qcg in range(2):
                    for dmt in range(32):
                        if ph3sub < 4:
                            continue
                        for qc4 in range(2):
                            k = qcg * 64 + dmt * 2 + qc4
                            sy.wait_ge(s_wo.h, k + 1)
                            st_wo.inc(sy.dma_start(
                                out=aot_dram[dmt * 128:(dmt + 1) * 128,
                                             qcg * 1024 + qc4 * 512: qcg * 1024 + qc4 * 512 + 512],
                                in_=vb(A_WOUT + (k % 2) * 256, 256)), k)

            @block.gpsimd
            def _(g):
                for qcg in range(2):
                    for dmt in range(32):
                        if ph3sub < 4:
                            continue
                        if dmt < 2:
                            g.wait_ge(t_av.h, 2 * 64 * (qcg + 1))
                            g.wait_ge(v_zt.h, 128 * (qcg + 1))
                        else:
                            g.wait_ge(t_wo.h, qcg * 64 + 2 * dmt - 2)
                        g_wo.inc(g.dma_start(
                            out=vb(A_WOC + (dmt % 2) * 2048, 2048).rearrange("p (hd c) -> p hd c", c=128),
                            in_=d_wo.rearrange("(hd p) c -> p hd c", p=128)[:, :, dmt * 128:(dmt + 1) * 128]),
                            qcg * 32 + dmt)

        nc.leave_named_scope("ph2", sc2, False)

        @block.gpsimd
        def _(g):
            g.dma_start(out=barr_bf, in_=d_barr).then_inc(c_ld.h, 16)
        c_ld.n += 16

        sc3, _ = nc.enter_named_scope("ph3", False)
        if phases >= 3:
            _ph3()
        nc.leave_named_scope("ph3", sc3, False)

        M_AB, M_HGT, M_GX = 0, 16384, 24576
        M_XT1, M_XT2 = 26624, 27648
        M_MX = [28672, 32768]
        M_SQS = 36864
        M_MAT = [40960, 43008]
        M_SM = 45056
        G_SCR, G_SCR2, G_M8 = M_SM, M_SM + 64, M_SM + 128
        G_NEGM, G_ZD, G_Z16, G_RZ, G_RZ2 = M_SM + 144, M_SM + 145, M_SM + 146, M_SM + 147, M_SM + 148
        G_MASK = M_SM + 152
        G_ME, G_E16O = M_SM + 184, M_SM + 248
        G_G, G_GT32, G_GTB = M_SM + 264, M_SM + 328, M_SM + 456
        GT4 = 46336
        M_MO = [45824, 46336]

        def msm(bi, j):
            return vf(46848 + bi * 16 + j, 1)

        def moe_call(call, d_src, d_dst, d_res):
            KB = call * 16
            RB = call * 128
            HB = call * 128
            GB = call * 16
            XB = call * 32

            @block.gpsimd
            def _(g):
                if call == 0:
                    g.wait_ge(t_wo.h, 128)
                else:
                    g.wait_ge(fs_x3.h, 256)
                    g.wait_ge(fv_x3.h, 16)
                    g.wait_ge(ft_B.h, 1024)
                    g.wait_ge(fv_ev.h, 1024)
                    g.wait_ge(fv_g.h, 48)
                    g.wait_ge(fv_pe.h, 256)
                    g.wait_ge(fv_gw.h, 256)
                    g.wait_ge(fv_he.h, 256)
                mg_aA.inc(g.dma_start(
                    out=vb(M_AB, 16384).rearrange("p (c e) -> p c e", e=1024),
                    in_=d_at.rearrange("(c p) e -> p c e", p=128)), 16)
                for k in range(16):
                    kk = KB + k
                    if k == 0 and call == 0:
                        g.wait_ge(t_av.h, 256)
                        g.wait_ge(t_sc.h, 4096)
                    if k >= 2:
                        g.wait_ge(mv_1.h, 3 * (kk - 2) + 3)
                        g.wait_ge(mc_s.h, 2 * (kk - 2) + 1)
                        if call == 0:
                            ms_x1.wait(g, k, k // 2)
                    if call == 1:
                        g.wait_ge(fs_x3.h, 16 * (k + 1))
                        if k == 0:
                            g.wait_ge(fs_xn.h, 256)
                            g.wait_ge(fs_g.h, 256)
                    row = k * 128
                    mg_x.inc(g.dma_start(out=vf(M_MX[k % 2], 4096),
                                         in_=d_src[row:row + 128, :]), kk)
                g.wait_ge(tm_h.h, HB + 128)
                g.wait_ge(c_ld.h, 272)
                mg_aB.inc(g.dma_start(
                    out=vb(M_AB, 16384).rearrange("p (e d) -> p e d", d=4096),
                    in_=barr_bf.rearrange("(e p) d -> p e d", p=128)), 16)


            @block.sync
            def _(sy):
                if call == 0:
                    st_wo.wait_total(sy, 128)
                    sy.wait_ge(t_sc.h, 4096)
                    sy.wait_ge(t_v.h, 2048)
                for k in range(16):
                    kk = KB + k
                    if call == 0:
                        if k >= 2:
                            sy.wait_ge(mv_1.h, 3 * (k - 2) + 3)
                            ms_xn.wait(sy, k, k // 2)
                        ms_t.inc(sy.dma_start(out=vb(M_MAT[k % 2], 2048),
                                              in_=aot_dram[:, k * 128:(k + 1) * 128],
                                              transpose=True), k)
                        sy.wait_ge(mv_1.h, 3 * k + 1)
                        ms_x1.inc(sy.dma_start(out=x1_dram[k * 128:(k + 1) * 128, :],
                                               in_=vf(M_MX[k % 2], 4096)), k)
                    sy.wait_ge(mv_1.h, 3 * kk + 3)
                    ms_xn.inc(sy.dma_start(out=xnm_dram[k * 128:(k + 1) * 128, :],
                                           in_=vb(M_MAT[k % 2], 2048)), kk)
                for mc in range(4):
                    ms_xn.wait_total(sy, KB + mc * 4 + 4)
                    for ci in range(32):
                        R = RB + mc * 32 + ci
                        if call == 0 and mc == 0 and ci == 0:
                            sy.wait_ge(v_zt.h, 256)
                        if R >= 4:
                            sy.wait_ge(tm_r.h, R - 3)
                        ms_T1.inc(sy.dma_start(out=vb(M_XT1 + (ci % 4) * 256, 256),
                                               in_=xnm_dram[mc * 512:(mc + 1) * 512,
                                                            ci * 128:(ci + 1) * 128],
                                               transpose=True), R)
                for k in range(16):
                    for dc in range(8):
                        kk = (KB + k) * 8 + dc
                        sy.wait_ge(mv_oe.h, kk + 1)
                        row = k * 128
                        ms_o.inc(sy.dma_start(out=d_dst[row:row + 128, dc * 512:(dc + 1) * 512],
                                              in_=vf(M_MO[kk % 2], 512)), kk)

            @block.vector
            def _(v):
                for k in range(16):
                    kk = KB + k
                    bi = k % 2
                    mg_x.wait_done(v, kk)
                    if k == 0 and call == 0:
                        v.wait_ge(t_sc.h, 4096)
                        v.wait_ge(t_v.h, 2048)
                    if call == 0:
                        ms_t.wait_done(v, k)
                        vsync(v, v.tensor_tensor(vf(M_MX[bi], 4096), vf(M_MX[bi], 4096),
                                                 vb(M_MAT[bi], 2048), op=ALU.add))
                    vsync(v, v.reduce_sum(msm(bi, 0), vf(M_MX[bi], 4096), axis=AX.X))
                    mv_1.inc(v.tensor_scalar(msm(bi, 1), msm(bi, 0), -1.0 / D, None,
                                             op0=ALU.mult))
                    vsync(v, v.engine_nop())
                    v.wait_ge(mc_s.h, 2 * kk + 1)
                    vsync(v, v.tensor_tensor(msm(bi, 2), msm(bi, 1), msm(bi, 1),
                                             op=ALU.mult))
                    vsync(v, v.tensor_scalar(msm(bi, 3), msm(bi, 4), 1.0 / D, None,
                                             op0=ALU.mult))
                    mv_1.inc(v.tensor_tensor(msm(bi, 5), msm(bi, 3), msm(bi, 2),
                                             op=ALU.subtract))
                    v.wait_ge(mc_s.h, 2 * kk + 2)
                    vsync(v, v.tensor_tensor(msm(bi, 6), msm(bi, 1), msm(bi, 7),
                                             op=ALU.mult))
                    if kk >= 2:
                        ms_xn.wait(v, kk, kk // 2)
                    mv_1.inc(v.tensor_scalar(vb(M_MAT[bi], 2048), vf(M_MX[bi], 4096),
                                             msm(bi, 7), msm(bi, 6),
                                             op0=ALU.mult, op1=ALU.add))
                def hg_evac(v, hp):
                    v.wait_ge(tm_h.h, HB + (hp + 1) * 32)
                    for er in range(8):
                        mv_hg.inc(v.tensor_tensor(
                            vb(M_HGT, 8192)[:, hp * 4096 + er * 512:hp * 4096 + (er + 1) * 512],
                            PS[:, er * 512:(er + 1) * 512],
                            vb(M_GX, 2048)[:, er * 512:(er + 1) * 512], op=ALU.mult))

                for mc in range(4):
                    v.wait_ge(tm_r.h, RB + (mc + 1) * 32)
                    if call == 1 and mc == 0:
                        ms_o.wait_total(v, 128)
                    for tt in range(4):
                        ins = v.tensor_copy(vf(GT4 + tt * 64, 64),
                                            PS[:, tt * 512:tt * 512 + 64])
                        if tt < 3:
                            vsync(v, ins)
                    mv_g.inc(ins)
                    for tt in range(4):
                        Gi = GB + mc * 4 + tt
                        scr = vf(GT4 + tt * 64, 64)
                        vsync(v, v.max(out=vf(G_M8, 8), in_=scr))
                        vsync(v, v.match_replace(out=vf(G_SCR2, 64), in_to_replace=vf(G_M8, 8),
                                                 in_values=scr, imm_value=NEG))
                        vsync(v, v.max(out=vf(G_M8 + 8, 8), in_=vf(G_SCR2, 64)))
                        mv_g.inc(v.tensor_scalar(vf(G_NEGM, 1), vf(G_M8, 1), -1.0,
                                                 None, op0=ALU.mult))
                        v.wait_ge(mc_e.h, Gi + 1)
                        vsync(v, v.reciprocal(vf(G_RZ, 1), vf(G_Z16, 1)))
                        vsync(v, v.tensor_scalar(vf(G_RZ2, 1), vf(G_RZ, 1), SCALING, None,
                                                 op0=ALU.mult))
                        vsync(v, v.tensor_scalar(vb(G_MASK, 32), scr,
                                                 vf(G_M8 + 15, 1), None, op0=ALU.is_ge))
                        vsync(v, v.scalar_tensor_tensor(vf(G_G, 64), vf(G_ME, 64),
                                                        vf(G_RZ2, 1), vb(G_MASK, 32),
                                                        op0=ALU.mult, op1=ALU.mult))
                        for bi4 in range(4):
                            for bj in range(2):
                                ins = v.transpose(
                                    vf(G_GT32, 128, 32 * bj, 32 * bj + 32)[:, 32 * bi4:32 * bi4 + 32],
                                    vf(G_G, 64, 32 * bi4, 32 * bi4 + 32)[:, 32 * bj:32 * bj + 32])
                        vsync(v, ins)
                        mv_g.inc(v.tensor_copy(
                            vb(G_GTB, 256, 0, 64)[:, tt * 128:(tt + 1) * 128],
                            vf(G_GT32, 128, 0, 64)))
                    if mc >= 1:
                        hg_evac(v, mc - 1)
                    for er in range(8):
                        gx = XB + mc * 8 + er
                        v.wait_ge(tm_g.h, gx + 1)
                        mv_gx.inc(v.tensor_copy(vb(M_GX, 2048)[:, er * 512:(er + 1) * 512],
                                                PS[:, 4 * 512:4 * 512 + 512]))
                hg_evac(v, 3)
                for k in range(16):
                    for dc in range(8):
                        kk = (KB + k) * 8 + dc
                        v.wait_ge(tm_b.h, kk + 1)
                        mg_r.wait_done(v, KB + k)
                        if kk >= 2:
                            ms_o.wait(v, kk, kk // 2)
                        mv_oe.inc(v.tensor_tensor(
                            vf(M_MO[kk % 2], 512),
                            PS[:, (kk % 2) * 512:(kk % 2) * 512 + 512],
                            vf(M_MX[k % 2], 4096)[:, dc * 512:(dc + 1) * 512], op=ALU.add))

            @block.scalar
            def _(s):
                for k in range(16):
                    kk = KB + k
                    bi = k % 2
                    s.wait_ge(mv_1.h, 3 * kk + 1)
                    ssync(s, s.memzero(msm(bi, 4)))
                    mc_s.inc(s.activation(vf(M_SQS, 4096), vf(M_MX[bi], 4096),
                                          AF.Square, accum_out=msm(bi, 4)))
                    s.wait_ge(mv_1.h, 3 * kk + 2)
                    ssync(s, s.activation(msm(bi, 8), msm(bi, 5), AF.Ln, bias=s_eps[:]))
                    mc_s.inc(s.activation(msm(bi, 7), msm(bi, 8), AF.Exp, scale=-0.5))
                def t2_loads(hp):
                    for ci in range(32):
                        H = HB + hp * 32 + ci
                        if H >= 4:
                            s.wait_ge(tm_h.h, H - 3)
                        if call == 0 and hp == 0 and ci == 0:
                            s.wait_ge(v_zt.h, 256)
                        ms_T2.inc(s.dma_start(out=vb(M_XT2 + (ci % 4) * 256, 256),
                                              in_=xnm_dram[hp * 512:(hp + 1) * 512,
                                                           ci * 128:(ci + 1) * 128],
                                              transpose=True), H)

                for mc in range(4):
                    for tt in range(4):
                        Gi = GB + mc * 4 + tt
                        s.wait_ge(mv_g.h, call * 36 + mc * 9 + 2 * tt + 2)
                        s.memzero(vf(G_ZD, 1))
                        ssync(s, s.memzero(vf(G_Z16, 1)))
                        s.activation(vf(G_ME, 64), vf(GT4 + tt * 64, 64), AF.Exp,
                                     bias=vf(G_NEGM, 1), accum_out=vf(G_ZD, 1))
                        mc_e.inc(s.activation(vf(G_E16O, 16), vf(G_M8, 16), AF.Exp,
                                              bias=vf(G_NEGM, 1), accum_out=vf(G_Z16, 1)))
                    if mc >= 1:
                        t2_loads(mc - 1)
                t2_loads(3)
                for k in range(16):
                    kk = KB + k
                    if k >= 2:
                        s.wait_ge(mv_oe.h, call * 128 + (k - 2) * 8 + 8)
                        s.wait_ge(mc_s.h, 2 * KB + 32)
                    else:
                        if call == 0:
                            ms_x1.wait_total(s, 16)
                    if call == 1:
                        s.wait_ge(fs_x3.h, 16 * (k + 1))
                    row = k * 128
                    mg_r.inc(s.dma_start(out=vf(M_MX[k % 2], 4096),
                                         in_=d_res[row:row + 128, :]), kk)

            @block.tensor
            def _(t):
                def hpass(t, hp):
                    for ci in range(32):
                        H = HB + hp * 32 + ci
                        ms_T2.wait_done(t, H)
                        if ci == 0:
                            t.wait_ge(mg_aA.h, 16 * (call + 1))
                            t.wait_ge(mv_gx.h, XB + hp * 8 + 8)
                            if hp <= 2:
                                t.wait_ge(mv_g.h, call * 36 + (hp + 1) * 9 + 1)
                            if call == 0 and hp == 0:
                                t.wait_ge(v_rz.h, 128)
                                t.wait_ge(v_at.h, 256)
                                t.wait_ge(v_ve.h, 512)
                            if hp > 0 or call == 1:
                                t.wait_ge(mv_hg.h, call * 32 + hp * 8)
                        for er in range(8):
                            ins = t.matmul(PS[:, er * 512:(er + 1) * 512],
                                           vb(M_AB, 16384)
                                           [:, ci * 1024 + er * 128:ci * 1024 + (er + 1) * 128],
                                           vb(M_XT2 + (ci % 4) * 256, 256),
                                           start=(ci == 0), stop=(ci == 31))
                        tm_h.inc(ins)

                for mc in range(4):
                    for ci in range(32):
                        R = RB + mc * 32 + ci
                        ms_T1.wait_done(t, R)
                        if ci == 0 and mc >= 1:
                            t.wait_ge(mv_hg.h, call * 32 + (mc - 1) * 8)
                        if R == 0:
                            t.wait_ge(s_wo.h, 128)
                            t.wait_ge(s_ex.h, 2048)
                        if call == 1 and R == RB:
                            t.wait_ge(mv_oe.h, 128)
                            t.wait_ge(fv_ev.h, fv_ev.n)
                            t.wait_ge(fse_A.h, fse_A.n)
                            t.wait_ge(fve_A.h, fve_A.n)
                            t.wait_ge(fv_pe.h, fv_pe.n)
                            t.wait_ge(fv_gw.h, fv_gw.n)
                            t.wait_ge(fv_he.h, fv_he.n)
                        for tt in range(4):
                            ins = t.matmul(PS[:, tt * 512:tt * 512 + 64],
                                           vb(M_XT1 + (ci % 4) * 256, 256)
                                           [:, tt * 128:(tt + 1) * 128],
                                           s_wrt[:, ci * 64:(ci + 1) * 64],
                                           start=(ci == 0), stop=(ci == 31))
                        tm_r.inc(ins)
                    if mc >= 1:
                        hpass(t, mc - 1)
                    for er in range(8):
                        gx = XB + mc * 8 + er
                        t.wait_ge(mv_g.h, call * 36 + mc * 9 + 9)
                        if er == 0 and mc >= 1:
                            t.wait_ge(mv_hg.h, call * 32 + (mc - 1) * 8 + 5)
                        if call == 0 and mc == 0 and er == 0:
                            t.wait_ge(v_at.h, 256)
                        if gx >= 1:
                            t.wait_ge(mv_gx.h, gx)
                        tm_g.inc(t.matmul(PS[:, 4 * 512:4 * 512 + 512],
                                          s_efull[:, er * 128:(er + 1) * 128],
                                          vb(G_GTB, 256, 0, 64), start=True, stop=True))
                hpass(t, 3)
                for k in range(16):
                    mc, tt = k // 4, k % 4
                    for dc in range(8):
                        kk = (KB + k) * 8 + dc
                        t.wait_ge(mg_aB.h, 16 * (call + 1))
                        if k == 0 and dc == 0:
                            t.wait_ge(mv_hg.h, call * 32 + 32)
                        if kk >= 2:
                            t.wait_ge(mv_oe.h, kk - 1)
                        for er in range(8):
                            ins = t.matmul(
                                PS[:, (kk % 2) * 512:(kk % 2) * 512 + 512],
                                vb(M_HGT, 8192)[:, mc * 4096 + er * 512 + tt * 128:
                                                mc * 4096 + er * 512 + (tt + 1) * 128],
                                vb(M_AB, 16384)[:, er * 4096 + dc * 512:er * 4096 + (dc + 1) * 512],
                                start=(er == 0), stop=(er == 7))
                        tm_b.inc(ins)

        sc4, _ = nc.enter_named_scope("ph4", False)
        if phases >= 4:
            moe_call(0, d_xq, x2_dram, x1_dram)
        nc.leave_named_scope("ph4", sc4, False)

        if phases <= 4:
            @block.gpsimd
            def _(g):
                if phases == 4:
                    ms_o.wait_total(g, 128)
                elif phases == 3:
                    if ph3sub >= 4:
                        st_wo.wait_total(g, 128)
                    elif ph3sub == 3:
                        g.wait_ge(v_at.h, 256)
                    elif ph3sub == 2:
                        g.wait_ge(s_ex.h, 2048)
                    else:
                        g.wait_ge(v_ve.h, 512)
                elif phases == 2:
                    g.wait_ge(dq5.h, dq5.n)
                else:
                    stq.wait_total(g, 32)
            return nc
        F_X2 = [0, 4096]
        F_V, F_ST = 8192, 12288
        F_W1, F_W2, F_W3, F_W4 = 20480, 24576, 28672, 32768
        F_XN, F_SM, F_SLAB = 36864, 38912, 39424
        F_E, F_MASK, F_G = 30720, 24576, 36864

        def fsm5(j):
            return vf(F_SM + j, 1)

        NCH = OWN // FCH
        fse_A, fve_A = sqt, vmc
        FSE0, FVE0, FCS0 = fse_A.n, fve_A.n, 0
        s_slabF = vb(F_SLAB, 4096, 0, 64)
        s_slabF2 = vb(F_SLAB + 4096, 4096, 0, 64)
        st4 = vb(F_ST, 8192, 0, 64).rearrange("y (r c j) -> y r c j", r=128, c=2, j=64)
        P4 = vb(F_W1, 4096, 0, 64).rearrange("k (j r) -> k j r", r=128)
        H4 = vb(F_W4, 4096).rearrange("p (j r) -> p j r", r=128)

        sc5, _ = nc.enter_named_scope("ph5", False)
        @block.tensor
        def _(t):
            for fc in range(NCH):
                for tf in range(4):
                    T = 4 * fc + tf
                    bmap = [0, 1, 6, 7] if tf < 3 else [0, 1, 2, 3]
                    for g4 in range(32):
                        if T == 0 and g4 == 0:
                            t.wait_ge(mv_oe.h, 128)
                        if g4 == 0:
                            if tf == 0:
                                t.wait_ge(fs_D.h, 16 * (fc + 1))
                            elif tf == 1:
                                t.wait_ge(fv_pe.h, 16 * (fc + 1))
                            elif tf == 2:
                                t.wait_ge(fs_gD.h, 16 * (fc + 1))
                            else:
                                t.wait_ge(fv_he.h, 16 * (fc + 1))
                            if T >= 1:
                                t.wait_ge(fse_A.h, FSE0 + 8 * T)
                                t.wait_ge(fve_A.h, FVE0 + 8 * T)
                        if g4 == 2:
                            if tf == 0 and fc >= 1:
                                t.wait_ge(fv_he.h, 16 * fc)
                            if tf == 3:
                                t.wait_ge(fv_ev.h, 16 * (4 * fc + 3))
                        if g4 >= 4:
                            q2 = g4 // 2 - 2
                            if q2 % 2 == 0:
                                t.wait_ge(fse_A.h, FSE0 + 8 * T + q2 // 2 + 1)
                            else:
                                t.wait_ge(fve_A.h, FVE0 + 8 * T + q2 // 2 + 1)
                        bk = bmap[g4 % 4]
                        for j in range(4):
                            r = g4 * 4 + j
                            ob = PS[0:64, bk * 512 + j * 128:bk * 512 + (j + 1) * 128]
                            if tf == 0:
                                ins = t.matmul(ob, vb(F_W4, 4096, 0, 64)[:, r * 64:(r + 1) * 64],
                                               s_fa[:], start=True, stop=True)
                            elif tf == 2:
                                ins = t.matmul(ob, vb(F_W1, 4096, 0, 64)[:, r * 64:(r + 1) * 64],
                                               s_fa[:], start=True, stop=True)
                            elif tf == 1:
                                ins = t.matmul(ob, P4[:, :, r], s_fa[:], start=True, stop=True)
                            else:
                                ins = t.matmul(ob, H4[:, :, r], s_fha[:], start=True, stop=True)
                        ft_A.inc(ins)
                    for jg in range(16):
                        GP = 64 * fc + 16 * tf + jg
                        if jg == 0:
                            t.wait_ge(fse_A.h, FSE0 + 8 * (T + 1))
                            t.wait_ge(fve_A.h, FVE0 + 8 * (T + 1))
                            if T == 0:
                                t.wait_ge(fs_cs.h, FCS0 + 16)
                            if tf == 3:
                                t.wait_ge(fv_gw.h, 16 * (fc + 1))
                        if GP >= 2:
                            t.wait_ge(fv_ev.h, GP - 1)
                        bb = (2 + jg % 2) if tf < 3 else (4 + jg % 2)
                        for q in range(4):
                            j1 = 4 * jg + q
                            if tf in (0, 2):
                                ob = PS[0:128, bb * 512 + q * 128:bb * 512 + (q + 1) * 128]
                                t.matmul(ob, s_slabF[:, j1 * 128:(j1 + 1) * 128],
                                         st4[:, :, 0, j1], start=True, stop=False)
                                ins = t.matmul(ob, s_slabF2[:, j1 * 128:(j1 + 1) * 128],
                                               st4[:, :, 1, j1], start=False, stop=True)
                            else:
                                ob = PS[0:64, bb * 512 + q * 128:bb * 512 + (q + 1) * 128]
                                t.matmul(ob, s_slabF[:, j1 * 128:j1 * 128 + 64],
                                         st4[:, :, 0, j1], start=True, stop=False)
                                ins = t.matmul(ob, s_slabF[:, j1 * 128 + 64:(j1 + 1) * 128],
                                               st4[:, :, 1, j1], start=False, stop=True)
                        ft_B.inc(ins)
                    if tf == 0:
                        for i in range(16):
                            if i == 0:
                                t.wait_ge(fc_s.h, 3 * fc + 1)
                                t.wait_ge(fv_ev.h, 64 * fc)
                                if fc == 0:
                                    t.wait_ge(mv_gx.h, 32)
                                    t.wait_ge(mv_hg.h, 32)
                            if 16 * fc + i >= 2:
                                t.wait_ge(fv_pe.h, 16 * fc + i - 1)
                            ft_P.inc(t.matmul(
                                PS[0:64, 2048 + (i % 2) * 512:2048 + (i % 2) * 512 + 512],
                                s_lh1[:, 0:64], vb(F_W2, 4096)[:, i * 512:(i + 1) * 512],
                                start=True, stop=True))
                    if tf == 2:
                        for i in range(16):
                            t.wait_ge(fv_ev.h, 64 * fc + 32 + i + 1)
                            if 16 * fc + i >= 2:
                                t.wait_ge(fv_gw.h, 16 * fc + i - 1)
                            ft_G.inc(t.matmul(
                                PS[:, 2048 + (i % 2) * 512:2048 + (i % 2) * 512 + 512],
                                s_sw[:], vb(F_W2, 4096)[:, i * 512:(i + 1) * 512],
                                start=True, stop=True))
                        for i in range(16):
                            if i == 0:
                                t.wait_ge(fv_g.h, 3 * fc + 3)
                            if 16 * fc + i >= 2:
                                t.wait_ge(fv_he.h, 16 * fc + i - 1)
                            pb = PS[:, 3072 + (i % 2) * 512:3072 + (i % 2) * 512 + 512]
                            t.matmul(pb, s_lh1[:], vb(F_W1, 4096)[:, i * 512:(i + 1) * 512],
                                     start=True, stop=False)
                            ft_H.inc(t.matmul(pb, s_lh2[:], vb(F_W3, 4096)[:, i * 512:(i + 1) * 512],
                                              start=False, stop=True))

        def ph5_ln(v, c):
            fg_x.wait_done(v, c)
            fx2 = vf(F_X2[c % 2], 4096)
            vsync(v, v.reduce_sum(fsm5(0), fx2, axis=AX.X))
            fv_ln.inc(v.tensor_scalar(fsm5(1), fsm5(0), -1.0 / D, None, op0=ALU.mult))
            vsync(v, v.engine_nop())
            v.wait_ge(fc_ln.h, 2 * c + 1)
            vsync(v, v.tensor_tensor(fsm5(48), fsm5(40), fsm5(41), op=ALU.add))
            vsync(v, v.tensor_tensor(fsm5(49), fsm5(42), fsm5(43), op=ALU.add))
            vsync(v, v.tensor_tensor(fsm5(50), fsm5(44), fsm5(45), op=ALU.add))
            vsync(v, v.tensor_tensor(fsm5(51), fsm5(46), fsm5(47), op=ALU.add))
            vsync(v, v.tensor_tensor(fsm5(48), fsm5(48), fsm5(49), op=ALU.add))
            vsync(v, v.tensor_tensor(fsm5(50), fsm5(50), fsm5(51), op=ALU.add))
            vsync(v, v.tensor_tensor(fsm5(4), fsm5(48), fsm5(50), op=ALU.add))
            vsync(v, v.tensor_tensor(fsm5(2), fsm5(1), fsm5(1), op=ALU.mult))
            vsync(v, v.tensor_scalar(fsm5(3), fsm5(4), 1.0 / D, None, op0=ALU.mult))
            fv_ln.inc(v.tensor_tensor(fsm5(5), fsm5(3), fsm5(2), op=ALU.subtract))
            v.wait_ge(fc_ln.h, 2 * c + 2)
            vsync(v, v.tensor_tensor(fsm5(6), fsm5(1), fsm5(7), op=ALU.mult))
            if c >= 1:
                v.wait_ge(fs_xn.h, 16 * c)
                v.wait_ge(fs_g.h, 16 * c)
            fv_ln.inc(v.tensor_scalar(vb(F_XN, 2048), fx2, fsm5(7), fsm5(6),
                                      op0=ALU.mult, op1=ALU.add))

        @block.vector
        def _(v):
            for fc in range(NCH):
                if fc == 0:
                    v.wait_ge(mv_oe.h, 128)
                    ph5_ln(v, 0)
                for tf in range(4):
                    T = 4 * fc + tf
                    for q in range(1, 16, 2):
                        v.wait_ge(ft_A.h, 32 * T + 2 * q + 2)
                        if q == 1:
                            v.wait_ge(ft_B.h, 16 * T)
                            if tf == 3:
                                v.wait_ge(fv_g.h, 3 * fc + 3)
                        base = 3072 if tf < 3 else 1024
                        fve_A.inc(v.tensor_copy(
                            vb(F_ST, 8192, 0, 64)[:, q * 1024:(q + 1) * 1024],
                            PS[0:64, base:base + 1024]))
                    for jg in range(16):
                        GP = 64 * fc + 16 * tf + jg
                        v.wait_ge(ft_B.h, GP + 1)
                        bb = (2 + jg % 2) if tf < 3 else (4 + jg % 2)
                        if tf == 0:
                            fv_ev.inc(v.tensor_copy(
                                vb(F_V, 4096)[:, jg * 512:(jg + 1) * 512],
                                PS[:, bb * 512:bb * 512 + 512]))
                        elif tf == 2:
                            if jg == 0:
                                v.wait_ge(fv_g.h, 3 * fc + 2)
                                v.wait_ge(fs_c.h, 16 * (fc + 1))
                            fv_ev.inc(v.tensor_copy(
                                vb(F_W2, 4096)[:, jg * 512:(jg + 1) * 512],
                                PS[:, bb * 512:bb * 512 + 512]))
                        else:
                            wreg = F_W2 if tf == 1 else F_W1
                            dst = vb(wreg, 4096, 0, 64).rearrange(
                                "p (r j) -> p r j", j=64)[:, :, 4 * jg:4 * jg + 4]
                            srcp = PS[0:64, bb * 512:bb * 512 + 512].rearrange(
                                "p (j r) -> p r j", j=4)
                            fv_ev.inc(v.tensor_copy(dst, srcp))
                    if tf == 0:
                        if fc >= 1:
                            v.wait_ge(fs_or.h, 16 * fc)
                            fv_x3.inc(v.tensor_tensor(vf(F_W3, 4096), vf(F_X2[(fc - 1) % 2], 4096),
                                                      vb(F_W2, 2048), op=ALU.add))
                        for i in range(16):
                            v.wait_ge(ft_P.h, 16 * fc + i + 1)
                            if i == 0 and fc >= 1:
                                v.wait_ge(fs_o.h, 16 * fc)
                            fv_pe.inc(v.tensor_copy(
                                vb(F_W1, 4096, 0, 64)[:, i * 512:(i + 1) * 512],
                                PS[0:64, 2048 + (i % 2) * 512:2048 + (i % 2) * 512 + 512]))
                    elif tf == 1:
                        v.wait_ge(fs_cr.h, 16 * (fc + 1))
                        vsync(v, v.max(out=vb(F_SM + 16, 4), in_=vb(F_W3, 2048)))
                        fv_g.inc(v.tensor_scalar(fsm5(24), vb(F_SM + 16, 4)[:, 0:1], -1.0,
                                                 None, op0=ALU.mult))
                        v.wait_ge(fc_s.h, 3 * fc + 3)
                        vsync(v, v.reciprocal(fsm5(25), fsm5(26)))
                        vsync(v, v.tensor_copy(fsm5(27), vb(F_SM + 16, 4)[:, 7:8]))
                        vsync(v, v.tensor_scalar(vb(F_MASK, 2048), vb(F_W3, 2048),
                                                 fsm5(27), None, op0=ALU.is_ge))
                        v.wait_ge(fs_xn.h, 16 * (fc + 1))
                        fv_g.inc(v.scalar_tensor_tensor(vb(F_G, 2048), vb(F_E, 2048),
                                                        fsm5(25), vb(F_MASK, 2048),
                                                        op0=ALU.mult, op1=ALU.mult))
                    elif tf == 2:
                        for i in range(16):
                            v.wait_ge(ft_G.h, 16 * fc + i + 1)
                            fv_gw.inc(v.tensor_copy(
                                vb(F_ST, 4096)[:, i * 512:(i + 1) * 512],
                                PS[:, 2048 + (i % 2) * 512:2048 + (i % 2) * 512 + 512]))
                        vsync(v, v.engine_nop())
                        vsync(v, v.tensor_tensor(vb(F_W1, 4096), vb(F_V, 4096),
                                                 vb(F_W2, 4096), op=ALU.mult))
                        fv_g.inc(v.tensor_tensor(vb(F_W3, 4096), vb(F_V, 4096),
                                                 vb(F_ST, 4096), op=ALU.mult))
                        for i in range(16):
                            v.wait_ge(ft_H.h, 16 * fc + i + 1)
                            fv_he.inc(v.tensor_copy(
                                vb(F_W4, 4096)[:, i * 512:(i + 1) * 512],
                                PS[:, 3072 + (i % 2) * 512:3072 + (i % 2) * 512 + 512]))
                        if fc + 1 < NCH:
                            ph5_ln(v, fc + 1)

            v.wait_ge(fs_or.h, 16 * NCH)
            fv_x3.inc(v.tensor_tensor(vf(F_W3, 4096), vf(F_X2[(NCH - 1) % 2], 4096),
                                      vb(F_W2, 2048), op=ALU.add))

        def ph5_lns(s, c):
            s.wait_ge(fv_ln.h, 3 * c + 1)
            for i in range(8):
                ssync(s, s.memzero(fsm5(40 + i)))
                ins = s.activation(vb(47616, 256),
                                   vf(F_X2[c % 2], 4096)[:, i * 512:(i + 1) * 512],
                                   AF.Square, accum_out=fsm5(40 + i))
                if i < 7:
                    ssync(s, ins)
            fc_ln.inc(ins)
            s.wait_ge(fv_ln.h, 3 * c + 2)
            ssync(s, s.activation(fsm5(8), fsm5(5), AF.Ln, bias=s_eps[:]))
            fc_ln.inc(s.activation(fsm5(7), fsm5(8), AF.Exp, scale=-0.5))

        @block.scalar
        def _(s):
            for fc in range(NCH):
                if fc == 0:
                    ph5_lns(s, 0)
                for tf in range(4):
                    T = 4 * fc + tf
                    for q in range(0, 16, 2):
                        s.wait_ge(ft_A.h, 32 * T + 2 * q + 2)
                        if q == 0:
                            s.wait_ge(ft_B.h, 16 * T)
                            if tf == 3:
                                s.wait_ge(fv_g.h, 3 * fc + 3)
                        fse_A.inc(s.activation(
                            vb(F_ST, 8192, 0, 64)[:, q * 1024:(q + 1) * 1024],
                            PS[0:64, 0:1024], AF.Identity))
                    if tf == 0:
                        s.wait_ge(fv_ev.h, 64 * fc + 16)
                        s.wait_ge(fv_x3.h, fc)
                        fc_s.inc(s.activation(vb(F_W2, 4096), vb(F_V, 4096), AF.Square))
                    elif tf == 1:
                        s.wait_ge(fv_g.h, 3 * fc + 1)
                        s.wait_ge(fs_cr.h, 16 * (fc + 1))
                        fc_s.inc(s.activation(vb(F_E, 2048), vb(F_W3, 2048), AF.Exp,
                                              bias=fsm5(24)))
                        ssync(s, s.memzero(fsm5(26)))
                        fc_s.inc(s.activation(vf(F_SM + 32, 8), vb(F_SM + 16, 4), AF.Exp,
                                              bias=fsm5(24), accum_out=fsm5(26)))
                    elif tf == 2 and fc + 1 < NCH:
                        ph5_lns(s, fc + 1)

        @block.sync
        def _(sy):
            def do_xn_D(c):
                row2 = (c % 2) * FCH
                sy.wait_ge(fv_ln.h, 3 * c + 3)
                fs_xn.inc(sy.dma_start(out=fb_dram[row2:row2 + FCH, :], in_=vb(F_XN, 2048)), 16)
                sy.wait_ge(fs_xn.h, 16 * (c + 1))
                if c == 0:
                    sy.wait_ge(mv_oe.h, 128)
                else:
                    sy.wait_ge(ft_A.h, 128 * c)
                fs_D.inc(sy.dma_start(
                    out=vb(F_W4, 4096, 0, 64).rearrange("x (r y) -> x r y", r=FCH),
                    in_=fb_dram[row2:row2 + FCH, :].rearrange("r (x y) -> x r y", x=64, y=64)), 16)

            do_xn_D(0)
            for fc in range(NCH):
                row2 = (fc % 2) * FCH
                if fc >= 1:
                    sy.wait_ge(fv_x3.h, fc)
                    fs_x3.inc(sy.dma_start(out=x3_dram[(fc - 1) * FCH:fc * FCH, :],
                                           in_=vf(F_W3, 4096)), 16)
                sy.wait_ge(fv_ev.h, 64 * fc + 32)
                fs_c.inc(sy.dma_start(out=fc_dram[(fc % 2) * 64:(fc % 2) * 64 + 64, :],
                                      in_=vb(F_W2, 4096, 0, 64)), 16)
                sy.wait_ge(fs_c.h, 16 * (fc + 1))
                sy.wait_ge(fs_x3.h, 16 * fc)
                fs_cr.inc(sy.dma_start(
                    out=vb(F_W3, 2048).rearrange("r (l m) -> r l m", l=64),
                    in_=fc_dram[(fc % 2) * 64:(fc % 2) * 64 + 64, :]
                    .rearrange("l (r m) -> r l m", r=FCH)), 16)
                sy.wait_ge(fv_g.h, 3 * fc + 2)
                fs_g.inc(sy.dma_start(out=fg_dram[row2:row2 + FCH, :], in_=vb(F_G, 2048)), 16)
                sy.wait_ge(fs_g.h, 16 * (fc + 1))
                fs_gD.inc(sy.dma_start(
                    out=vb(F_W1, 4096, 0, 64).rearrange("x (r y) -> x r y", r=FCH),
                    in_=fg_dram[row2:row2 + FCH, :].rearrange("r (x y) -> x r y", x=64, y=64)), 16)
                if fc + 1 < NCH:
                    do_xn_D(fc + 1)
                sy.wait_ge(fv_ev.h, 64 * fc + 64)
                fs_o.inc(sy.dma_start(out=fo_dram[(fc % 2) * 64:(fc % 2) * 64 + 64, :],
                                      in_=vb(F_W1, 4096, 0, 64)), 16)
                sy.wait_ge(fs_o.h, 16 * (fc + 1))
                sy.wait_ge(fv_g.h, 3 * fc + 3)
                fs_or.inc(sy.dma_start(
                    out=vb(F_W2, 2048).rearrange("r (l m) -> r l m", l=64),
                    in_=fo_dram[(fc % 2) * 64:(fc % 2) * 64 + 64, :]
                    .rearrange("l (r m) -> r l m", r=FCH)), 16)
            sy.wait_ge(fv_x3.h, NCH)
            fs_x3.inc(sy.dma_start(out=x3_dram[(NCH - 1) * FCH:NCH * FCH, :],
                                   in_=vf(F_W3, 4096)), 16)

        @block.gpsimd
        def _(g):
            g.wait_ge(tm_b.h, 128)
            ms_xn.wait_total(g, 16)
            ms_o.wait_total(g, 128)
            fs_cs.inc(g.dma_start(out=vb(F_SLAB, 8192, 0, 64), in_=d_slab), 16)
            for fc in range(NCH):
                if fc == 0:
                    g.wait_ge(mv_oe.h, 128)
                ms_o.wait_total(g, 8 * (fc + 1))
                if fc >= 2:
                    g.wait_ge(fv_x3.h, fc - 1)
                fg_x.inc(g.dma_start(out=vf(F_X2[fc % 2], 4096),
                                     in_=x2_dram[fc * FCH:(fc + 1) * FCH, :]), fc)

        nc.leave_named_scope("ph5", sc5, False)
        if phases <= 5:
            @block.gpsimd
            def _(g):
                g.wait_ge(fs_x3.h, fs_x3.n)
            return nc
        sc6, _ = nc.enter_named_scope("ph6", False)
        moe_call(1, x3_dram, d_out, x3_dram)
        nc.leave_named_scope("ph6", sc6, False)

        @block.gpsimd
        def _(g):
            ms_o.wait_total(g, 256)
    return nc


# version 14 (final)
# speedup vs baseline: 1.0105x; 1.0034x over previous
import numpy as np
import ml_dtypes
from contextlib import ExitStack

import concourse.bass as bass
import concourse.mybir as mybir
from concourse.bass_utils import run_bass_kernel_spmd

F32 = mybir.dt.float32
BF16 = mybir.dt.bfloat16
AF = mybir.ActivationFunctionType
ALU = mybir.AluOpType
AX = mybir.AxisListType

D = 4096
SEQ = 4096
OWN = 2048
NH, KVL, QL, HD = 32, 4, 12, 128
NEXP, RANK = 64, 16
SCALING = 2.0
EPS = 1e-5
NEG = -1.0e30

QCG = 1024
QS = 512
MCH = 512
FCH = 128


def _consts():
    jj = np.arange(64)
    c64 = np.cos(2 * np.pi * np.outer(jj, jj) / 64).astype(np.float32)
    s64 = np.sin(2 * np.pi * np.outer(jj, jj) / 64).astype(np.float32)
    C = {}
    C["FA"] = np.concatenate([c64, -s64], axis=1)
    C["FHA"] = np.block([[c64, -s64], [s64, c64]])
    f = np.arange(D)
    ang = 2 * np.pi * np.outer(jj, f) / D
    Cb = np.cos(ang).astype(np.float32)
    Sb = np.sin(ang).astype(np.float32)
    slab = np.zeros((64, 16384), np.float32)
    for j1 in range(64):
        slab[:, j1 * 128:j1 * 128 + 64] = Cb[:, j1::64]
        slab[:, j1 * 128 + 64:(j1 + 1) * 128] = Sb[:, j1::64]
        slab[:, 8192 + j1 * 128:8192 + j1 * 128 + 64] = Sb[:, j1::64]
        slab[:, 8192 + j1 * 128 + 64:8192 + (j1 + 1) * 128] = -Cb[:, j1::64]
    C["SLAB"] = slab
    C["ONES128"] = np.ones((128, 1), np.float32)
    C["ONES1"] = np.ones((1, 128), np.float32)
    hm = np.zeros((128, 32), np.float32)
    for h in range(NH):
        hm[4 * h:4 * h + 4, h] = 1.0
    C["HEADMASK"] = hm
    ef = np.zeros((64, NEXP * RANK), np.float32)
    for e in range(NEXP):
        ef[e, RANK * e:RANK * (e + 1)] = 1.0
    C["EFULL"] = ef
    I64 = np.eye(64, dtype=np.float32)
    L1 = np.zeros((128, 128), np.float32)
    L1[0:64, 0:64] = I64
    L1[64:128, 0:64] = I64
    L2 = np.zeros((128, 128), np.float32)
    L2[0:64, 64:128] = -I64
    L2[64:128, 64:128] = I64
    C["LH1"] = L1 / D
    C["LH2"] = L2 / D
    SW = np.zeros((128, 128), np.float32)
    SW[0:64, 64:128] = I64
    SW[64:128, 0:64] = I64
    C["SW"] = SW
    return C


def _shard_inputs(inp):
    x = np.ascontiguousarray(np.asarray(inp["L0_attn_a_x"], np.float32))
    Wq = np.asarray(inp["L0_attn_a_Wq_d"], np.float32)
    bq = np.asarray(inp["L0_attn_a_bq_d"], np.float32)
    Wqk = np.asarray(inp["L0_attn_a_W_qk"], np.float32)
    bqk = np.asarray(inp["L0_attn_a_b_qk"], np.float32)
    Wkv = np.asarray(inp["L0_attn_a_Wkv_d"], np.float32)
    bkv = np.asarray(inp["L0_attn_a_bkv_d"], np.float32)
    Wvu = np.asarray(inp["L0_attn_a_Wv_u"], np.float32)
    bvu = np.asarray(inp["L0_attn_a_bv_u"], np.float32)
    Wo = np.ascontiguousarray(np.asarray(inp["L0_attn_a_Wo"], np.float32))
    bo = np.asarray(inp["L0_attn_a_bo"], np.float32)
    A = np.asarray(inp["L0_ffn_A"], np.float32)
    Bm = np.asarray(inp["L0_ffn_B"], np.float32)
    Wr = np.asarray(inp["L0_ffn_Wr"], np.float32)

    wpack = np.zeros((D, 36), np.float32)
    wpack[:, 0:12] = Wq
    wpack[:, 32:36] = Wkv
    b36 = np.zeros((36, 1), np.float32)
    b36[0:12, 0] = bq
    b36[32:36, 0] = bkv
    wv5 = np.zeros((5, D), np.float32)
    wv5[0:4] = Wvu
    wv5[4] = bvu
    AT = np.ascontiguousarray(A.reshape(NEXP * RANK, D).T)
    Barr = np.ascontiguousarray(Bm.transpose(0, 2, 1).reshape(NEXP * RANK, D))
    WrT = np.ascontiguousarray(Wr.T)

    C = _consts()
    shared = dict(wpack=wpack, b36=b36, wqk=Wqk, bqk=bqk.reshape(128, 1),
                  wv5=wv5, wo=Wo,
                  bo=np.ascontiguousarray(bo.reshape(32, 128).T),
                  at=AT, barr=Barr, wrt=WrT, **{k.lower(): v for k, v in C.items()})
    maps = []
    for c in range(8):
        b, half = c // 2, c % 2
        m = dict(shared)
        m["xq"] = np.ascontiguousarray(x[b, half * OWN:(half + 1) * OWN])
        m["xf"] = np.ascontiguousarray(x[b, (1 - half) * OWN:(2 - half) * OWN])
        maps.append(m)
    return maps


class Sem:
    def __init__(self, ctx, nc, name):
        self.h = ctx.enter_context(nc.semaphore(name))
        self.n = 0

    def inc(self, instr, k=1):
        instr.then_inc(self.h, k)
        self.n += k
        return self.n


class SemPair:
    def __init__(self, ctx, nc, name):
        self.s = [Sem(ctx, nc, name + "a"), Sem(ctx, nc, name + "b")]

    def inc(self, instr, i, k=16):
        return self.s[i % 2].inc(instr, k)

    def wait(self, eng, i, cnt_of_parity):
        eng.wait_ge(self.s[i % 2].h, 16 * cnt_of_parity)

    def wait_done(self, eng, i):
        eng.wait_ge(self.s[i % 2].h, 16 * (i // 2 + 1))

    def wait_total(self, eng, total):
        eng.wait_ge(self.s[0].h, 16 * ((total + 1) // 2))
        eng.wait_ge(self.s[1].h, 16 * (total // 2))


class SemQuad:
    def __init__(self, ctx, nc, name):
        self.s = [Sem(ctx, nc, name + chr(97 + i)) for i in range(4)]

    def inc(self, instr, i, k=16):
        return self.s[i % 4].inc(instr, k)

    def wait_done(self, eng, i):
        eng.wait_ge(self.s[i % 4].h, 16 * (i // 4 + 1))


def kernel(**inputs):
    maps = _shard_inputs(inputs)
    nc = build_program()
    res = run_bass_kernel_spmd(nc, maps, core_ids=list(range(8)))
    out = np.empty((4, SEQ, D), np.float32)
    for c in range(8):
        b, half = c // 2, c % 2
        out[b, half * OWN:(half + 1) * OWN] = res.results[c]["out"]
    return out


def build_program(phases=6, dbg=False, ph3sub=4):
    nc = bass.Bass("TRN2", target_bir_lowering=False, debug=False)

    d_xq = nc.dram_tensor("xq", [OWN, D], F32, kind="ExternalInput").ap()
    d_xf = nc.dram_tensor("xf", [OWN, D], F32, kind="ExternalInput").ap()
    d_wpack = nc.dram_tensor("wpack", [D, 36], F32, kind="ExternalInput").ap()
    d_b36 = nc.dram_tensor("b36", [36, 1], F32, kind="ExternalInput").ap()
    d_wqk = nc.dram_tensor("wqk", [QL, 128], F32, kind="ExternalInput").ap()
    d_bqk = nc.dram_tensor("bqk", [128, 1], F32, kind="ExternalInput").ap()
    d_wv5 = nc.dram_tensor("wv5", [5, D], F32, kind="ExternalInput").ap()
    d_wo = nc.dram_tensor("wo", [D, D], F32, kind="ExternalInput").ap()
    d_bo = nc.dram_tensor("bo", [128, 32], F32, kind="ExternalInput").ap()
    d_at = nc.dram_tensor("at", [D, NEXP * RANK], F32, kind="ExternalInput").ap()
    d_barr = nc.dram_tensor("barr", [NEXP * RANK, D], F32, kind="ExternalInput").ap()
    d_wrt = nc.dram_tensor("wrt", [D, NEXP], F32, kind="ExternalInput").ap()
    d_fa = nc.dram_tensor("fa", [64, 128], F32, kind="ExternalInput").ap()
    d_fha = nc.dram_tensor("fha", [128, 128], F32, kind="ExternalInput").ap()
    d_slab = nc.dram_tensor("slab", [64, 16384], F32, kind="ExternalInput").ap()
    d_ones128 = nc.dram_tensor("ones128", [128, 1], F32, kind="ExternalInput").ap()
    d_ones1 = nc.dram_tensor("ones1", [1, 128], F32, kind="ExternalInput").ap()
    d_headmask = nc.dram_tensor("headmask", [128, 32], F32, kind="ExternalInput").ap()
    d_efull = nc.dram_tensor("efull", [64, NEXP * RANK], F32, kind="ExternalInput").ap()
    d_lh1 = nc.dram_tensor("lh1", [128, 128], F32, kind="ExternalInput").ap()
    d_lh2 = nc.dram_tensor("lh2", [128, 128], F32, kind="ExternalInput").ap()
    d_sw = nc.dram_tensor("sw", [128, 128], F32, kind="ExternalInput").ap()
    d_out = nc.dram_tensor("out", [OWN, D], F32, kind="ExternalOutput").ap()

    dk0 = dict(kind="ExternalOutput") if dbg else {}
    xn_dram = nc.dram_tensor("xn_s", [SEQ, D], BF16, **dk0).ap()
    qt_dram = nc.dram_tensor("qt_s", [128, OWN], BF16, **dk0).ap()
    nb_dram = nc.dram_tensor("nb_s", [32, OWN], BF16, **dk0).ap()
    ckv_dram = nc.dram_tensor("ckv_s", [5, SEQ], BF16, **dk0).ap()
    mx_dram = nc.dram_tensor("mx_s", [4, 1], F32).ap()
    mx2_dram = nc.dram_tensor("mx2_s", [1, 1], F32).ap()
    aot_dram = nc.dram_tensor("aot_s", [D, OWN], BF16, **({"kind": "ExternalOutput"} if dbg else {})).ap()
    dk = dict(kind="ExternalOutput") if dbg else {}
    x1_dram = nc.dram_tensor("x1_s", [OWN, D], F32, **dk).ap()
    x2_dram = nc.dram_tensor("x2_s", [OWN, D], F32, **dk).ap()
    x3_dram = nc.dram_tensor("x3_s", [OWN, D], F32, **dk).ap()
    xnm_dram = nc.dram_tensor("xnm_s", [OWN, D], BF16).ap()
    fb_dram = nc.dram_tensor("fb_s", [2 * FCH, D], BF16).ap()
    fc_dram = nc.dram_tensor("fc_s", [128, 8192], BF16).ap()
    fg_dram = nc.dram_tensor("fg_s", [2 * FCH, D], BF16).ap()
    barr_bf = nc.dram_tensor("barr_bf", [NEXP * RANK, D], BF16).ap()
    fo_dram = nc.dram_tensor("fo_s", [128, 8192], BF16).ap()

    with ExitStack() as ctx:
        sb = lambda n, s, d: ctx.enter_context(nc.sbuf_tensor(n, s, d))
        mksem = lambda n: Sem(ctx, nc, n)

        s_wpack = sb("c_wpack", [128, 32 * 36], BF16)
        s_b36 = sb("c_b36", [36, 1], F32)
        s_wqk_raw = sb("c_wqkr", [QL, 128], F32)
        s_wqk05 = sb("c_wqk05", [QL, 128], BF16)
        s_bqk = sb("c_bqk", [128, 1], F32)
        s_bqk05 = sb("c_bqk05", [128, 1], F32)
        s_wv5 = sb("c_wv5", [5, D], BF16)
        s_wrt = sb("c_wrt", [128, 32 * NEXP], BF16)
        s_bo = sb("c_bo", [128, 32], F32)
        s_fa = sb("c_fa", [64, 128], BF16)
        s_fha = sb("c_fha", [128, 128], BF16)
        s_ones128 = sb("c_ones128", [128, 1], BF16)
        s_ones1 = sb("c_ones1", [1, 128], BF16)
        s_headmask = sb("c_hm", [128, 32], BF16)
        s_efull = sb("c_efull", [64, NEXP * RANK], BF16)
        s_lh1 = sb("c_lh1", [128, 128], BF16)
        s_lh2 = sb("c_lh2", [128, 128], BF16)
        s_sw = sb("c_sw", [128, 128], BF16)
        s_eps = sb("c_eps", [128, 1], F32)

        AW = 47872
        arena = sb("arena", [128, AW], F32)
        PS = ctx.enter_context(nc.psum_tensor("PS", [128, 4096], F32))

        def vf(off, n, p0=0, p1=128):
            return arena.ap()[p0:p1, off:off + n]

        def vb(off, n, p0=0, p1=128):
            return arena.ap()[p0:p1, off:off + n].bitcast(BF16)

        def bank(i, w=512):
            return PS[:, i * 512:i * 512 + w]

        def vsync(v, instr):
            vch.inc(instr)
            v.wait_ge(vch.h, vch.n)

        def ssync(s, instr):
            sch.inc(instr)
            s.wait_ge(sch.h, sch.n)

        def gsync(g, instr):
            gch.inc(instr)
            g.wait_ge(gch.h, gch.n)

        c_ld = mksem("c_ld")
        q_ld = SemPair(ctx, nc, "q_ld")
        vst = mksem("vst"); sst = mksem("sst"); vst2 = mksem("vst2")
        vch = mksem("vch"); sch = mksem("sch"); gch = mksem("gch")
        sst2 = mksem("sst2"); nrm = mksem("nrm"); stq = SemPair(ctx, nc, "stq")
        txn = SemPair(ctx, nc, "txn"); tpj = mksem("tpj"); spj = mksem("spj")
        tqt = mksem("tqt"); sqt = mksem("sqt"); sab = mksem("sab")
        tbs = mksem("tbs"); vmc = mksem("vmc"); dmc = mksem("dmc")
        vnb = mksem("vnb"); dq5 = mksem("dq5"); dck = mksem("dck")
        t_v = mksem("t_v"); v_ve = mksem("v_ve")
        t_sc = mksem("t_sc"); s_ex = mksem("s_ex"); t_av = mksem("t_av")
        v_zt = mksem("v_zt"); t_zm = mksem("t_zm"); v_rz = mksem("v_rz")
        t_zb = mksem("t_zb"); v_at = mksem("v_at")
        g_wo = SemPair(ctx, nc, "g_wo"); t_wo = mksem("t_wo"); s_wo = mksem("s_wo")
        st_wo = SemPair(ctx, nc, "st_wo")
        mg_x = SemPair(ctx, nc, "mg_x"); mg_aA = mksem("mg_aA"); mg_aB = mksem("mg_aB"); mg_r = SemPair(ctx, nc, "mg_r")
        ms_t = SemPair(ctx, nc, "ms_t"); ms_xn = SemPair(ctx, nc, "ms_xn"); ms_T1 = SemQuad(ctx, nc, "msT1")
        ms_T2 = SemQuad(ctx, nc, "msT2"); ms_o = SemPair(ctx, nc, "ms_o"); ms_x1 = SemPair(ctx, nc, "ms_x1")
        mv_1 = mksem("mv_1"); mv_g = mksem("mv_g"); mv_gx = mksem("mv_gx")
        mv_hg = mksem("mv_hg"); mv_oe = mksem("mv_oe")
        mc_s = mksem("mc_s"); mc_e = mksem("mc_e")
        tm_r = mksem("tm_r"); tm_g = mksem("tm_g"); tm_h = mksem("tm_h")
        tm_b = mksem("tm_b")
        fg_x = SemPair(ctx, nc, "fg_x"); fs_xn = mksem("fs_xn"); fs_D = mksem("fs_D")
        ft_A = mksem("ft_A"); ft_B = mksem("ft_B"); fs_cs = mksem("fs_cs")
        fv_ev = mksem("fv_ev"); fs_c = mksem("fs_c"); fs_cr = mksem("fs_cr")
        fv_g = mksem("fv_g"); fc_s = mksem("fc_s"); fs_g = mksem("fs_g")
        fs_gD = mksem("fs_gD"); fs_o = mksem("fs_o")
        fs_or = mksem("fs_or"); fv_x3 = mksem("fv_x3"); fs_x3 = mksem("fs_x3")
        fv_ln = mksem("fv_ln"); fc_ln = mksem("fc_ln")
        ft_P = mksem("ft_P"); fv_pe = mksem("fv_pe")
        ft_G = mksem("ft_G"); fv_gw = mksem("fv_gw")
        ft_H = mksem("ft_H"); fv_he = mksem("fv_he")

        block = ctx.enter_context(nc.Block())

        sc0, _ = nc.enter_named_scope("ph0", False)

        @block.gpsimd
        def _(g):
            g.dma_start(out=s_wpack.ap().rearrange("p (c w) -> p c w", w=36),
                        in_=d_wpack.rearrange("(c p) w -> p c w", p=128)).then_inc(c_ld.h, 16)
            g.dma_start(out=s_b36[:], in_=d_b36).then_inc(c_ld.h, 16)
            g.dma_start(out=s_wqk_raw[:], in_=d_wqk).then_inc(c_ld.h, 16)
            g.dma_start(out=s_bqk[:], in_=d_bqk).then_inc(c_ld.h, 16)
            g.dma_start(out=s_wv5[:], in_=d_wv5).then_inc(c_ld.h, 16)
            g.dma_start(out=s_wrt.ap().rearrange("p (c e) -> p c e", e=NEXP),
                        in_=d_wrt.rearrange("(c p) e -> p c e", p=128)).then_inc(c_ld.h, 16)
            g.dma_start(out=s_bo.ap().unsqueeze(2),
                        in_=d_bo.rearrange("(c p) o -> p c o", p=128)).then_inc(c_ld.h, 16)
            for s_t, d_t in [(s_fa, d_fa), (s_fha, d_fha),
                             (s_ones128, d_ones128), (s_ones1, d_ones1),
                             (s_headmask, d_headmask), (s_efull, d_efull),
                             (s_lh1, d_lh1), (s_lh2, d_lh2), (s_sw, d_sw)]:
                g.dma_start(out=s_t[:], in_=d_t).then_inc(c_ld.h, 16)
        c_ld.n = 16 * 16

        @block.vector
        def _(v):
            v.wait_ge(c_ld.h, c_ld.n)
            v.memset(s_eps[:], EPS)
            v.tensor_scalar(s_wqk05[:], s_wqk_raw[:], 0.5, None, op0=ALU.mult)
            vst.inc(v.tensor_scalar(s_bqk05[:], s_bqk[:], 0.5, None, op0=ALU.mult))

        nc.leave_named_scope("ph0", sc0, False)
        sc1, _ = nc.enter_named_scope("ph1", False)
        a_x = [0, 4096]
        a_sqs = 8192
        a_xn = [12288, 14336]
        a_sm = 16384

        def ph1_smalls(bi, k):
            return vf(a_sm + bi * 16 + k, 1)

        @block.gpsimd
        def _(g):
            for i in range(32):
                if i >= 2:
                    g.wait_ge(nrm.h, i - 1)
                    g.wait_ge(sst.h, i - 1)
                src = d_xq if i < 16 else d_xf
                row = (i % 16) * 128
                q_ld.inc(g.dma_start(out=vf(a_x[i % 2], 4096),
                                     in_=src[row:row + 128, :]), i)

        @block.vector
        def _(v):
            for i in range(32):
                bi = i % 2
                q_ld.wait_done(v, i)
                vsync(v, v.reduce_sum(ph1_smalls(bi, 0), vf(a_x[bi], 4096), axis=AX.X))
                vst.inc(v.tensor_scalar(ph1_smalls(bi, 1), ph1_smalls(bi, 0),
                                        -1.0 / D, None, op0=ALU.mult))
                v.wait_ge(vch.h, vch.n)
                vsync(v, v.engine_nop())
                v.wait_ge(sst.h, i + 1)
                vsync(v, v.tensor_tensor(ph1_smalls(bi, 2), ph1_smalls(bi, 1),
                                         ph1_smalls(bi, 1), op=ALU.mult))
                vsync(v, v.tensor_scalar(ph1_smalls(bi, 3), ph1_smalls(bi, 4), 1.0 / D,
                                         None, op0=ALU.mult))
                vst2.inc(v.tensor_tensor(ph1_smalls(bi, 5), ph1_smalls(bi, 3),
                                         ph1_smalls(bi, 2), op=ALU.subtract))
                v.wait_ge(sst2.h, i + 1)
                v.tensor_tensor(ph1_smalls(bi, 6), ph1_smalls(bi, 1),
                                ph1_smalls(bi, 7), op=ALU.mult)
                vsync(v, v.engine_nop())
                if i >= 2:
                    stq.wait(v, i, i // 2)
                nrm.inc(v.tensor_scalar(vb(a_xn[bi], 2048), vf(a_x[bi], 4096),
                                        ph1_smalls(bi, 7), ph1_smalls(bi, 6),
                                        op0=ALU.mult, op1=ALU.add))

        @block.scalar
        def _(s):
            for i in range(32):
                bi = i % 2
                q_ld.wait_done(s, i)
                ssync(s, s.memzero(ph1_smalls(bi, 4)))
                sst.inc(s.activation(vf(a_sqs, 4096), vf(a_x[bi], 4096), AF.Square,
                                     accum_out=ph1_smalls(bi, 4)))
                s.wait_ge(vst2.h, i + 1)
                ssync(s, s.activation(ph1_smalls(bi, 8), ph1_smalls(bi, 5), AF.Ln,
                                      bias=s_eps[:]))
                sst2.inc(s.activation(ph1_smalls(bi, 7), ph1_smalls(bi, 8), AF.Exp,
                                      scale=-0.5))

        @block.sync
        def _(sy):
            for i in range(32):
                bi = i % 2
                sy.wait_ge(nrm.h, i + 1)
                row = (i % 16) * 128 + (0 if i < 16 else OWN)
                stq.inc(sy.dma_start(out=xn_dram[row:row + 128, :],
                                     in_=vb(a_xn[bi], 2048)), i)

        nc.leave_named_scope("ph1", sc1, False)
        sc2, _ = nc.enter_named_scope("ph2", False)
        A_Q5, A_CKV4 = 32768, 44032
        HG = [(0, 0, 11), (1, 11, 11), (2, 22, 10)]

        @block.sync
        def _(sy):
            stq.wait_total(sy, 32)
            for ci in range(32):
                if ci >= 2:
                    sy.wait_ge(tpj.h, ci - 1)
                txn.inc(sy.dma_start(out=vb(2048 * (ci % 2), 2048),
                                     in_=xn_dram[:, ci * 128:(ci + 1) * 128],
                                     transpose=True), ci)

        @block.tensor
        def _(t):
            for ci in range(32):
                txn.wait_done(t, ci)
                if ci == 0:
                    t.wait_ge(c_ld.h, c_ld.n)
                for nb in range(8):
                    ins = t.matmul(PS[0:36, nb * 512:(nb + 1) * 512],
                                   s_wpack[:, ci * 36:(ci + 1) * 36],
                                   vb(2048 * (ci % 2), 2048)[:, nb * 512:(nb + 1) * 512],
                                   start=(ci == 0), stop=(ci == 31))
                tpj.inc(ins)

        @block.scalar
        def _(s):
            s.wait_ge(tpj.h, 32)
            s.wait_ge(vst.h, 34)
            for nb in range(8):
                spj.inc(s.activation(vb(4096, 2048, 0, 12)[:, nb * 512:(nb + 1) * 512],
                                     PS[0:12, nb * 512:(nb + 1) * 512], AF.Identity,
                                     bias=s_b36[0:12, :]))
                spj.inc(s.activation(vb(44032, 2048, 0, 4)[:, nb * 512:(nb + 1) * 512],
                                     PS[32:36, nb * 512:(nb + 1) * 512], AF.Identity,
                                     bias=s_b36[32:36, :]))

        @block.vector
        def _(v):
            v.wait_ge(c_ld.h, c_ld.n)
            vst.inc(v.memset(vb(44032, 2048, 0, 8), 1.0))

        @block.tensor
        def _(t):
            t.wait_ge(spj.h, 16)
            t.wait_ge(vst.h, 1)
            for nb in range(4):
                tqt.inc(t.matmul(bank(nb), s_wqk05[:], 
                                 vb(4096, 2048, 0, 12)[:, nb * 512:(nb + 1) * 512],
                                 start=True, stop=True))

        @block.scalar
        def _(s):
            for nb in range(4):
                s.wait_ge(tqt.h, nb + 1)
                sqt.inc(s.activation(vb(6144, 1024)[:, nb * 512:(nb + 1) * 512],
                                     bank(nb), AF.Identity, bias=s_bqk05[:]))
            s.wait_ge(sqt.h, 4)
            sab.inc(s.activation(vb(7168, 1024), vb(6144, 1024), AF.Abs))

        @block.tensor
        def _(t):
            t.wait_ge(sab.h, 1)
            for nb in range(4):
                tbs.inc(t.matmul(PS[0:32, 2048 + nb * 512: 2048 + (nb + 1) * 512],
                                 s_headmask[:], vb(7168, 1024)[:, nb * 512:(nb + 1) * 512],
                                 start=True, stop=True))

        @block.vector
        def _(v):
            v.wait_ge(spj.h, 16)
            vmc.inc(v.reduce_max(vf(9216, 1, 0, 4), vb(44032, 2048, 0, 4), axis=AX.X,
                                 apply_absolute_value=True))

        @block.sync
        def _(sy):
            sy.wait_ge(vmc.h, 1)
            dmc.inc(sy.dma_start(out=mx_dram, in_=vf(9216, 1, 0, 4)), 16)
            sy.wait_ge(dmc.h, 16)
            dmc.inc(sy.dma_start(out=vf(9220, 4, 0, 1), in_=mx_dram.rearrange("a b -> b a")), 16)

        @block.vector
        def _(v):
            v.wait_ge(dmc.h, 32)
            vsync(v, v.reduce_max(vf(9232, 1, 0, 1), vf(9220, 4, 0, 1), axis=AX.X))
            vmc.inc(v.tensor_scalar(vf(9236, 1, 0, 1), vf(9232, 1, 0, 1), -1.0, None,
                                    op0=ALU.mult))

        @block.sync
        def _(sy):
            sy.wait_ge(vmc.h, 2)
            dmc.inc(sy.dma_start(out=mx2_dram, in_=vf(9236, 1, 0, 1)), 16)
            sy.wait_ge(dmc.h, 48)
            dmc.inc(sy.dma_start(out=vf(9240, 1, 0, 32),
                                 in_=mx2_dram.to_broadcast([32, 1])), 16)

        @block.vector
        def _(v):
            v.wait_ge(dmc.h, 64)
            v.wait_ge(tbs.h, 4)
            for nb in range(4):
                vnb.inc(v.tensor_scalar(vb(8192, 1024, 0, 32)[:, nb * 512:(nb + 1) * 512],
                                        PS[0:32, 2048 + nb * 512:2048 + (nb + 1) * 512],
                                        vf(9240, 1, 0, 32), None, op0=ALU.mult))

        @block.sync
        def _(sy):
            sy.wait_ge(sqt.h, 4)
            dq5.inc(sy.dma_start(out=qt_dram, in_=vb(6144, 1024)), 16)
            sy.wait_ge(vnb.h, 4)
            dq5.inc(sy.dma_start(out=nb_dram, in_=vb(8192, 1024, 0, 32)), 16)
            sy.wait_ge(spj.h, 16)
            sy.wait_ge(vst.h, 34)
            dq5.inc(sy.dma_start(out=ckv_dram, in_=vb(44032, 2048, 0, 5)), 16)
            sy.wait_ge(dq5.h, 48)
            for grp, h0, nh in HG:
                p0 = 32 * grp
                dq5.inc(sy.dma_start(
                    out=vb(A_Q5, nh * 1024, p0, p0 + 4).rearrange("d (h t) -> d h t", h=nh),
                    in_=qt_dram.rearrange("(h d) t -> d h t", d=4)[:, h0:h0 + nh, :]), 16)
                dq5.inc(sy.dma_start(
                    out=vb(A_Q5, nh * 1024, p0 + 4, p0 + 5).rearrange("a (h t) -> a h t", h=nh),
                    in_=nb_dram.unsqueeze(0)[:, h0:h0 + nh, :]), 16)
                dck.inc(sy.dma_start(out=vb(A_CKV4, 2048, p0, p0 + 5), in_=ckv_dram), 16)

        def _ph3():
            A_AT, A_PB, A_ZT, A_VH = 0, 16384, 24576, 28672
            A_Z128, A_RZH, A_RZ, A_RZBF, A_WOUT = 46080, 46336, 46592, 47104, 20480
            A_WOC = 16384

            def q5slice(h, qsi_g, w=512):
                g = min(h // 11, 2)
                p0 = 32 * g
                nh = 11 if g < 2 else 10
                base = (h - 11 * g) * 2048 + qsi_g * 512
                return vb(A_Q5, nh * 1024, p0, p0 + 5)[:, base:base + w]

            def ckv4s(kb, p0):
                return vb(A_CKV4, 2048, p0, p0 + 5)[:, kb * 128:(kb + 1) * 128]

            def z_mm(t, ITz):
                t.wait_ge(v_zt.h, 2 * (ITz + 1))
                if ITz >= 1:
                    t.wait_ge(v_at.h, 2 * ITz)
                t.matmul(PS[0:1, 3072:3584], s_ones128[:],
                         vb(A_ZT, 256), start=True, stop=False)
                t_zm.inc(t.matmul(PS[0:1, 3072:3584], s_ones128[:],
                                  vb(A_ZT + 2048, 256), start=False, stop=True))

            def z_bc(t, ITz):
                t.wait_ge(v_rz.h, ITz + 1)
                t_zb.inc(t.matmul(PS[:, 3072:3584], s_ones1[:],
                                  vb(A_RZH, 256, 0, 1), start=True, stop=True))

            def attnv_half(t, H):
                ITp, hb = H // 2, H % 2
                HIp = ITp // 2
                t.wait_ge(s_ex.h, 8 * H + 8)
                t.wait_ge(v_ve.h, 8 * HIp + 4 + 4 * hb)
                if hb == 0 and ITp >= 1:
                    t.wait_ge(v_at.h, 2 * ITp)
                bc = 2048 + hb * 512
                for i in range(16):
                    kb = 16 * hb + i
                    ins = t.matmul(
                        PS[:, bc:bc + 512],
                        vb(A_VH + (HIp % 2) * 2048, 2048)[:, kb * 128:(kb + 1) * 128],
                        vb(A_PB, 8192)[:, kb * 512:(kb + 1) * 512],
                        start=(i == 0), stop=(i == 15))
                t_av.inc(ins, 1)

            @block.tensor
            def _(t):
                for qcg in range(2):
                    for h in range(32):
                        HI = qcg * 32 + h
                        for g8 in range(8):
                            if HI == 0 and g8 == 0:
                                t.wait_ge(vnb.h, 4)
                                t.wait_ge(dck.h, dck.n)
                                t.wait_ge(dq5.h, dq5.n)
                            if 8 * HI + g8 >= 1:
                                t.wait_ge(v_ve.h, 8 * HI + g8)
                            for j in range(4):
                                kb = g8 * 4 + j
                                ins = t.matmul(PS[:, 3584 + j * 128:3584 + (j + 1) * 128],
                                               ckv4s(kb, 0), s_wv5[:, h * 128:(h + 1) * 128],
                                               start=True, stop=True)
                            t_v.inc(ins, 4)
                        for qsi in range(2):
                            if ph3sub < 2:
                                continue
                            IT = qcg * 64 + h * 2 + qsi
                            QSI = qcg * 2 + qsi
                            for p in range(16):
                                PI = 16 * IT + p
                                if p == 0 and IT == 64 and ph3sub >= 4:
                                    t.wait_ge(s_wo.h, 64)
                                if PI >= 2:
                                    t.wait_ge(s_ex.h, PI - 1)
                                cb = (PI % 2) * 1024
                                t.matmul(PS[:, cb:cb + 512], ckv4s(2 * p, 32 * min(h // 11, 2)),
                                         q5slice(h, QSI), start=True, stop=True)
                                ins = t.matmul(PS[:, cb + 512:cb + 1024],
                                               ckv4s(2 * p + 1, 32 * min(h // 11, 2)),
                                               q5slice(h, QSI), start=True, stop=True)
                                t_sc.inc(ins, 2)
                                if p == 7 and ph3sub >= 3 and 2 * IT - 1 >= 2 * qcg * 64:
                                    z_mm(t, IT - 1)
                                    attnv_half(t, 2 * IT - 1)
                                if p == 10 and ph3sub >= 3 and IT - 1 >= qcg * 64:
                                    z_bc(t, IT - 1)
                            if ph3sub < 3:
                                continue
                            attnv_half(t, 2 * IT)
                    if ph3sub >= 3:
                        z_mm(t, qcg * 64 + 63)
                        attnv_half(t, 2 * (qcg * 64 + 63) + 1)
                        z_bc(t, qcg * 64 + 63)
                    if ph3sub < 4:
                        continue
                    t.wait_ge(v_at.h, 2 * 64 * (qcg + 1))
                    for dmt in range(32):
                        g_wo.wait_done(t, qcg * 32 + dmt)
                        for qc4 in range(2):
                            k = qcg * 64 + dmt * 2 + qc4
                            if k >= 2:
                                t.wait_ge(s_wo.h, k - 1)
                            for hd in range(32):
                                ins = t.matmul(
                                    PS[:, (k % 2) * 512:(k % 2) * 512 + 512],
                                    vb(A_WOC + (dmt % 2) * 2048, 2048)[:, hd * 128:(hd + 1) * 128],
                                    vb(A_AT, 16384)[:, hd * 1024 + qc4 * 512: hd * 1024 + qc4 * 512 + 512],
                                    start=(hd == 0), stop=(hd == 31))
                            t_wo.inc(ins, 1)

            def z_inv(s, ITz):
                s.wait_ge(t_zm.h, ITz + 1)
                ssync(s, s.activation(vf(A_RZBF, 512, 0, 1), PS[0:1, 3072:3584], AF.Ln))
                v_rz.inc(s.activation(vb(A_RZH, 256, 0, 1), vf(A_RZBF, 512, 0, 1),
                                      AF.Exp, scale=-1.0))

            @block.scalar
            def _(s):
                for qcg in range(2):
                    for h in range(32):
                        for qsi in range(2):
                            IT = qcg * 64 + h * 2 + qsi
                            if ph3sub < 2:
                                continue
                            for p in range(16):
                                PI = 16 * IT + p
                                s.wait_ge(t_sc.h, 2 * (PI + 1))
                                if IT >= 1 and ph3sub >= 3 and p % 8 == 0:
                                    s.wait_ge(t_av.h, 2 * (IT - 1) + p // 8 + 1)
                                if p % 8 == 0 and IT >= 1 and ph3sub >= 3:
                                    s.wait_ge(v_zt.h, 2 * (IT - 1) + p // 8 + 1)
                                if p == 0 and IT == 64 and ph3sub >= 4:
                                    s.wait_ge(t_wo.h, 64)
                                    s.wait_ge(s_wo.h, 64)
                                    st_wo.wait_total(s, 64)
                                cb = (PI % 2) * 1024
                                s_ex.inc(s.activation(
                                    vb(A_PB, 8192)[:, p * 1024:(p + 1) * 1024],
                                    PS[:, cb:cb + 1024], AF.Exp))
                                if p == 7 and IT - 1 >= qcg * 64 and ph3sub >= 3:
                                    z_inv(s, IT - 1)
                    if ph3sub >= 3:
                        z_inv(s, qcg * 64 + 63)
                    for dmt in range(32):
                        if ph3sub < 4:
                            continue
                        for qc4 in range(2):
                            k = qcg * 64 + dmt * 2 + qc4
                            s.wait_ge(t_wo.h, k + 1)
                            if k >= 2:
                                st_wo.wait(s, k, k // 2)
                            s_wo.inc(s.activation(
                                vb(A_WOUT + (k % 2) * 256, 256), PS[:, (k % 2) * 512:(k % 2) * 512 + 512],
                                AF.Identity, bias=s_bo[:, dmt:dmt + 1]))

            def att_evac(v, ITE):
                hE, qsE = (ITE % 64) // 2, ITE % 2
                v.wait_ge(t_zb.h, ITE + 1)
                v_at.inc(v.engine_nop())
                v.wait_ge(t_av.h, 2 * (ITE + 1))
                if ITE == 64 and ph3sub >= 4:
                    v.wait_ge(t_wo.h, 64)
                vsync(v, v.tensor_copy(vb(A_RZ, 256), PS[:, 2048:2560]))
                vsync(v, v.tensor_tensor(vb(A_RZ + 256, 256), PS[:, 2560:3072],
                                         vb(A_RZ, 256), op=ALU.add))
                v_at.inc(v.tensor_tensor(
                    vb(A_AT, 16384)[:, hE * 1024 + qsE * 512: hE * 1024 + qsE * 512 + 512],
                    PS[:, 3072:3584], vb(A_RZ + 256, 256), op=ALU.mult))

            @block.vector
            def _(v):
                for qcg in range(2):
                    for h in range(32):
                        HI = qcg * 32 + h
                        for g8 in range(8):
                            v.wait_ge(t_v.h, 32 * HI + 4 * (g8 + 1))
                            if g8 == 0 and HI >= 2 and ph3sub >= 3:
                                v.wait_ge(t_av.h, 2 * (2 * HI - 2))
                            v_ve.inc(v.tensor_copy(
                                vb(A_VH + (HI % 2) * 2048, 2048)[:, g8 * 512:(g8 + 1) * 512],
                                PS[:, 3584:4096]))
                        for qsi in range(2):
                            if ph3sub < 3:
                                continue
                            IT = qcg * 64 + h * 2 + qsi
                            zt = vb(A_ZT, 4096)
                            pb = vb(A_PB, 8192)
                            v.wait_ge(s_ex.h, 16 * IT + 8)
                            v.wait_ge(t_zm.h, IT)
                            vsync(v, v.tensor_tensor(zt[:, 0:4096], pb[:, 0:4096],
                                                     pb[:, 4096:8192], op=ALU.add))
                            vsync(v, v.tensor_tensor(zt[:, 0:2048], zt[:, 0:2048],
                                                     zt[:, 2048:4096], op=ALU.add))
                            vsync(v, v.tensor_tensor(zt[:, 0:1024], zt[:, 0:1024],
                                                     zt[:, 1024:2048], op=ALU.add))
                            v_zt.inc(v.tensor_tensor(zt[:, 0:512], zt[:, 0:512],
                                                     zt[:, 512:1024], op=ALU.add))
                            if IT - 1 >= qcg * 64:
                                att_evac(v, IT - 1)
                                v.wait_ge(v_at.h, v_at.n)
                            v.wait_ge(s_ex.h, 16 * (IT + 1))
                            vsync(v, v.tensor_tensor(zt[:, 4096:8192], pb[:, 8192:12288],
                                                     pb[:, 12288:16384], op=ALU.add))
                            vsync(v, v.tensor_tensor(zt[:, 4096:6144], zt[:, 4096:6144],
                                                     zt[:, 6144:8192], op=ALU.add))
                            vsync(v, v.tensor_tensor(zt[:, 4096:5120], zt[:, 4096:5120],
                                                     zt[:, 5120:6144], op=ALU.add))
                            v_zt.inc(v.tensor_tensor(zt[:, 4096:4608], zt[:, 4096:4608],
                                                     zt[:, 4608:5120], op=ALU.add))
                    if ph3sub >= 3:
                        att_evac(v, qcg * 64 + 63)


            @block.sync
            def _(sy):
                for qcg in range(2):
                    for dmt in range(32):
                        if ph3sub < 4:
                            continue
                        for qc4 in range(2):
                            k = qcg * 64 + dmt * 2 + qc4
                            sy.wait_ge(s_wo.h, k + 1)
                            st_wo.inc(sy.dma_start(
                                out=aot_dram[dmt * 128:(dmt + 1) * 128,
                                             qcg * 1024 + qc4 * 512: qcg * 1024 + qc4 * 512 + 512],
                                in_=vb(A_WOUT + (k % 2) * 256, 256)), k)

            @block.gpsimd
            def _(g):
                for qcg in range(2):
                    for dmt in range(32):
                        if ph3sub < 4:
                            continue
                        if dmt < 2:
                            g.wait_ge(t_av.h, 2 * 64 * (qcg + 1))
                            g.wait_ge(v_zt.h, 128 * (qcg + 1))
                        else:
                            g.wait_ge(t_wo.h, qcg * 64 + 2 * dmt - 2)
                        g_wo.inc(g.dma_start(
                            out=vb(A_WOC + (dmt % 2) * 2048, 2048).rearrange("p (hd c) -> p hd c", c=128),
                            in_=d_wo.rearrange("(hd p) c -> p hd c", p=128)[:, :, dmt * 128:(dmt + 1) * 128]),
                            qcg * 32 + dmt)

        nc.leave_named_scope("ph2", sc2, False)

        @block.gpsimd
        def _(g):
            g.dma_start(out=barr_bf, in_=d_barr).then_inc(c_ld.h, 16)
        c_ld.n += 16

        sc3, _ = nc.enter_named_scope("ph3", False)
        if phases >= 3:
            _ph3()
        nc.leave_named_scope("ph3", sc3, False)

        M_AB, M_HGT, M_GX = 0, 16384, 24576
        M_XT1, M_XT2 = 26624, 27648
        M_MX = [28672, 32768]
        M_SQS = 36864
        M_MAT = [40960, 43008]
        M_SM = 45056
        G_SCR, G_SCR2, G_M8 = M_SM, M_SM + 64, M_SM + 128
        G_NEGM, G_ZD, G_Z16, G_RZ, G_RZ2 = M_SM + 144, M_SM + 145, M_SM + 146, M_SM + 147, M_SM + 148
        G_MASK = M_SM + 152
        G_ME, G_E16O = M_SM + 184, M_SM + 248
        G_G, G_GT32, G_GTB = M_SM + 264, M_SM + 328, M_SM + 456
        GT4 = 46336
        M_MO = [45824, 46336]

        def msm(bi, j):
            return vf(46848 + bi * 16 + j, 1)

        def moe_call(call, d_src, d_dst, d_res):
            KB = call * 16
            RB = call * 128
            HB = call * 128
            GB = call * 16
            XB = call * 32

            @block.gpsimd
            def _(g):
                if call == 0:
                    g.wait_ge(t_wo.h, 128)
                else:
                    g.wait_ge(fs_x3.h, 256)
                    g.wait_ge(fv_x3.h, 16)
                    g.wait_ge(ft_B.h, 1024)
                    g.wait_ge(fv_ev.h, 1024)
                    g.wait_ge(fv_g.h, 64)
                    g.wait_ge(fv_pe.h, 256)
                    g.wait_ge(fv_gw.h, 256)
                    g.wait_ge(fv_he.h, 256)
                mg_aA.inc(g.dma_start(
                    out=vb(M_AB, 16384).rearrange("p (c e) -> p c e", e=1024),
                    in_=d_at.rearrange("(c p) e -> p c e", p=128)), 16)
                for k in range(16):
                    kk = KB + k
                    if k == 0 and call == 0:
                        g.wait_ge(t_av.h, 256)
                        g.wait_ge(t_sc.h, 4096)
                    if k >= 2:
                        g.wait_ge(mv_1.h, 3 * (kk - 2) + 3)
                        g.wait_ge(mc_s.h, 2 * (kk - 2) + 1)
                        if call == 0:
                            ms_x1.wait(g, k, k // 2)
                    if call == 1:
                        g.wait_ge(fs_x3.h, 16 * (k + 1))
                        if k == 0:
                            g.wait_ge(fs_xn.h, 256)
                            g.wait_ge(fs_g.h, 256)
                    row = k * 128
                    mg_x.inc(g.dma_start(out=vf(M_MX[k % 2], 4096),
                                         in_=d_src[row:row + 128, :]), kk)
                g.wait_ge(tm_h.h, HB + 128)
                g.wait_ge(c_ld.h, 272)
                mg_aB.inc(g.dma_start(
                    out=vb(M_AB, 16384).rearrange("p (e d) -> p e d", d=4096),
                    in_=barr_bf.rearrange("(e p) d -> p e d", p=128)), 16)


            @block.sync
            def _(sy):
                if call == 0:
                    st_wo.wait_total(sy, 128)
                    sy.wait_ge(t_sc.h, 4096)
                    sy.wait_ge(t_v.h, 2048)
                for k in range(16):
                    kk = KB + k
                    if call == 0:
                        if k >= 2:
                            sy.wait_ge(mv_1.h, 3 * (k - 2) + 3)
                            ms_xn.wait(sy, k, k // 2)
                        ms_t.inc(sy.dma_start(out=vb(M_MAT[k % 2], 2048),
                                              in_=aot_dram[:, k * 128:(k + 1) * 128],
                                              transpose=True), k)
                        sy.wait_ge(mv_1.h, 3 * k + 1)
                        ms_x1.inc(sy.dma_start(out=x1_dram[k * 128:(k + 1) * 128, :],
                                               in_=vf(M_MX[k % 2], 4096)), k)
                    sy.wait_ge(mv_1.h, 3 * kk + 3)
                    ms_xn.inc(sy.dma_start(out=xnm_dram[k * 128:(k + 1) * 128, :],
                                           in_=vb(M_MAT[k % 2], 2048)), kk)
                for mc in range(4):
                    ms_xn.wait_total(sy, KB + mc * 4 + 4)
                    for ci in range(32):
                        R = RB + mc * 32 + ci
                        if call == 0 and mc == 0 and ci == 0:
                            sy.wait_ge(v_zt.h, 256)
                        if R >= 4:
                            sy.wait_ge(tm_r.h, R - 3)
                        ms_T1.inc(sy.dma_start(out=vb(M_XT1 + (ci % 4) * 256, 256),
                                               in_=xnm_dram[mc * 512:(mc + 1) * 512,
                                                            ci * 128:(ci + 1) * 128],
                                               transpose=True), R)
                for k in range(16):
                    for dc in range(8):
                        kk = (KB + k) * 8 + dc
                        sy.wait_ge(mv_oe.h, kk + 1)
                        row = k * 128
                        ms_o.inc(sy.dma_start(out=d_dst[row:row + 128, dc * 512:(dc + 1) * 512],
                                              in_=vf(M_MO[kk % 2], 512)), kk)

            @block.vector
            def _(v):
                for k in range(16):
                    kk = KB + k
                    bi = k % 2
                    mg_x.wait_done(v, kk)
                    if k == 0 and call == 0:
                        v.wait_ge(t_sc.h, 4096)
                        v.wait_ge(t_v.h, 2048)
                    if call == 0:
                        ms_t.wait_done(v, k)
                        vsync(v, v.tensor_tensor(vf(M_MX[bi], 4096), vf(M_MX[bi], 4096),
                                                 vb(M_MAT[bi], 2048), op=ALU.add))
                    vsync(v, v.reduce_sum(msm(bi, 0), vf(M_MX[bi], 4096), axis=AX.X))
                    mv_1.inc(v.tensor_scalar(msm(bi, 1), msm(bi, 0), -1.0 / D, None,
                                             op0=ALU.mult))
                    vsync(v, v.engine_nop())
                    v.wait_ge(mc_s.h, 2 * kk + 1)
                    vsync(v, v.tensor_tensor(msm(bi, 2), msm(bi, 1), msm(bi, 1),
                                             op=ALU.mult))
                    vsync(v, v.tensor_scalar(msm(bi, 3), msm(bi, 4), 1.0 / D, None,
                                             op0=ALU.mult))
                    mv_1.inc(v.tensor_tensor(msm(bi, 5), msm(bi, 3), msm(bi, 2),
                                             op=ALU.subtract))
                    v.wait_ge(mc_s.h, 2 * kk + 2)
                    vsync(v, v.tensor_tensor(msm(bi, 6), msm(bi, 1), msm(bi, 7),
                                             op=ALU.mult))
                    if kk >= 2:
                        ms_xn.wait(v, kk, kk // 2)
                    mv_1.inc(v.tensor_scalar(vb(M_MAT[bi], 2048), vf(M_MX[bi], 4096),
                                             msm(bi, 7), msm(bi, 6),
                                             op0=ALU.mult, op1=ALU.add))
                def hg_evac(v, hp):
                    v.wait_ge(tm_h.h, HB + (hp + 1) * 32)
                    for er in range(8):
                        mv_hg.inc(v.tensor_tensor(
                            vb(M_HGT, 8192)[:, hp * 4096 + er * 512:hp * 4096 + (er + 1) * 512],
                            PS[:, er * 512:(er + 1) * 512],
                            vb(M_GX, 2048)[:, er * 512:(er + 1) * 512], op=ALU.mult))

                for mc in range(4):
                    v.wait_ge(tm_r.h, RB + (mc + 1) * 32)
                    if call == 1 and mc == 0:
                        ms_o.wait_total(v, 128)
                    for tt in range(4):
                        ins = v.tensor_copy(vf(GT4 + tt * 64, 64),
                                            PS[:, tt * 512:tt * 512 + 64])
                        if tt < 3:
                            vsync(v, ins)
                    mv_g.inc(ins)
                    for tt in range(4):
                        Gi = GB + mc * 4 + tt
                        scr = vf(GT4 + tt * 64, 64)
                        vsync(v, v.max(out=vf(G_M8, 8), in_=scr))
                        vsync(v, v.match_replace(out=vf(G_SCR2, 64), in_to_replace=vf(G_M8, 8),
                                                 in_values=scr, imm_value=NEG))
                        vsync(v, v.max(out=vf(G_M8 + 8, 8), in_=vf(G_SCR2, 64)))
                        mv_g.inc(v.tensor_scalar(vf(G_NEGM, 1), vf(G_M8, 1), -1.0,
                                                 None, op0=ALU.mult))
                        v.wait_ge(mc_e.h, Gi + 1)
                        vsync(v, v.reciprocal(vf(G_RZ, 1), vf(G_Z16, 1)))
                        vsync(v, v.tensor_scalar(vf(G_RZ2, 1), vf(G_RZ, 1), SCALING, None,
                                                 op0=ALU.mult))
                        vsync(v, v.tensor_scalar(vb(G_MASK, 32), scr,
                                                 vf(G_M8 + 15, 1), None, op0=ALU.is_ge))
                        vsync(v, v.scalar_tensor_tensor(vf(G_G, 64), vf(G_ME, 64),
                                                        vf(G_RZ2, 1), vb(G_MASK, 32),
                                                        op0=ALU.mult, op1=ALU.mult))
                        for bi4 in range(4):
                            for bj in range(2):
                                ins = v.transpose(
                                    vf(G_GT32, 128, 32 * bj, 32 * bj + 32)[:, 32 * bi4:32 * bi4 + 32],
                                    vf(G_G, 64, 32 * bi4, 32 * bi4 + 32)[:, 32 * bj:32 * bj + 32])
                        vsync(v, ins)
                        mv_g.inc(v.tensor_copy(
                            vb(G_GTB, 256, 0, 64)[:, tt * 128:(tt + 1) * 128],
                            vf(G_GT32, 128, 0, 64)))
                    if mc >= 1:
                        hg_evac(v, mc - 1)
                    for er in range(8):
                        gx = XB + mc * 8 + er
                        v.wait_ge(tm_g.h, gx + 1)
                        mv_gx.inc(v.tensor_copy(vb(M_GX, 2048)[:, er * 512:(er + 1) * 512],
                                                PS[:, 4 * 512:4 * 512 + 512]))
                hg_evac(v, 3)
                for k in range(16):
                    for dc in range(8):
                        kk = (KB + k) * 8 + dc
                        v.wait_ge(tm_b.h, kk + 1)
                        mg_r.wait_done(v, KB + k)
                        if kk >= 2:
                            ms_o.wait(v, kk, kk // 2)
                        mv_oe.inc(v.tensor_tensor(
                            vf(M_MO[kk % 2], 512),
                            PS[:, (kk % 2) * 512:(kk % 2) * 512 + 512],
                            vf(M_MX[k % 2], 4096)[:, dc * 512:(dc + 1) * 512], op=ALU.add))

            @block.scalar
            def _(s):
                for k in range(16):
                    kk = KB + k
                    bi = k % 2
                    s.wait_ge(mv_1.h, 3 * kk + 1)
                    ssync(s, s.memzero(msm(bi, 4)))
                    mc_s.inc(s.activation(vf(M_SQS, 4096), vf(M_MX[bi], 4096),
                                          AF.Square, accum_out=msm(bi, 4)))
                    s.wait_ge(mv_1.h, 3 * kk + 2)
                    ssync(s, s.activation(msm(bi, 8), msm(bi, 5), AF.Ln, bias=s_eps[:]))
                    mc_s.inc(s.activation(msm(bi, 7), msm(bi, 8), AF.Exp, scale=-0.5))
                def t2_loads(hp):
                    for ci in range(32):
                        H = HB + hp * 32 + ci
                        if H >= 4:
                            s.wait_ge(tm_h.h, H - 3)
                        if call == 0 and hp == 0 and ci == 0:
                            s.wait_ge(v_zt.h, 256)
                        ms_T2.inc(s.dma_start(out=vb(M_XT2 + (ci % 4) * 256, 256),
                                              in_=xnm_dram[hp * 512:(hp + 1) * 512,
                                                           ci * 128:(ci + 1) * 128],
                                              transpose=True), H)

                for mc in range(4):
                    for tt in range(4):
                        Gi = GB + mc * 4 + tt
                        s.wait_ge(mv_g.h, call * 36 + mc * 9 + 2 * tt + 2)
                        s.memzero(vf(G_ZD, 1))
                        ssync(s, s.memzero(vf(G_Z16, 1)))
                        s.activation(vf(G_ME, 64), vf(GT4 + tt * 64, 64), AF.Exp,
                                     bias=vf(G_NEGM, 1), accum_out=vf(G_ZD, 1))
                        mc_e.inc(s.activation(vf(G_E16O, 16), vf(G_M8, 16), AF.Exp,
                                              bias=vf(G_NEGM, 1), accum_out=vf(G_Z16, 1)))
                    if mc >= 1:
                        t2_loads(mc - 1)
                t2_loads(3)
                for k in range(16):
                    kk = KB + k
                    if k >= 2:
                        s.wait_ge(mv_oe.h, call * 128 + (k - 2) * 8 + 8)
                        s.wait_ge(mc_s.h, 2 * KB + 32)
                    else:
                        if call == 0:
                            ms_x1.wait_total(s, 16)
                    if call == 1:
                        s.wait_ge(fs_x3.h, 16 * (k + 1))
                    row = k * 128
                    mg_r.inc(s.dma_start(out=vf(M_MX[k % 2], 4096),
                                         in_=d_res[row:row + 128, :]), kk)

            @block.tensor
            def _(t):
                def hpass(t, hp):
                    for ci in range(32):
                        H = HB + hp * 32 + ci
                        ms_T2.wait_done(t, H)
                        if ci == 0:
                            t.wait_ge(mg_aA.h, 16 * (call + 1))
                            t.wait_ge(mv_gx.h, XB + hp * 8 + 8)
                            if hp <= 2:
                                t.wait_ge(mv_g.h, call * 36 + (hp + 1) * 9 + 1)
                            if call == 0 and hp == 0:
                                t.wait_ge(v_rz.h, 128)
                                t.wait_ge(v_at.h, 256)
                                t.wait_ge(v_ve.h, 512)
                            if hp > 0 or call == 1:
                                t.wait_ge(mv_hg.h, call * 32 + hp * 8)
                        for er in range(8):
                            ins = t.matmul(PS[:, er * 512:(er + 1) * 512],
                                           vb(M_AB, 16384)
                                           [:, ci * 1024 + er * 128:ci * 1024 + (er + 1) * 128],
                                           vb(M_XT2 + (ci % 4) * 256, 256),
                                           start=(ci == 0), stop=(ci == 31))
                        tm_h.inc(ins)

                for mc in range(4):
                    for ci in range(32):
                        R = RB + mc * 32 + ci
                        ms_T1.wait_done(t, R)
                        if ci == 0 and mc >= 1:
                            t.wait_ge(mv_hg.h, call * 32 + (mc - 1) * 8)
                        if R == 0:
                            t.wait_ge(s_wo.h, 128)
                            t.wait_ge(s_ex.h, 2048)
                        if call == 1 and R == RB:
                            t.wait_ge(mv_oe.h, 128)
                            t.wait_ge(fv_ev.h, fv_ev.n)
                            t.wait_ge(fse_A.h, fse_A.n)
                            t.wait_ge(fve_A.h, fve_A.n)
                            t.wait_ge(fv_pe.h, fv_pe.n)
                            t.wait_ge(fv_gw.h, fv_gw.n)
                            t.wait_ge(fv_he.h, fv_he.n)
                        for tt in range(4):
                            ins = t.matmul(PS[:, tt * 512:tt * 512 + 64],
                                           vb(M_XT1 + (ci % 4) * 256, 256)
                                           [:, tt * 128:(tt + 1) * 128],
                                           s_wrt[:, ci * 64:(ci + 1) * 64],
                                           start=(ci == 0), stop=(ci == 31))
                        tm_r.inc(ins)
                    if mc >= 1:
                        hpass(t, mc - 1)
                    for er in range(8):
                        gx = XB + mc * 8 + er
                        t.wait_ge(mv_g.h, call * 36 + mc * 9 + 9)
                        if er == 0 and mc >= 1:
                            t.wait_ge(mv_hg.h, call * 32 + (mc - 1) * 8 + 5)
                        if call == 0 and mc == 0 and er == 0:
                            t.wait_ge(v_at.h, 256)
                        if gx >= 1:
                            t.wait_ge(mv_gx.h, gx)
                        tm_g.inc(t.matmul(PS[:, 4 * 512:4 * 512 + 512],
                                          s_efull[:, er * 128:(er + 1) * 128],
                                          vb(G_GTB, 256, 0, 64), start=True, stop=True))
                hpass(t, 3)
                for k in range(16):
                    mc, tt = k // 4, k % 4
                    for dc in range(8):
                        kk = (KB + k) * 8 + dc
                        t.wait_ge(mg_aB.h, 16 * (call + 1))
                        if k == 0 and dc == 0:
                            t.wait_ge(mv_hg.h, call * 32 + 32)
                        if kk >= 2:
                            t.wait_ge(mv_oe.h, kk - 1)
                        for er in range(8):
                            ins = t.matmul(
                                PS[:, (kk % 2) * 512:(kk % 2) * 512 + 512],
                                vb(M_HGT, 8192)[:, mc * 4096 + er * 512 + tt * 128:
                                                mc * 4096 + er * 512 + (tt + 1) * 128],
                                vb(M_AB, 16384)[:, er * 4096 + dc * 512:er * 4096 + (dc + 1) * 512],
                                start=(er == 0), stop=(er == 7))
                        tm_b.inc(ins)

        sc4, _ = nc.enter_named_scope("ph4", False)
        if phases >= 4:
            moe_call(0, d_xq, x2_dram, x1_dram)
        nc.leave_named_scope("ph4", sc4, False)

        if phases <= 4:
            @block.gpsimd
            def _(g):
                if phases == 4:
                    ms_o.wait_total(g, 128)
                elif phases == 3:
                    if ph3sub >= 4:
                        st_wo.wait_total(g, 128)
                    elif ph3sub == 3:
                        g.wait_ge(v_at.h, 256)
                    elif ph3sub == 2:
                        g.wait_ge(s_ex.h, 2048)
                    else:
                        g.wait_ge(v_ve.h, 512)
                elif phases == 2:
                    g.wait_ge(dq5.h, dq5.n)
                else:
                    stq.wait_total(g, 32)
            return nc
        F_X2 = [0, 4096]
        F_V, F_ST = 8192, 12288
        F_W1, F_W2, F_W3, F_W4 = 20480, 24576, 28672, 32768
        F_XN, F_SM, F_SLAB = 36864, 38912, 39424
        F_E, F_MASK, F_G = 30720, 24576, 36864

        def fsm5(j):
            return vf(F_SM + j, 1)

        NCH = OWN // FCH
        fse_A, fve_A = sqt, vmc
        FSE0, FVE0, FCS0 = fse_A.n, fve_A.n, 0
        s_slabF = vb(F_SLAB, 4096, 0, 64)
        s_slabF2 = vb(F_SLAB + 4096, 4096, 0, 64)
        st4 = vb(F_ST, 8192, 0, 64).rearrange("y (r c j) -> y r c j", r=128, c=2, j=64)
        P4 = vb(F_W1, 4096, 0, 64).rearrange("k (j r) -> k j r", r=128)
        H4 = vb(F_W4, 4096).rearrange("p (j r) -> p j r", r=128)

        sc5, _ = nc.enter_named_scope("ph5", False)
        @block.tensor
        def _(t):
            for fc in range(NCH):
                for tf in range(4):
                    T = 4 * fc + tf
                    bmap = [0, 1, 6, 7] if tf < 3 else [0, 1, 2, 3]
                    for g4 in range(32):
                        if T == 0 and g4 == 0:
                            t.wait_ge(mv_oe.h, 128)
                        if g4 == 0:
                            if tf == 0:
                                t.wait_ge(fs_D.h, 16 * (fc + 1))
                            elif tf == 1:
                                t.wait_ge(fv_pe.h, 16 * (fc + 1))
                            elif tf == 2:
                                t.wait_ge(fs_gD.h, 16 * (fc + 1))
                            else:
                                t.wait_ge(fv_he.h, 16 * (fc + 1))
                            if T >= 1:
                                t.wait_ge(fse_A.h, FSE0 + 8 * T)
                                t.wait_ge(fve_A.h, FVE0 + 8 * T)
                        if g4 == 2:
                            if tf == 0 and fc >= 1:
                                t.wait_ge(fv_he.h, 16 * fc)
                            if tf == 3:
                                t.wait_ge(fv_ev.h, 16 * (4 * fc + 3))
                        if g4 >= 4:
                            q2 = g4 // 2 - 2
                            if q2 % 2 == 0:
                                t.wait_ge(fse_A.h, FSE0 + 8 * T + q2 // 2 + 1)
                            else:
                                t.wait_ge(fve_A.h, FVE0 + 8 * T + q2 // 2 + 1)
                        bk = bmap[g4 % 4]
                        for j in range(4):
                            r = g4 * 4 + j
                            ob = PS[0:64, bk * 512 + j * 128:bk * 512 + (j + 1) * 128]
                            if tf == 0:
                                ins = t.matmul(ob, vb(F_W4, 4096, 0, 64)[:, r * 64:(r + 1) * 64],
                                               s_fa[:], start=True, stop=True)
                            elif tf == 2:
                                ins = t.matmul(ob, vb(F_W1, 4096, 0, 64)[:, r * 64:(r + 1) * 64],
                                               s_fa[:], start=True, stop=True)
                            elif tf == 1:
                                ins = t.matmul(ob, P4[:, :, r], s_fa[:], start=True, stop=True)
                            else:
                                ins = t.matmul(ob, H4[:, :, r], s_fha[:], start=True, stop=True)
                        ft_A.inc(ins)
                    for jg in range(16):
                        GP = 64 * fc + 16 * tf + jg
                        if jg == 0:
                            t.wait_ge(fse_A.h, FSE0 + 8 * (T + 1))
                            t.wait_ge(fve_A.h, FVE0 + 8 * (T + 1))
                            if T == 0:
                                t.wait_ge(fs_cs.h, FCS0 + 16)
                            if tf == 3:
                                t.wait_ge(fv_gw.h, 16 * (fc + 1))
                        if GP >= 2:
                            t.wait_ge(fv_ev.h, GP - 1)
                        bb = (2 + jg % 2) if tf < 3 else (4 + jg % 2)
                        for q in range(4):
                            j1 = 4 * jg + q
                            if tf in (0, 2):
                                ob = PS[0:128, bb * 512 + q * 128:bb * 512 + (q + 1) * 128]
                                t.matmul(ob, s_slabF[:, j1 * 128:(j1 + 1) * 128],
                                         st4[:, :, 0, j1], start=True, stop=False)
                                ins = t.matmul(ob, s_slabF2[:, j1 * 128:(j1 + 1) * 128],
                                               st4[:, :, 1, j1], start=False, stop=True)
                            else:
                                ob = PS[0:64, bb * 512 + q * 128:bb * 512 + (q + 1) * 128]
                                t.matmul(ob, s_slabF[:, j1 * 128:j1 * 128 + 64],
                                         st4[:, :, 0, j1], start=True, stop=False)
                                ins = t.matmul(ob, s_slabF[:, j1 * 128 + 64:(j1 + 1) * 128],
                                               st4[:, :, 1, j1], start=False, stop=True)
                        ft_B.inc(ins)
                    if tf == 0:
                        for i in range(16):
                            if i == 0:
                                t.wait_ge(fc_s.h, 4 * fc + 1)
                            if i == 8:
                                t.wait_ge(fc_s.h, 4 * fc + 2)
                                t.wait_ge(fv_ev.h, 64 * fc)
                                if fc == 0:
                                    t.wait_ge(mv_gx.h, 32)
                                    t.wait_ge(mv_hg.h, 32)
                            if 16 * fc + i >= 2:
                                t.wait_ge(fv_pe.h, 16 * fc + i - 1)
                            ft_P.inc(t.matmul(
                                PS[0:64, 2048 + (i % 2) * 512:2048 + (i % 2) * 512 + 512],
                                s_lh1[:, 0:64], vb(F_W2, 4096)[:, i * 512:(i + 1) * 512],
                                start=True, stop=True))
                    if tf == 2:
                        for i in range(16):
                            t.wait_ge(fv_ev.h, 64 * fc + 32 + i + 1)
                            if 16 * fc + i >= 2:
                                t.wait_ge(fv_gw.h, 16 * fc + i - 1)
                            ft_G.inc(t.matmul(
                                PS[:, 2048 + (i % 2) * 512:2048 + (i % 2) * 512 + 512],
                                s_sw[:], vb(F_W2, 4096)[:, i * 512:(i + 1) * 512],
                                start=True, stop=True))
                        for i in range(16):
                            if i == 0:
                                t.wait_ge(fv_g.h, 4 * fc + 3)
                            if i == 8:
                                t.wait_ge(fv_g.h, 4 * fc + 4)
                            if 16 * fc + i >= 2:
                                t.wait_ge(fv_he.h, 16 * fc + i - 1)
                            pb = PS[:, 3072 + (i % 2) * 512:3072 + (i % 2) * 512 + 512]
                            t.matmul(pb, s_lh1[:], vb(F_W1, 4096)[:, i * 512:(i + 1) * 512],
                                     start=True, stop=False)
                            ft_H.inc(t.matmul(pb, s_lh2[:], vb(F_W3, 4096)[:, i * 512:(i + 1) * 512],
                                              start=False, stop=True))

        def ph5_ln(v, c):
            fg_x.wait_done(v, c)
            fx2 = vf(F_X2[c % 2], 4096)
            vsync(v, v.reduce_sum(fsm5(0), fx2, axis=AX.X))
            fv_ln.inc(v.tensor_scalar(fsm5(1), fsm5(0), -1.0 / D, None, op0=ALU.mult))
            vsync(v, v.engine_nop())
            v.wait_ge(fc_ln.h, 2 * c + 1)
            vsync(v, v.tensor_tensor(fsm5(48), fsm5(40), fsm5(41), op=ALU.add))
            vsync(v, v.tensor_tensor(fsm5(49), fsm5(42), fsm5(43), op=ALU.add))
            vsync(v, v.tensor_tensor(fsm5(50), fsm5(44), fsm5(45), op=ALU.add))
            vsync(v, v.tensor_tensor(fsm5(51), fsm5(46), fsm5(47), op=ALU.add))
            vsync(v, v.tensor_tensor(fsm5(48), fsm5(48), fsm5(49), op=ALU.add))
            vsync(v, v.tensor_tensor(fsm5(50), fsm5(50), fsm5(51), op=ALU.add))
            vsync(v, v.tensor_tensor(fsm5(4), fsm5(48), fsm5(50), op=ALU.add))
            vsync(v, v.tensor_tensor(fsm5(2), fsm5(1), fsm5(1), op=ALU.mult))
            vsync(v, v.tensor_scalar(fsm5(3), fsm5(4), 1.0 / D, None, op0=ALU.mult))
            fv_ln.inc(v.tensor_tensor(fsm5(5), fsm5(3), fsm5(2), op=ALU.subtract))
            v.wait_ge(fc_ln.h, 2 * c + 2)
            vsync(v, v.tensor_tensor(fsm5(6), fsm5(1), fsm5(7), op=ALU.mult))
            if c >= 1:
                v.wait_ge(fs_xn.h, 16 * c)
                v.wait_ge(fs_g.h, 16 * c)
            fv_ln.inc(v.tensor_scalar(vb(F_XN, 2048), fx2, fsm5(7), fsm5(6),
                                      op0=ALU.mult, op1=ALU.add))

        @block.vector
        def _(v):
            for fc in range(NCH):
                if fc == 0:
                    v.wait_ge(mv_oe.h, 128)
                    ph5_ln(v, 0)
                for tf in range(4):
                    T = 4 * fc + tf
                    for q in range(1, 16, 2):
                        v.wait_ge(ft_A.h, 32 * T + 2 * q + 2)
                        if q == 1:
                            v.wait_ge(ft_B.h, 16 * T)
                            if tf == 3:
                                v.wait_ge(fv_g.h, 4 * fc + 4)
                        base = 3072 if tf < 3 else 1024
                        fve_A.inc(v.tensor_copy(
                            vb(F_ST, 8192, 0, 64)[:, q * 1024:(q + 1) * 1024],
                            PS[0:64, base:base + 1024]))
                    for jg in range(16):
                        GP = 64 * fc + 16 * tf + jg
                        v.wait_ge(ft_B.h, GP + 1)
                        bb = (2 + jg % 2) if tf < 3 else (4 + jg % 2)
                        if tf == 0:
                            fv_ev.inc(v.tensor_copy(
                                vb(F_V, 4096)[:, jg * 512:(jg + 1) * 512],
                                PS[:, bb * 512:bb * 512 + 512]))
                        elif tf == 2:
                            if jg == 0:
                                v.wait_ge(fv_g.h, 4 * fc + 2)
                                v.wait_ge(fs_c.h, 16 * (fc + 1))
                            fv_ev.inc(v.tensor_copy(
                                vb(F_W2, 4096)[:, jg * 512:(jg + 1) * 512],
                                PS[:, bb * 512:bb * 512 + 512]))
                        else:
                            wreg = F_W2 if tf == 1 else F_W1
                            dst = vb(wreg, 4096, 0, 64).rearrange(
                                "p (r j) -> p r j", j=64)[:, :, 4 * jg:4 * jg + 4]
                            srcp = PS[0:64, bb * 512:bb * 512 + 512].rearrange(
                                "p (j r) -> p r j", j=4)
                            fv_ev.inc(v.tensor_copy(dst, srcp))
                    if tf == 0:
                        if fc >= 1:
                            v.wait_ge(fs_or.h, 16 * fc)
                            fv_x3.inc(v.tensor_tensor(vf(F_W3, 4096), vf(F_X2[(fc - 1) % 2], 4096),
                                                      vb(F_W2, 2048), op=ALU.add))
                        for i in range(16):
                            v.wait_ge(ft_P.h, 16 * fc + i + 1)
                            if i == 0 and fc >= 1:
                                v.wait_ge(fs_o.h, 16 * fc)
                            fv_pe.inc(v.tensor_copy(
                                vb(F_W1, 4096, 0, 64)[:, i * 512:(i + 1) * 512],
                                PS[0:64, 2048 + (i % 2) * 512:2048 + (i % 2) * 512 + 512]))
                    elif tf == 1:
                        v.wait_ge(fs_cr.h, 16 * (fc + 1))
                        vsync(v, v.max(out=vb(F_SM + 16, 4), in_=vb(F_W3, 2048)))
                        fv_g.inc(v.tensor_scalar(fsm5(24), vb(F_SM + 16, 4)[:, 0:1], -1.0,
                                                 None, op0=ALU.mult))
                        v.wait_ge(fc_s.h, 4 * fc + 4)
                        vsync(v, v.reciprocal(fsm5(25), fsm5(26)))
                        vsync(v, v.tensor_copy(fsm5(27), vb(F_SM + 16, 4)[:, 7:8]))
                        vsync(v, v.tensor_scalar(vb(F_MASK, 2048), vb(F_W3, 2048),
                                                 fsm5(27), None, op0=ALU.is_ge))
                        v.wait_ge(fs_xn.h, 16 * (fc + 1))
                        fv_g.inc(v.scalar_tensor_tensor(vb(F_G, 2048), vb(F_E, 2048),
                                                        fsm5(25), vb(F_MASK, 2048),
                                                        op0=ALU.mult, op1=ALU.mult))
                    elif tf == 2:
                        for i in range(16):
                            v.wait_ge(ft_G.h, 16 * fc + i + 1)
                            fv_gw.inc(v.tensor_copy(
                                vb(F_ST, 4096)[:, i * 512:(i + 1) * 512],
                                PS[:, 2048 + (i % 2) * 512:2048 + (i % 2) * 512 + 512]))
                        vsync(v, v.engine_nop())
                        vsync(v, v.tensor_tensor(vb(F_W1, 2048), vb(F_V, 2048),
                                                 vb(F_W2, 2048), op=ALU.mult))
                        fv_g.inc(v.tensor_tensor(vb(F_W3, 2048), vb(F_V, 2048),
                                                 vb(F_ST, 2048), op=ALU.mult))
                        vsync(v, v.tensor_tensor(vb(F_W1 + 2048, 2048), vb(F_V + 2048, 2048),
                                                 vb(F_W2 + 2048, 2048), op=ALU.mult))
                        fv_g.inc(v.tensor_tensor(vb(F_W3 + 2048, 2048), vb(F_V + 2048, 2048),
                                                 vb(F_ST + 2048, 2048), op=ALU.mult))
                        for i in range(16):
                            v.wait_ge(ft_H.h, 16 * fc + i + 1)
                            fv_he.inc(v.tensor_copy(
                                vb(F_W4, 4096)[:, i * 512:(i + 1) * 512],
                                PS[:, 3072 + (i % 2) * 512:3072 + (i % 2) * 512 + 512]))
                        if fc + 1 < NCH:
                            ph5_ln(v, fc + 1)

            v.wait_ge(fs_or.h, 16 * NCH)
            fv_x3.inc(v.tensor_tensor(vf(F_W3, 4096), vf(F_X2[(NCH - 1) % 2], 4096),
                                      vb(F_W2, 2048), op=ALU.add))

        def ph5_lns(s, c):
            s.wait_ge(fv_ln.h, 3 * c + 1)
            for i in range(8):
                ssync(s, s.memzero(fsm5(40 + i)))
                ins = s.activation(vb(47616, 256),
                                   vf(F_X2[c % 2], 4096)[:, i * 512:(i + 1) * 512],
                                   AF.Square, accum_out=fsm5(40 + i))
                if i < 7:
                    ssync(s, ins)
            fc_ln.inc(ins)
            s.wait_ge(fv_ln.h, 3 * c + 2)
            ssync(s, s.activation(fsm5(8), fsm5(5), AF.Ln, bias=s_eps[:]))
            fc_ln.inc(s.activation(fsm5(7), fsm5(8), AF.Exp, scale=-0.5))

        @block.scalar
        def _(s):
            for fc in range(NCH):
                if fc == 0:
                    ph5_lns(s, 0)
                for tf in range(4):
                    T = 4 * fc + tf
                    for q in range(0, 16, 2):
                        s.wait_ge(ft_A.h, 32 * T + 2 * q + 2)
                        if q == 0:
                            s.wait_ge(ft_B.h, 16 * T)
                            if tf == 3:
                                s.wait_ge(fv_g.h, 4 * fc + 4)
                        fse_A.inc(s.activation(
                            vb(F_ST, 8192, 0, 64)[:, q * 1024:(q + 1) * 1024],
                            PS[0:64, 0:1024], AF.Identity))
                    if tf == 0:
                        s.wait_ge(fv_ev.h, 64 * fc + 8)
                        s.wait_ge(fv_x3.h, fc)
                        fc_s.inc(s.activation(vb(F_W2, 2048), vb(F_V, 2048), AF.Square))
                        s.wait_ge(fv_ev.h, 64 * fc + 16)
                        fc_s.inc(s.activation(vb(F_W2 + 2048, 2048), vb(F_V + 2048, 2048),
                                              AF.Square))
                    elif tf == 1:
                        s.wait_ge(fv_g.h, 4 * fc + 1)
                        s.wait_ge(fs_cr.h, 16 * (fc + 1))
                        fc_s.inc(s.activation(vb(F_E, 2048), vb(F_W3, 2048), AF.Exp,
                                              bias=fsm5(24)))
                        ssync(s, s.memzero(fsm5(26)))
                        fc_s.inc(s.activation(vf(F_SM + 32, 8), vb(F_SM + 16, 4), AF.Exp,
                                              bias=fsm5(24), accum_out=fsm5(26)))
                    elif tf == 2 and fc + 1 < NCH:
                        ph5_lns(s, fc + 1)

        @block.sync
        def _(sy):
            def do_xn_D(c):
                row2 = (c % 2) * FCH
                sy.wait_ge(fv_ln.h, 3 * c + 3)
                fs_xn.inc(sy.dma_start(out=fb_dram[row2:row2 + FCH, :], in_=vb(F_XN, 2048)), 16)
                sy.wait_ge(fs_xn.h, 16 * (c + 1))
                if c == 0:
                    sy.wait_ge(mv_oe.h, 128)
                else:
                    sy.wait_ge(ft_A.h, 128 * c)
                fs_D.inc(sy.dma_start(
                    out=vb(F_W4, 4096, 0, 64).rearrange("x (r y) -> x r y", r=FCH),
                    in_=fb_dram[row2:row2 + FCH, :].rearrange("r (x y) -> x r y", x=64, y=64)), 16)

            do_xn_D(0)
            for fc in range(NCH):
                row2 = (fc % 2) * FCH
                if fc >= 1:
                    sy.wait_ge(fv_x3.h, fc)
                    fs_x3.inc(sy.dma_start(out=x3_dram[(fc - 1) * FCH:fc * FCH, :],
                                           in_=vf(F_W3, 4096)), 16)
                sy.wait_ge(fv_ev.h, 64 * fc + 32)
                fs_c.inc(sy.dma_start(out=fc_dram[(fc % 2) * 64:(fc % 2) * 64 + 64, :],
                                      in_=vb(F_W2, 4096, 0, 64)), 16)
                sy.wait_ge(fs_c.h, 16 * (fc + 1))
                sy.wait_ge(fs_x3.h, 16 * fc)
                fs_cr.inc(sy.dma_start(
                    out=vb(F_W3, 2048).rearrange("r (l m) -> r l m", l=64),
                    in_=fc_dram[(fc % 2) * 64:(fc % 2) * 64 + 64, :]
                    .rearrange("l (r m) -> r l m", r=FCH)), 16)
                sy.wait_ge(fv_g.h, 4 * fc + 2)
                fs_g.inc(sy.dma_start(out=fg_dram[row2:row2 + FCH, :], in_=vb(F_G, 2048)), 16)
                sy.wait_ge(fs_g.h, 16 * (fc + 1))
                fs_gD.inc(sy.dma_start(
                    out=vb(F_W1, 4096, 0, 64).rearrange("x (r y) -> x r y", r=FCH),
                    in_=fg_dram[row2:row2 + FCH, :].rearrange("r (x y) -> x r y", x=64, y=64)), 16)
                if fc + 1 < NCH:
                    do_xn_D(fc + 1)
                sy.wait_ge(fv_ev.h, 64 * fc + 64)
                fs_o.inc(sy.dma_start(out=fo_dram[(fc % 2) * 64:(fc % 2) * 64 + 64, :],
                                      in_=vb(F_W1, 4096, 0, 64)), 16)
                sy.wait_ge(fs_o.h, 16 * (fc + 1))
                sy.wait_ge(fv_g.h, 4 * fc + 4)
                fs_or.inc(sy.dma_start(
                    out=vb(F_W2, 2048).rearrange("r (l m) -> r l m", l=64),
                    in_=fo_dram[(fc % 2) * 64:(fc % 2) * 64 + 64, :]
                    .rearrange("l (r m) -> r l m", r=FCH)), 16)
            sy.wait_ge(fv_x3.h, NCH)
            fs_x3.inc(sy.dma_start(out=x3_dram[(NCH - 1) * FCH:NCH * FCH, :],
                                   in_=vf(F_W3, 4096)), 16)

        @block.gpsimd
        def _(g):
            g.wait_ge(tm_b.h, 128)
            ms_xn.wait_total(g, 16)
            ms_o.wait_total(g, 128)
            fs_cs.inc(g.dma_start(out=vb(F_SLAB, 8192, 0, 64), in_=d_slab), 16)
            for fc in range(NCH):
                if fc == 0:
                    g.wait_ge(mv_oe.h, 128)
                ms_o.wait_total(g, 8 * (fc + 1))
                if fc >= 2:
                    g.wait_ge(fv_x3.h, fc - 1)
                fg_x.inc(g.dma_start(out=vf(F_X2[fc % 2], 4096),
                                     in_=x2_dram[fc * FCH:(fc + 1) * FCH, :]), fc)

        nc.leave_named_scope("ph5", sc5, False)
        if phases <= 5:
            @block.gpsimd
            def _(g):
                g.wait_ge(fs_x3.h, fs_x3.n)
            return nc
        sc6, _ = nc.enter_named_scope("ph6", False)
        moe_call(1, x3_dram, d_out, x3_dram)
        nc.leave_named_scope("ph6", sc6, False)

        @block.gpsimd
        def _(g):
            ms_o.wait_total(g, 256)
    return nc
